# version 10
# speedup vs baseline: 1.7994x; 1.0148x over previous
import numpy as np
import ml_dtypes
import concourse.bass as bass
import concourse.mybir as mybir
from concourse.alu_op_type import AluOpType
from concourse.bass_utils import run_bass_kernel_spmd
from contextlib import ExitStack

F32 = mybir.dt.float32
BF16 = mybir.dt.bfloat16
I32 = mybir.dt.int32
AF = mybir.ActivationFunctionType
AX = mybir.AxisListType
OP = AluOpType

EMB = 1024
HEADS = 16
HD = 64
S = 4096
TOK = 2048
XROWS = 2304
NXT = 18
NOT = 16
MASKRAW = -2.0 ** 33
LN_EPS = float(np.float32(1e-5))
FP = lambda x: float(np.float32(x))
BF = ml_dtypes.bfloat16

_QPOS = np.float32(np.float32(0.9) * np.float32(16383.0))
_LOI = int(np.floor(_QPOS))
_HW = np.float32(_QPOS - np.float32(_LOI))
_LW = np.float32(np.float32(1.0) - _HW)
K1 = float(_LOI + 1)
K2 = float(_LOI + 2)


class SemAlias:

    def __init__(self, base_sem):
        self.h = base_sem.h
        self._s = base_sem
        self.b = base_sem.n

    @property
    def n(self):
        return self._s.n - self.b

    @n.setter
    def n(self, v):
        self._s.n = self.b + v

    def inc(self, instr, k=1):
        instr.then_inc(self.h, k)
        self._s.n += k
        return instr

    def wait(self, eng, at=None):
        eng.wait_ge(self.h, self._s.n if at is None else self.b + at)


class Sem:

    def __init__(self, nc, es, name):
        self.h = es.enter_context(nc.semaphore(name))
        self.n = 0

    def inc(self, instr, k=1):
        instr.then_inc(self.h, k)
        self.n += k
        return instr

    def wait(self, eng, at=None):
        eng.wait_ge(self.h, self.n if at is None else at)


def _host_consts():
    c = {}
    c["identf"] = np.eye(128, dtype=np.float32)
    ii = np.arange(128)
    band = np.where(ii[None, :] >= ii[:, None], 0.0, MASKRAW).astype(np.float32)
    diag = np.where(ii[None, :] <= ii[:, None], 0.0, MASKRAW).astype(np.float32)
    c["mband"] = band
    c["mdiag"] = diag
    R = np.zeros((HD, HD), np.float32)
    for i in range(0, HD, 2):
        R[i, i + 1] = -1.0
        R[i + 1, i] = 1.0
    R2 = np.zeros((128, 128), np.float32)
    R2[0:64, 0:64] = R
    R2[64:128, 64:128] = R
    c["rperm"] = np.ascontiguousarray(R2.T)
    c["idx8"] = np.broadcast_to(np.arange(8, dtype=np.float32), (128, 8)).copy()
    c["negbig"] = np.full((128, 8), -3e38, np.float32)
    c["neg1e9"] = np.full((128, 8), -1e9, np.float32)
    inv_freq = 1.0 / (10000.0 ** (np.arange(0, HD, 2, dtype=np.float32) / HD))
    pos = np.arange(S, dtype=np.float32)
    fr = np.outer(pos, inv_freq)
    emb = np.concatenate([fr, fr], -1)
    c["cosS"] = np.cos(emb).astype(np.float32)
    c["sinS"] = np.sin(emb).astype(np.float32)
    return c


_CONSTS = _host_consts()


def _noise_table():
    import jax
    with jax.default_device(jax.devices("cpu")[0]):
        n = np.asarray(jax.random.normal(jax.random.key(1), (4, S, 8)),
                       np.float32)
    return n


def _core_inputs(c, inp, noise):
    b, h = c // 2, c % 2
    p0 = 2048 * h
    x = inp["L0_attn_x"][b]
    xa = np.zeros((XROWS, EMB), np.float32)
    if h == 1:
        xa[0:128] = x[p0 - 128:p0]
    xa[128:128 + TOK] = x[p0:p0 + TOK]
    xa[128 + TOK:128 + TOK + 4] = x[0:4]
    m = {"xa": xa}
    posn = np.zeros(XROWS, np.int64)
    posn[0:128] = np.arange(p0 - 128, p0) if h == 1 else 0
    posn[128:128 + TOK] = np.arange(p0, p0 + TOK)
    posn[128 + TOK:128 + TOK + 4] = np.arange(4)
    cos1 = _CONSTS["cosS"][posn]
    sin1 = _CONSTS["sinS"][posn]
    m["cosb"] = np.concatenate([cos1, cos1], 1).T.copy()
    m["sinb"] = np.concatenate([sin1, sin1], 1).T.copy()
    m["mA0"] = (np.full((128, 128), MASKRAW, np.float32) if h == 0
                else _CONSTS["mband"])
    m["gmask0"] = np.full((128, 4), MASKRAW if h == 0 else 0.0, np.float32)
    m["qkvwT"] = np.ascontiguousarray(inp["L0_attn_qkv_w"].T)
    m["outwT"] = np.ascontiguousarray(inp["L0_attn_out_w"].T)
    m["noise"] = np.ascontiguousarray(
        noise[b, p0:p0 + TOK, :].reshape(16, 128, 8).transpose(1, 0, 2))
    for k in ("identf", "mband", "mdiag", "rperm", "idx8", "negbig",
              "neg1e9"):
        m[k] = _CONSTS[k]
    m["gw1"] = inp["L0_ffn_gate_w1"]
    m["gw2"] = inp["L0_ffn_gate_w2"]
    m["gw3"] = inp["L0_ffn_gate_w3"]
    m["affw"] = inp["L0_ffn_aff_w"]
    m["gb1"] = inp["L0_ffn_gate_b1"].reshape(8, 128).T.copy()
    m["gb2"] = inp["L0_ffn_gate_b2"].reshape(4, 128).T.copy()
    m["lnsc"] = inp["L0_ffn_ln_scale"].reshape(8, 128).T.copy()
    m["lnof"] = inp["L0_ffn_ln_offset"].reshape(8, 128).T.copy()
    m["gb3"] = inp["L0_ffn_gate_b3"].reshape(8, 1).copy()
    m["affb"] = inp["L0_ffn_aff_b"].reshape(8, 1).copy()
    m["usage"] = inp["L0_ffn_expert_usage"].reshape(1, 8).copy()
    sel = np.zeros((1, 8), np.float32)
    sel[0, 2 * b] = 1.0
    sel[0, 2 * b + 1] = 1.0
    m["rankmask"] = sel
    m["e0w1"] = inp["L0_ffn_e0_w1"]
    m["e0w2"] = inp["L0_ffn_e0_w2"]
    m["e1w1"] = inp["L0_ffn_e1_w1"]
    m["e1w2"] = inp["L0_ffn_e1_w2"]
    m["e1w3"] = inp["L0_ffn_e1_w3"]
    m["e2w1"] = inp["L0_ffn_e2_w1"]
    m["e2w2"] = inp["L0_ffn_e2_w2"]
    m["e0b1"] = inp["L0_ffn_e0_b1"].reshape(3, 32, 128).transpose(0, 2, 1).copy()
    m["e0b2"] = inp["L0_ffn_e0_b2"].reshape(3, 8, 128).transpose(0, 2, 1).copy()
    m["e1b1"] = inp["L0_ffn_e1_b1"].reshape(3, 16, 128).transpose(0, 2, 1).copy()
    m["e1b2"] = inp["L0_ffn_e1_b2"].reshape(3, 16, 128).transpose(0, 2, 1).copy()
    m["e1b3"] = inp["L0_ffn_e1_b3"].reshape(3, 8, 128).transpose(0, 2, 1).copy()
    m["e2b1"] = inp["L0_ffn_e2_b1"].reshape(2, 24, 128).transpose(0, 2, 1).copy()
    m["e2b2"] = inp["L0_ffn_e2_b2"].reshape(2, 8, 128).transpose(0, 2, 1).copy()
    q = np.arange(128)
    m["ltri"] = (q[:, None] < q[None, :]).astype(np.float32)
    m["r256"] = np.broadcast_to(8.0 * np.arange(256, dtype=np.float32),
                                (128, 256)).copy()
    st = np.arange(16)
    m["gc16"] = (8.0 * (128.0 * (st[None, :] % 2) + q[:, None])
                 + st[None, :] // 2).astype(np.float32)
    m["rltm"] = np.broadcast_to(
        (np.arange(8)[None, :] < c).astype(np.float32), (128, 8)).copy()
    return m


FAM = {0: ("e0", 0), 1: ("e1", 0), 2: ("e2", 0), 3: ("e0", 1),
       4: ("e1", 1), 5: ("e2", 1), 6: ("e0", 2), 7: ("e1", 2)}


def build_nc(debug=()):
    nc = bass.Bass("TRN2", target_bir_lowering=False)
    D = {}

    def din(name, shape, dt=F32):
        D[name] = nc.dram_tensor(name, list(shape), dt, kind="ExternalInput")
        return D[name]

    def dout(name, shape, dt=F32):
        D[name] = nc.dram_tensor(name, list(shape), dt, kind="ExternalOutput")
        return D[name]

    din("xa", (XROWS, EMB))
    din("cosb", (128, XROWS)); din("sinb", (128, XROWS))
    din("mA0", (128, 128)); din("gmask0", (128, 4))
    din("mband", (128, 128)); din("mdiag", (128, 128))
    din("identf", (128, 128)); din("rperm", (128, 128))
    din("idx8", (128, 8)); din("negbig", (128, 8)); din("neg1e9", (128, 8))
    din("qkvwT", (EMB, 3 * EMB)); din("outwT", (EMB, EMB))
    din("noise", (128, 16, 8))
    din("gw1", (1024, 1024)); din("gw2", (1024, 512)); din("gw3", (512, 8))
    din("affw", (1024, 8))
    din("gb1", (128, 8)); din("gb2", (128, 4)); din("lnsc", (128, 8))
    din("lnof", (128, 8)); din("gb3", (8, 1)); din("affb", (8, 1))
    din("usage", (1, 8)); din("rankmask", (1, 8))
    din("e0w1", (3, 1024, 4096)); din("e0w2", (3, 4096, 1024))
    din("e1w1", (3, 1024, 2048)); din("e1w2", (3, 2048, 2048))
    din("e1w3", (3, 2048, 1024))
    din("e2w1", (2, 1024, 3072)); din("e2w2", (2, 3072, 1024))
    din("e0b1", (3, 128, 32)); din("e0b2", (3, 128, 8))
    din("e1b1", (3, 128, 16)); din("e1b2", (3, 128, 16))
    din("e1b3", (3, 128, 8))
    din("e2b1", (2, 128, 24)); din("e2b2", (2, 128, 8))
    din("ltri", (128, 128)); din("r256", (128, 256))
    din("gc16", (128, 16)); din("rltm", (128, 8))
    for nm, sh in (("c_e0w1", (3, 1024, 4096)), ("c_e0w2", (3, 4096, 1024)),
                   ("c_e1w1", (3, 1024, 2048)), ("c_e1w2", (3, 2048, 2048)),
                   ("c_e1w3", (3, 2048, 1024)), ("c_e2w1", (2, 1024, 3072)),
                   ("c_e2w2", (2, 3072, 1024))):
        D[nm] = nc.dram_tensor(nm, list(sh), BF16, kind="Internal")
    D["rsin"] = nc.dram_tensor("rsin", [8, 128, 2, 1024], BF16,
                               kind="Internal")
    D["rsout"] = nc.dram_tensor("rsout", [128, 2, 1024], BF16,
                                kind="Internal")
    D["wrsin"] = nc.dram_tensor("wrsin", [8, 8, 256], F32, kind="Internal")
    D["wrsout"] = nc.dram_tensor("wrsout", [8, 256], F32, kind="Internal")
    D["agin2"] = nc.dram_tensor("agin2", [2, 128, 1024], BF16,
                                kind="Internal")
    D["agout2"] = nc.dram_tensor("agout2", [8, 2, 128, 1024], BF16,
                                 kind="Internal")

    dout("out", (TOK, EMB))

    D["yspill"] = nc.dram_tensor("yspill", [TOK, EMB], F32)
    D["agin"] = nc.dram_tensor("agin", [128, 144], F32)
    D["agout"] = nc.dram_tensor("agout", [1024, 144], F32)

    dbg = {}
    if "ln1" in debug:
        dbg["d_xlnfm"] = dout("d_xlnfm", (128, 8, XROWS), BF16)
    if "attn" in debug:
        dbg["d_ofm"] = dout("d_ofm", (128, 8, TOK), BF16)
    if "attn2" in debug:
        dbg["d_qrot"] = dout("d_qrot", (128, TOK), BF16)
        dbg["d_krot"] = dout("d_krot", (128, XROWS), BF16)
        dbg["d_qa"] = dout("d_qa", (128, TOK), BF16)
        dbg["d_vtm"] = dout("d_vtm", (2, 128, 18, 128), BF16)
    if "y" in debug:
        dbg["d_y"] = dout("d_y", (TOK, EMB))
        dbg["d_u"] = dout("d_u", (128, 8, TOK), BF16)
        dbg["d_xchg"] = dout("d_xchg", (128, 144))
    if "thr" in debug:
        dbg["d_thr"] = dout("d_thr", (128, 16))
    if "sp" in debug:
        dbg["d_grank"] = dout("d_grank", (128, 16))
        dbg["d_gx"] = dout("d_gx", (128, 2, 1024), BF16)
        dbg["d_wsl"] = dout("d_wsl", (8, 256))
        dbg["d_comb"] = dout("d_comb", (128, 2, 1024))
    if "gate" in debug:
        dbg["d_logits"] = dout("d_logits", (128, 16, 8))
        dbg["d_w"] = dout("d_w", (128, 16, 8))
        dbg["d_xnorm"] = dout("d_xnorm", (128, 8, TOK), BF16)
        dbg["d_sel2"] = dout("d_sel2", (128, 16, 8))
        dbg["d_i1"] = dout("d_i1", (128, 16))
        dbg["d_i2"] = dout("d_i2", (128, 16))

    es = ExitStack()

    def sb(name, shape, dt):
        return es.enter_context(nc.sbuf_tensor(name, list(shape), dt))

    def ps(name, shape, dt=F32):
        return es.enter_context(nc.psum_tensor(name, list(shape), dt))

    def sem(name):
        return Sem(nc, es, name)

    identf = sb("identf_s", (128, 128), F32)
    identb = sb("identb", (128, 128), BF16)
    ones1 = sb("ones1", (1, 128), BF16)
    onesc = sb("onesc", (128, 1), BF16)
    masks = sb("masks", (128, 4 * 128 + 4), BF16)
    idx8 = sb("idx8_s", (128, 8), F32)
    ltri_b = sb("ltri_b", (128, 128), BF16)
    r2560 = sb("r2560", (128, 256), F32)
    gc16_sb = sb("gc16_sb", (128, 16), F32)
    rltm_sb = sb("rltm_sb", (128, 8), F32)
    negbig = sb("negbig_s", (128, 8), F32)
    neg1e9 = sb("neg1e9_s", (128, 8), F32)
    xchg = sb("xchg", (128, 144), F32)
    gath = sb("gath", (128, 8, 144), F32)
    u_fm = sb("u_fm", (128, 8, TOK), BF16)
    scal = sb("scal", (128, 64), F32)
    SC_THR = 0
    SC_RTEMP = 1
    SC_NSC = 2
    SC_B3P = 3
    SC_LO = 4; SC_HI = 5; SC_MID = 6; SC_PRED = 7; SC_CNT = 8
    SC_T1 = 9; SC_T2 = 10; SC_S1 = 11; SC_S2 = 12
    SC_AFF = 13
    SC_TMP = 14; SC_TMP2 = 15
    gcols = sb("gcols", (128, 16 * 10), F32)
    biases = sb("biases", (128, 332), F32)
    BOFF = {}
    _bo = 0
    for nm, w in (("gb1", 8), ("gb2", 4), ("lnsc", 8), ("lnof", 8),
                  ("e0b1", 32 * 3), ("e0b2", 8 * 3), ("e1b1", 16 * 3),
                  ("e1b2", 16 * 3), ("e1b3", 8 * 3), ("e2b1", 24 * 2),
                  ("e2b2", 8 * 2)):
        BOFF[nm] = _bo
        _bo += w
    assert _bo <= 332, _bo
    noise_sb = sb("noise_sb", (128, 16, 8), F32)
    usage_sb = sb("usage_sb", (1, 8), F32)
    rankm_sb = sb("rankm_sb", (1, 8), F32)
    gb3_sb = sb("gb3_sb", (8, 1), F32)
    affb_sb = sb("affb_sb", (8, 1), F32)

    pb = [ps(f"pb{i}", (128, 512), F32) for i in range(6)]
    pt0 = ps("pt0", (128, 1024), BF16)
    pt1 = ps("pt1", (128, 1024), BF16)

    s_xload = [sem(f"s_xload{j}") for j in range(3)]
    s_ln1 = sem("s_ln1")
    s_sq1 = sem("s_sq1")
    s_ap1 = sem("s_ap1")
    s_tp1 = sem("s_tp1")
    s_ev1 = sem("s_ev1")
    s_cload = sem("s_cload")
    s_wcast = sem("s_wcast")
    s_gload = sem("s_gload")
    s_sc = sem("s_sc")
    s_exp = sem("s_exp")
    s_pn = sem("s_pn")
    s_pt = sem("s_pt")
    s_pte = sem("s_pte")
    s_av = sem("s_av")
    s_ave = sem("s_ave")
    s_op = sem("s_op")
    s_utp = sem("s_utp")
    s_ue = sem("s_ue")
    s_qtm = sem("s_qtm")
    s_ag = sem("s_ag")
    s_agl = sem("s_agl")
    s_gwl = sem("s_gwl")
    s_out = sem("s_out")
    s_dbg = sem("s_dbg")

    blk = es.enter_context(nc.Block())

    esL1b = ExitStack()
    O_fm = esL1b.enter_context(nc.sbuf_tensor("O_fm", [128, 8, TOK], BF16))
    outw_sb = esL1b.enter_context(nc.sbuf_tensor("outw_sb", [128, 8, EMB], BF16))

    esA = ExitStack()

    def sbA(name, shape, dt):
        return esA.enter_context(nc.sbuf_tensor(name, list(shape), dt))

    xln_fm = sbA("xln_fm", (128, 8, XROWS), BF16)
    cos_sb = sbA("cos_sb", (128, XROWS), BF16)
    sin_sb = sbA("sin_sb", (128, XROWS), BF16)

    @blk.sync
    def _(sy):
        s_cload.inc(sy.dma_start(out=identf[:], in_=D["identf"].ap()), 16)
        s_cload.inc(sy.dma_start(out=idx8[:], in_=D["idx8"].ap()), 16)
        s_cload.inc(sy.dma_start(out=negbig[:], in_=D["negbig"].ap()), 16)
        s_cload.inc(sy.dma_start(out=neg1e9[:], in_=D["neg1e9"].ap()), 16)
        s_cload.inc(sy.dma_start(out=noise_sb[:, :, :], in_=D["noise"].ap()), 16)
        s_cload.inc(sy.dma_start(out=usage_sb[:], in_=D["usage"].ap()), 16)
        s_cload.inc(sy.dma_start(out=rankm_sb[:], in_=D["rankmask"].ap()), 16)
        s_cload.inc(sy.dma_start(out=gb3_sb[:], in_=D["gb3"].ap()), 16)
        s_cload.inc(sy.dma_start(out=affb_sb[:], in_=D["affb"].ap()), 16)

    @blk.gpsimd
    def _(g):
        s_gload.inc(g.dma_start(out=identb[:], in_=D["identf"].ap()), 16)
        s_gload.inc(g.dma_start(out=ltri_b[:], in_=D["ltri"].ap()), 16)
        s_gload.inc(g.dma_start(out=r2560[:], in_=D["r256"].ap()), 16)
        s_gload.inc(g.dma_start(out=gc16_sb[:], in_=D["gc16"].ap()), 16)
        s_gload.inc(g.dma_start(out=rltm_sb[:], in_=D["rltm"].ap()), 16)

        s_gload.inc(g.dma_start(out=masks[:, 0:128], in_=D["mband"].ap()), 16)
        s_gload.inc(g.dma_start(out=masks[:, 128:256], in_=D["mdiag"].ap()), 16)
        s_gload.inc(g.dma_start(out=masks[:, 256:384], in_=D["mA0"].ap()), 16)
        s_gload.inc(g.dma_start(out=masks[:, 384:512], in_=D["rperm"].ap()), 16)
        s_gload.inc(g.dma_start(out=masks[:, 512:516], in_=D["gmask0"].ap()), 16)
        for nm in ("gb1", "gb2", "lnsc", "lnof"):
            w = {"gb1": 8, "gb2": 4, "lnsc": 8, "lnof": 8}[nm]
            s_gload.inc(g.dma_start(
                out=biases[:, BOFF[nm]:BOFF[nm] + w], in_=D[nm].ap()), 16)
        for nm, per in (("e0b1", 32), ("e0b2", 8), ("e1b1", 16), ("e1b2", 16),
                        ("e1b3", 8)):
            for e in range(3):
                s_gload.inc(g.dma_start(
                    out=biases[:, BOFF[nm] + per * e:BOFF[nm] + per * (e + 1)],
                    in_=D[nm].ap()[e]), 16)
        for nm, per in (("e2b1", 24), ("e2b2", 8)):
            for e in range(2):
                s_gload.inc(g.dma_start(
                    out=biases[:, BOFF[nm] + per * e:BOFF[nm] + per * (e + 1)],
                    in_=D[nm].ap()[e]), 16)
        s_gload.inc(g.dma_start(out=cos_sb[:], in_=D["cosb"].ap()), 16)
        s_gload.inc(g.dma_start(out=sin_sb[:], in_=D["sinb"].ap()), 16)
        wsrc = D["outwT"].ap().rearrange("(c p) e -> p c e", p=128)
        s_gload.inc(g.dma_start(out=outw_sb[:, :, :], in_=wsrc), 16)

    @blk.vector
    def _(v):
        s_cload.wait(v)
        s_gload.wait(v)
        v.memset(ones1[:], 1.0)
        v.memset(scal[:], 0.0)
        v.memset(gcols[:], 0.0)
        s_ap1.inc(v.memset(onesc[:], 1.0))

    AP1_BASE = s_ap1.n

    mband = masks[:, 0:128]
    mdiag = masks[:, 128:256]
    mA0 = masks[:, 256:384]
    rpermb = masks[:, 384:512]
    gmask0 = masks[:, 512:516]

    esA1 = ExitStack()
    xr = [esA1.enter_context(nc.sbuf_tensor(f"xr{j}", [128, EMB], F32))
          for j in range(3)]
    xlt = [esA1.enter_context(nc.sbuf_tensor(f"xlt{j}", [128, EMB], BF16))
           for j in range(2)]
    sq_scr = esA1.enter_context(nc.sbuf_tensor("sq_scr", [128, EMB], F32))
    lncols = esA1.enter_context(nc.sbuf_tensor("lncols", [128, 24], F32))

    @blk.sync
    def _(sy):
        for i in range(NXT):
            if i >= 3:
                s_ap1.wait(sy, at=AP1_BASE + (i - 2))
            s_xload[i % 3].inc(sy.dma_start(
                out=xr[i % 3][:], in_=D["xa"].ap()[128 * i:128 * (i + 1), :]),
                16)

    def _ln_stats(v, x, cols):
        mn = cols[:, 0:1]; rstd = cols[:, 1:2]
        ssq = cols[:, 2:3]; t1 = cols[:, 3:4]; t2 = cols[:, 4:5]
        v.tensor_reduce(mn, x, AX.X, OP.add, negate=True)
        v.tensor_tensor(sq_scr[:], x, x, OP.mult)
        v.drain()
        v.tensor_reduce(ssq, sq_scr[:], AX.X, OP.add)
        v.tensor_tensor(t1, mn, mn, OP.mult)
        v.drain()
        v.scalar_tensor_tensor(ssq, t1, FP(-1.0 / EMB), ssq, OP.mult, OP.add)
        v.drain()
        v.tensor_scalar(t2, ssq, FP(1.0 / EMB), LN_EPS,
                        OP.mult, OP.add)
        v.drain()
        return v.reciprocal(rstd, t2)

    @blk.vector
    def _(v):
        for i in range(NXT):
            cols = lncols[:, 6 * (i % 4):6 * (i % 4) + 6]
            s_xload[i % 3].wait(v, at=16 * (i // 3 + 1))
            s_ln1.inc(_ln_stats(v, xr[i % 3][:], cols))
            s_sq1.wait(v, at=i + 1)
            if i >= 2:
                s_tp1.wait(v, at=i - 1)
            mn = cols[:, 0:1]; rstd = cols[:, 1:2]; t1 = cols[:, 3:4]
            v.tensor_scalar(t1, mn, FP(1.0 / EMB), None, OP.mult)
            v.drain()
            s_ap1.inc(v.tensor_scalar(xlt[i % 2][:], xr[i % 3][:], t1, rstd,
                                      OP.add, OP.mult))

    @blk.scalar
    def _(s):
        for i in range(NXT + 2):
            if i < NXT:
                cols = lncols[:, 6 * (i % 4):6 * (i % 4) + 6]
                s_ln1.wait(s, at=i + 1)
                s_sq1.inc(s.activation(cols[:, 1:2], cols[:, 1:2], AF.Sqrt))
            if i >= 2:
                j = i - 2
                s_tp1.wait(s, at=j + 1)
                bank = (pt0 if j % 2 == 0 else pt1)
                s_ev1.inc(s.copy(xln_fm[:, :, 128 * j:128 * (j + 1)],
                                 bank[:].rearrange("p (c j) -> p c j", c=8)))

    @blk.tensor
    def _(t):
        s_cload.wait(t)
        s_gload.wait(t)
        for i in range(NXT):
            t.wait_ge(s_ap1.h, AP1_BASE + i + 1)
            bank = (pt0 if i % 2 == 0 else pt1)
            if i >= 2:
                s_ev1.wait(t, at=i - 1)
            for j in range(8):
                ins = t.transpose(bank[:, 128 * j:128 * (j + 1)],
                                  xlt[i % 2][:, 128 * j:128 * (j + 1)],
                                  identb[:])
            s_tp1.inc(ins)


    esA1.close()
    esB = ExitStack()

    def sbB(name, shape, dt):
        return esB.enter_context(nc.sbuf_tensor(name, list(shape), dt))

    wq = [sbB(f"wq{j}", (128, 8, 128), BF16) for j in range(2)]
    wk = [sbB(f"wk{j}", (128, 8, 128), BF16) for j in range(2)]
    wv = [sbB(f"wv{j}", (128, 8, 128), BF16) for j in range(2)]
    qa_sb = sbB("qa_sb", (128, TOK), BF16)
    ka_sb = sbB("ka_sb", (128, XROWS), BF16)
    qrot = sbB("qrot", (128, TOK), BF16)
    krot = sbB("krot", (128, XROWS), BF16)
    rsc1 = sbB("rsc1", (128, 512), F32)
    rsc2 = sbB("rsc2", (128, 512), F32)
    vtmg = [sbB(f"vtmg{j}", (128, 18, 128), BF16) for j in range(2)]
    p_sb = [sbB(f"p_sb{j}", (128, 260), BF16) for j in range(2)]
    pn_sb = [sbB(f"pn_sb{j}", (128, 260), BF16) for j in range(2)]
    pt_sb = [sbB(f"pt_sb{j}", (128, 256), BF16) for j in range(2)]
    ptg_sb = [sbB(f"ptg_sb{j}", (4, 128), BF16) for j in range(2)]
    acols = sbB("acols", (128, 8), F32)

    s_wq = [sem(f"s_wq{j}") for j in range(2)]
    s_wk = [sem(f"s_wk{j}") for j in range(2)]
    s_wv = [sem(f"s_wv{j}") for j in range(2)]
    s_qkq = sem("s_qkq")
    s_qkk = sem("s_qkk")
    s_qkv = sem("s_qkv")
    s_qaev = sem("s_qaev")
    s_kaev = sem("s_kaev")
    s_vev = sem("s_vev")
    s_pmq = sem("s_pmq")
    s_pmk = sem("s_pmk")
    s_rpq = sem("s_rpq")
    s_rpk = sem("s_rpk")
    s_rmax = sem("s_rmax")

    NI = 32
    KCH = [(0, 512), (512, 512), (1024, 512), (1536, 512), (2048, 256)]
    QCH = [(128, 512), (640, 512), (1152, 512), (1664, 512)]

    def hq(i):
        return (i % 2, i // 2)

    CAST_CHUNKS = []
    for snm, n_e, R, C in (("e0w1", 3, 1024, 4096), ("e1w1", 3, 1024, 2048),
                           ("e1w2", 3, 2048, 2048), ("e2w1", 2, 1024, 3072)):
        rc = 128 if C >= 3072 else 256
        for ei in range(n_e):
            for ch in range(R // rc):
                CAST_CHUNKS.append((snm, ei, rc * ch, rc))
    NCAST = len(CAST_CHUNKS)

    @blk.gpsimd
    def _(g):
        s_tp1.wait(g, at=NXT)
        ci = 0
        for gi in range(8):
            j = gi % 2
            if gi >= 2:
                s_qkq.wait(g, at=4 * (gi - 1))
                s_qkk.wait(g, at=5 * (gi - 1))
                s_qkv.wait(g, at=5 * (gi - 1))
            qsrc = D["qkvwT"].ap()[:, 128 * gi:128 * (gi + 1)]
            ksrc = D["qkvwT"].ap()[:, 1024 + 128 * gi:1024 + 128 * (gi + 1)]
            vsrc = D["qkvwT"].ap()[:, 2048 + 128 * gi:2048 + 128 * (gi + 1)]
            s_wq[j].inc(g.dma_start(
                out=wq[j][:, :, :],
                in_=qsrc.rearrange("(c p) e -> p c e", p=128)), 16)
            s_wk[j].inc(g.dma_start(
                out=wk[j][:, :, :],
                in_=ksrc.rearrange("(c p) e -> p c e", p=128)), 16)
            s_wv[j].inc(g.dma_start(
                out=wv[j][:, :, :],
                in_=vsrc.rearrange("(c p) e -> p c e", p=128)), 16)
            if gi >= 1:
                ntake = NCAST * gi // 7 - ci if gi < 7 else NCAST - ci
                for _ in range(ntake):
                    snm, ei, r0, rc = CAST_CHUNKS[ci]
                    s_wcast.inc(g.dma_start(
                        out=D["c_" + snm].ap()[ei][r0:r0 + rc, :],
                        in_=D[snm].ap()[ei][r0:r0 + rc, :]), 16)
                    ci += 1

    class BankPair:

        def __init__(self):
            self.uses = []

        def acquire(self, eng):
            if len(self.uses) >= 2:
                sem, cnt = self.uses[-2]
                sem.wait(eng, at=cnt)

        def commit(self, release_sem, release_cnt):
            self.uses.append((release_sem, release_cnt))

    def emit_pt(t, io):
        ptbank = pt0 if io % 2 == 0 else pt1
        s_pn.wait(t, at=io + 1)
        if io >= 2:
            s_pte.wait(t, at=io - 1)
        t.transpose(ptbank[:, 0:128], pn_sb[io % 2][:, 0:128], identb[:])
        t.transpose(ptbank[:, 128:256], pn_sb[io % 2][:, 128:256], identb[:])
        s_pt.inc(t.transpose(ptbank[0:4, 256:384],
                             pn_sb[io % 2][:, 256:260], identb[:]))

    def emit_av(t, io):
        go = io // NI
        ho, qto = hq(io % NI)
        bank2 = pb[4 + io % 2]
        s_pte.wait(t, at=io + 1)
        if io >= 2:
            s_ave.wait(t, at=io - 1)
        vt = vtmg[go % 2]
        t.matmul(bank2[0:64, 0:128], vt[:, qto, 64 * ho:64 * ho + 64],
                 pt_sb[io % 2][:, 0:128], start=True, stop=False)
        t.matmul(bank2[0:64, 0:128], vt[:, qto + 1, 64 * ho:64 * ho + 64],
                 pt_sb[io % 2][:, 128:256], start=False, stop=False)
        s_av.inc(t.matmul(bank2[0:64, 0:128],
                          vt[0:4, 17, 64 * ho:64 * ho + 64],
                          ptg_sb[io % 2][:, :], start=False, stop=True))


    bp01 = BankPair()
    bp23 = BankPair()

    @blk.tensor
    def _(t):
        s_ev1.wait(t)
        for gi in range(8):
            j = gi % 2
            s_wq[j].wait(t, at=16 * (gi // 2 + 1))
            for tc, (o, w) in enumerate(QCH):
                bank = pb[tc % 2]
                bp01.acquire(t)
                for k in range(8):
                    ins = t.matmul(bank[:, 0:w], wq[j][:, k, :],
                                   xln_fm[:, k, o:o + w],
                                   start=(k == 0), stop=(k == 7))
                s_qkq.inc(ins)
                bp01.commit(s_qaev, gi * 4 + tc + 1)
            s_wk[j].wait(t, at=16 * (gi // 2 + 1))
            for tc, (o, w) in enumerate(KCH):
                bank = pb[tc % 2]
                bp01.acquire(t)
                for k in range(8):
                    ins = t.matmul(bank[:, 0:w], wk[j][:, k, :],
                                   xln_fm[:, k, o:o + w],
                                   start=(k == 0), stop=(k == 7))
                s_qkk.inc(ins)
                bp01.commit(s_kaev, gi * 5 + tc + 1)
            s_qaev.wait(t, at=4 * (gi + 1))
            for tc, (o, w) in enumerate(QCH):
                bank = pb[2 + tc % 2]
                bp23.acquire(t)
                s_pmq.inc(t.matmul(bank[:, 0:w], rpermb,
                                   qa_sb[:, o - 128:o - 128 + w],
                                   start=True, stop=True))
                bp23.commit(s_rpq, gi * 4 + tc + 1)
            s_kaev.wait(t, at=5 * (gi + 1))
            for tc, (o, w) in enumerate(KCH):
                bank = pb[2 + tc % 2]
                bp23.acquire(t)
                s_pmk.inc(t.matmul(bank[:, 0:w], rpermb, ka_sb[:, o:o + w],
                                   start=True, stop=True))
                bp23.commit(s_rpk, gi * 5 + tc + 1)
            s_wv[j].wait(t, at=16 * (gi // 2 + 1))
            for bg in range(5):
                bank = pb[bg % 2]
                r0 = 4 * bg
                nt = min(4, 18 - r0)
                bp01.acquire(t)
                for r in range(r0, r0 + nt):
                    for k in range(8):
                        ins = t.matmul(
                            bank[:, 128 * (r - r0):128 * (r - r0) + 128],
                            xln_fm[:, k, 128 * r:128 * (r + 1)],
                            wv[j][:, k, :],
                            start=(k == 0 and r == r0),
                            stop=(k == 7 and r == r0 + nt - 1))
                s_qkv.inc(ins)
                bp01.commit(s_vev, gi * 5 + bg + 1)
            s_rpq.wait(t, at=4 * (gi + 1))
            s_rpk.wait(t, at=5 * (gi + 1))
            s_vev.wait(t, at=5 * (gi + 1))
            for i in range(NI):
                h, qt = hq(i)
                ii = gi * NI + i
                bank = pb[2 + i % 2]
                bp23.acquire(t)
                q_sl = qrot[64 * h:64 * h + 64, 128 * qt:128 * qt + 128]
                t.matmul(bank[:, 0:128], q_sl,
                         krot[64 * h:64 * h + 64, 128 * qt:128 * qt + 128],
                         start=True, stop=False)
                t.matmul(bank[:, 128:256], q_sl,
                         krot[64 * h:64 * h + 64,
                              128 * qt + 128:128 * qt + 256],
                         start=False, stop=False)
                t.matmul(bank[:, 256:260], q_sl,
                         krot[64 * h:64 * h + 64, 2176:2180],
                         start=False, stop=False)
                masks_todo = [(slice(128, 256), mdiag)]
                if qt % 2 == 0:
                    masks_todo.append((slice(0, 128), mA0 if qt == 0 else mband))
                if qt in (0, 1):
                    masks_todo.append((slice(256, 260), gmask0))
                for mi, (sl, mt) in enumerate(masks_todo):
                    ins = t.matmul(bank[:, sl], identb[:], mt,
                                   start=False, stop=(mi == len(masks_todo) - 1))
                s_sc.inc(ins)
                bp23.commit(s_exp, ii + 1)
                if ii >= 1:
                    emit_pt(t, ii - 1)
                if ii >= 2:
                    emit_av(t, ii - 2)
        NTOT = 8 * NI
        emit_pt(t, NTOT - 1)
        emit_av(t, NTOT - 2)
        emit_av(t, NTOT - 1)

    @blk.scalar
    def _(s):
        for gi in range(8):
            for tc, (o, w) in enumerate(QCH):
                s_qkq.wait(s, at=gi * 4 + tc + 1)
                if gi > 0 and tc == 0:
                    s_rpq.wait(s, at=4 * gi)
                    s_pmq.wait(s, at=4 * gi)
                s_qaev.inc(s.copy(qa_sb[:, o - 128:o - 128 + w],
                                  pb[tc % 2][:, 0:w]))
            for tc, (o, w) in enumerate(KCH):
                s_qkk.wait(s, at=gi * 5 + tc + 1)
                if gi > 0 and tc == 0:
                    s_rpk.wait(s, at=5 * gi)
                    s_pmk.wait(s, at=5 * gi)
                s_kaev.inc(s.copy(ka_sb[:, o:o + w], pb[tc % 2][:, 0:w]))
            for bg in range(5):
                r0 = 4 * bg
                nt = min(4, 18 - r0)
                s_qkv.wait(s, at=gi * 5 + bg + 1)
                if gi >= 2 and bg == 0:
                    s_av.wait(s, at=(gi - 1) * NI)
                s_vev.inc(s.copy(
                    vtmg[gi % 2][:, r0:r0 + nt, :],
                    pb[bg % 2][:, 0:128 * nt].rearrange(
                        "p (r j) -> p r j", r=nt)))
            for i in range(NI):
                h, qt = hq(i)
                ii = gi * NI + i
                bank = pb[2 + i % 2]
                s_rmax.wait(s, at=ii + 1)
                s_exp.inc(s.activation(
                    p_sb[i % 2][:, 0:260], bank[:, 0:260], AF.Exp,
                    bias=acols[:, 4 + i % 2:5 + i % 2], scale=0.125,
                    accum_out=acols[:, 6 + i % 2:7 + i % 2]))
                if ii >= 1:
                    io = ii - 1
                    ptbank = pt0 if io % 2 == 0 else pt1
                    s_pt.wait(s, at=io + 1)
                    if io >= 2:
                        s_av.wait(s, at=io - 1)
                    s.copy(pt_sb[io % 2][:, 0:256], ptbank[:, 0:256])
                    s_pte.inc(s.copy(ptg_sb[io % 2][:, :],
                                     ptbank[0:4, 256:384]))
                if ii >= 2:
                    io = ii - 2
                    go = io // NI
                    ho, qto = hq(io % NI)
                    s_av.wait(s, at=io + 1)
                    s_ave.inc(s.copy(
                        O_fm[64 * ho:64 * ho + 64, go, 128 * qto:128 * qto + 128],
                        pb[4 + io % 2][0:64, 0:128]))
        NTOT = 8 * NI
        for io in (NTOT - 1,):
            ptbank = pt0 if io % 2 == 0 else pt1
            s_pt.wait(s, at=io + 1)
            s_av.wait(s, at=io - 1)
            s.copy(pt_sb[io % 2][:, 0:256], ptbank[:, 0:256])
            s_pte.inc(s.copy(ptg_sb[io % 2][:, :], ptbank[0:4, 256:384]))
        for io in (NTOT - 2, NTOT - 1):
            go = io // NI
            ho, qto = hq(io % NI)
            s_av.wait(s, at=io + 1)
            s_ave.inc(s.copy(
                O_fm[64 * ho:64 * ho + 64, go, 128 * qto:128 * qto + 128],
                pb[4 + io % 2][0:64, 0:128]))

    @blk.vector
    def _(v):
        for gi in range(8):
            for tc, (o, w) in enumerate(QCH):
                bank = pb[2 + tc % 2]
                s_pmq.wait(v, at=gi * 4 + tc + 1)
                if gi > 0 and tc == 0:
                    s_sc.wait(v, at=NI * gi)
                v.drain()
                v.tensor_tensor(rsc1[:, 0:w], qa_sb[:, o - 128:o - 128 + w],
                                cos_sb[:, o:o + w], OP.mult)
                v.tensor_tensor(rsc2[:, 0:w], bank[:, 0:w],
                                sin_sb[:, o:o + w], OP.mult)
                v.drain()
                s_rpq.inc(v.tensor_tensor(qrot[:, o - 128:o - 128 + w],
                                          rsc1[:, 0:w], rsc2[:, 0:w], OP.add))
            for tc, (o, w) in enumerate(KCH):
                bank = pb[2 + tc % 2]
                s_pmk.wait(v, at=gi * 5 + tc + 1)
                if gi > 0 and tc == 0:
                    s_sc.wait(v, at=NI * gi)
                v.drain()
                v.tensor_tensor(rsc1[:, 0:w], ka_sb[:, o:o + w],
                                cos_sb[:, o:o + w], OP.mult)
                v.tensor_tensor(rsc2[:, 0:w], bank[:, 0:w],
                                sin_sb[:, o:o + w], OP.mult)
                v.drain()
                s_rpk.inc(v.tensor_tensor(krot[:, o:o + w],
                                          rsc1[:, 0:w], rsc2[:, 0:w], OP.add))
            for i in range(NI):
                ii = gi * NI + i
                bank = pb[2 + i % 2]
                s_sc.wait(v, at=ii + 1)
                v.tensor_reduce(acols[:, 0 + i % 2:1 + i % 2], bank[:, 0:260],
                                AX.X, OP.max, negate=True)
                v.drain()
                s_rmax.inc(v.tensor_scalar(acols[:, 4 + i % 2:5 + i % 2],
                                           acols[:, 0 + i % 2:1 + i % 2],
                                           0.125, None, OP.mult))
                s_exp.wait(v, at=ii + 1)
                v.reciprocal(acols[:, 2 + i % 2:3 + i % 2],
                             acols[:, 6 + i % 2:7 + i % 2])
                v.drain()
                if ii >= 1:
                    s_pt.wait(v, at=ii - 1)
                s_pn.inc(v.tensor_scalar(pn_sb[i % 2][:, 0:260],
                                         p_sb[i % 2][:, 0:260],
                                         acols[:, 2 + i % 2:3 + i % 2],
                                         None, OP.mult))

    if "attn" in debug:
        @blk.sync
        def _(sy):
            s_ave.wait(sy)
            s_dbg.inc(sy.dma_start(out=dbg["d_ofm"].ap(), in_=O_fm[:]), 16)

    if "attn2" in debug:
        @blk.sync
        def _(sy):
            s_ave.wait(sy)
            s_dbg.inc(sy.dma_start(out=dbg["d_qrot"].ap(), in_=qrot[:]), 16)
            s_dbg.inc(sy.dma_start(out=dbg["d_krot"].ap(), in_=krot[:]), 16)
            s_dbg.inc(sy.dma_start(out=dbg["d_qa"].ap(), in_=qa_sb[:]), 16)
            s_dbg.inc(sy.dma_start(out=dbg["d_vtm"].ap()[0], in_=vtmg[0][:]), 16)
            s_dbg.inc(sy.dma_start(out=dbg["d_vtm"].ap()[1], in_=vtmg[1][:]), 16)

    esB.close()
    esA.close()

    NTOT = 8 * NI
    esC = ExitStack()

    def sbC(name, shape, dt):
        return esC.enter_context(nc.sbuf_tensor(name, list(shape), dt))

    xr2 = [sbC(f"xr2_{j}", (128, EMB), F32) for j in range(3)]
    y_sb = [sbC(f"y_sb{j}", (128, EMB), F32) for j in range(2)]
    xlt2 = [sbC(f"xlt2_{j}", (128, EMB), BF16) for j in range(2)]
    u_tm = [sbC(f"u_tm{j}", (128, EMB), BF16) for j in range(2)]
    ones128b = sbC("ones128b", (128, 128), BF16)
    ones128f = sbC("ones128f", (128, 128), F32)
    affw_sb = sbC("affw_sb", (128, 8, 8), BF16)
    dev_scr = sbC("dev_scr", (128, 128), F32)
    sq2_scr = sbC("sq2_scr", (128, 128), F32)
    rm_bc = sbC("rm_bc", (128, 8), F32)
    rankm_bf = sbC("rankm_bf", (1, 8), BF16)
    bis_scr = sbC("bis_scr", (128, 8, 128), F32)
    bis_mid = sbC("bis_mid", (128, 2, 8), F32)
    bis_cf = sbC("bis_cf", (128, 2, 8), F32)
    bis_cb = sbC("bis_cb", (128, 2, 8), BF16)
    bis_io = sbC("bis_io", (128, 8), F32)

    s_x2 = s_xload
    X2B = [sm.n for sm in s_x2]
    s_ysp = [sem(f"s_ysp{j}") for j in range(2)]
    s_yv = sem("s_yv")
    s_sq2 = sem("s_sq2")
    s_ln2 = sem("s_ln2")
    s_impw = sem("s_impw")
    s_u = sem("s_u")
    s_bv = [sem(f"s_bv{j}") for j in range(2)]
    s_bt = [sem(f"s_bt{j}") for j in range(2)]
    s_bhi = sem("s_bhi")
    s_tempc = sem("s_tempc")
    s_rmb = sem("s_rmb")
    s_mskc = sem("s_mskc")
    s_affm = sem("s_affm")
    s_selw = sem("s_selw")
    s_cc = sem("s_cc")

    @blk.gpsimd
    def _(g):
        s_ave.wait(g, at=NTOT)
        s_gwl.inc(g.dma_start(
            out=affw_sb[:, :, :],
            in_=D["affw"].ap().rearrange("(c p) e -> p c e", p=128)), 16)

    @blk.sync
    def _(sy):
        s_ave.wait(sy, at=NTOT)
        for ti in range(NOT + 2):
            if ti < NOT:
                if ti >= 3:
                    s_u.wait(sy, at=ti - 2)
                s_x2[ti % 3].inc(sy.dma_start(
                    out=xr2[ti % 3][:],
                    in_=D["xa"].ap()[128 * (ti + 1):128 * (ti + 2), :]), 16)
            if ti >= 2:
                tj = ti - 2
                s_yv.wait(sy, at=tj + 1)
                s_ysp[tj % 2].inc(sy.dma_start(
                    out=D["yspill"].ap()[128 * tj:128 * (tj + 1), :],
                    in_=y_sb[tj % 2][:]), 16)

    @blk.vector
    def _(v):
        s_ave.wait(v, at=NTOT)
        v.memset(ones128b[:], 1.0)
        v.memset(ones128f[:], 1.0)
        v.tensor_copy(rankm_bf[:], rankm_sb[:])
        for ti in range(NOT):
            cols = gcols[:, 10 * ti:10 * ti + 10]
            s_x2[ti % 3].wait(v, at=X2B[ti % 3] + 16 * (ti // 3 + 1))
            s_op.wait(v, at=2 * (ti + 1))
            if ti >= 2:
                s_ysp[ti % 2].wait(v, at=16 * (ti // 2))
            bank0 = pb[2 * (ti % 2)]
            bank1 = pb[2 * (ti % 2) + 1]
            v.tensor_tensor(y_sb[ti % 2][:, 0:512], xr2[ti % 3][:, 0:512],
                            bank0[:], OP.add)
            s_yv.inc(v.tensor_tensor(y_sb[ti % 2][:, 512:1024],
                                     xr2[ti % 3][:, 512:1024], bank1[:],
                                     OP.add))
            v.drain()
            s_ln1.inc(_ln_stats(v, y_sb[ti % 2][:], cols))
            s_sq2.wait(v, at=ti + 1)
            mn = cols[:, 0:1]; rstd = cols[:, 1:2]; t1 = cols[:, 3:4]
            imp2 = cols[:, 5:6]; L1c = cols[:, 6:7]; comb = cols[:, 7:8]
            v.tensor_scalar(t1, mn, FP(1.0 / EMB), None, OP.mult)
            v.tensor_tensor(imp2, rstd, rstd, OP.mult)
            v.drain()
            v.tensor_scalar(xlt2[ti % 2][:], y_sb[ti % 2][:], t1, rstd,
                            OP.add, OP.mult)
            s_ln2.inc(v.tensor_tensor(imp2, imp2, cols[:, 2:3], OP.mult))
            v.drain()
            v.tensor_reduce(L1c, xlt2[ti % 2][:], AX.X, OP.add,
                            apply_absolute_value=True)
            v.drain()
            v.tensor_scalar(L1c, L1c, 1.0, FP(1e-8), OP.mult, OP.add)
            v.drain()
            v.reciprocal(L1c, L1c)
            v.drain()
            v.tensor_tensor(comb, rstd, L1c, OP.mult)
            if ti >= 2:
                s_utp.wait(v, at=ti - 1)
            v.drain()
            s_u.inc(v.tensor_scalar(u_tm[ti % 2][:], y_sb[ti % 2][:], t1,
                                    comb, OP.add, OP.mult))

    @blk.scalar
    def _(s):
        for ti in range(NOT + 2):
            if ti < NOT:
                cols = gcols[:, 10 * ti:10 * ti + 10]
                s_ln1.wait(s, at=NXT + ti + 1)
                s_sq2.inc(s.activation(cols[:, 1:2], cols[:, 1:2], AF.Sqrt))
                s_ln2.wait(s, at=ti + 1)
                s_impw.inc(s.activation(xchg[:, ti:ti + 1], cols[:, 5:6],
                                        AF.Sqrt))
            if ti >= 2:
                j = ti - 2
                s_utp.wait(s, at=j + 1)
                bank = (pt0 if j % 2 == 0 else pt1)
                s_ue.inc(s.copy(u_fm[:, :, 128 * j:128 * (j + 1)],
                                bank[:].rearrange("p (c j) -> p c j", c=8)))

    @blk.tensor
    def _(t):
        s_ave.wait(t, at=NTOT)
        for ti in range(NOT + 2):
            if ti < NOT:
                for ec in range(2):
                    bank = pb[2 * (ti % 2) + ec]
                    if ti >= 2:
                        s_yv.wait(t, at=ti - 1)
                    for k in range(8):
                        ins = t.matmul(bank[:],
                                       O_fm[:, k, 128 * ti:128 * (ti + 1)],
                                       outw_sb[:, k, 512 * ec:512 * (ec + 1)],
                                       start=(k == 0), stop=(k == 7))
                    s_op.inc(ins)
            if ti >= 2:
                tj = ti - 2
                t.wait_ge(s_u.h, tj + 1)
                bank = (pt0 if tj % 2 == 0 else pt1)
                if tj >= 2:
                    s_ue.wait(t, at=tj - 1)
                for j in range(8):
                    ins = t.transpose(bank[:, 128 * j:128 * (j + 1)],
                                      u_tm[tj % 2][:, 128 * j:128 * (j + 1)],
                                      identb[:])
                s_utp.inc(ins)
        s_gwl.wait(t)
        for ti in range(NOT):
            s_ue.wait(t, at=ti + 1)
            for k in range(8):
                ins = t.matmul(pb[4][:, 8 * ti:8 * ti + 8],
                               u_fm[:, k, 128 * ti:128 * (ti + 1)],
                               affw_sb[:, k, :],
                               start=(k == 0 and ti == 0),
                               stop=(k == 7 and ti == NOT - 1))
        s_qtm.inc(ins)

    @blk.scalar
    def _(s):
        s_qtm.wait(s)
        s_qtm.inc(s.copy(xchg[:, 16:144], pb[4][:, 0:128]))

    @blk.gpsimd
    def _(g):
        s_impw.wait(g, at=NOT)
        s_qtm.wait(g, at=2)
        g.dma_start(out=D["agin"].ap(), in_=xchg[:]).then_inc(s_ag.h, 16)
        s_ag.n += 16
        g.wait_ge(s_ag.h, 16)
        g.collective_compute(
            "AllGather", OP.bypass, replica_groups=[list(range(8))],
            ins=[D["agin"].ap()], outs=[D["agout"].ap()],
        ).then_inc(s_cc.h, 1)
        s_cc.n += 1
        g.wait_ge(s_cc.h, 1)
        s_agl.inc(g.dma_start(
            out=gath[:, :, :],
            in_=D["agout"].ap().rearrange("(r p) c -> p r c", p=128)), 16)

    BIS_R = 18
    B_LO = (SC_LO, 16)
    B_HI = (SC_HI, 17)
    B_W = (SC_MID, 18)
    B_NC = (SC_PRED, 19)
    B_TB = 20
    B_TB2 = 21

    @blk.tensor
    def _(t):
        s_bhi.wait(t, at=1)
        s_bhi.inc(t.transpose(pb[5][0:1, 0:128], scal[:, B_TB:B_TB + 1],
                              identf[:]))
        s_bhi.wait(t, at=3)
        s_bhi.inc(t.matmul(pb[5][:, 8:9], ones128f[0:1, :],
                           scal[0:1, B_TB2:B_TB2 + 1], start=True, stop=True))
        for r in range(BIS_R):
            s_bv[0].wait(t, at=r + 1)
            s_bt[0].inc(t.matmul(
                pb[5][:, 16 + 8 * (r % 2):16 + 8 * (r % 2) + 8],
                ones128b[:], bis_cb[:, 0, :], start=True, stop=True))
            s_bv[1].wait(t, at=2 * (r + 1))
            s_bt[1].inc(t.matmul(
                pb[5][:, 40 + 8 * (r % 2):40 + 8 * (r % 2) + 8],
                ones128b[:], bis_cb[:, 1, :], start=True, stop=True))
        s_tempc.wait(t, at=1)
        s_tempc.inc(t.matmul(pb[5][:, 4:5], ones128f[:],
                             scal[:, SC_CNT:SC_CNT + 1], start=True,
                             stop=True))
        s_tempc.wait(t, at=3)
        s_tempc.inc(t.matmul(pb[5][:, 5:6], ones128f[:],
                             scal[:, SC_CNT:SC_CNT + 1], start=True,
                             stop=True))
        s_bhi.wait(t, at=1)
        s_qtm.wait(t, at=2)
        s_rmb.inc(t.matmul(pb[4][:, 8:16], ones1[0:1, :], rankm_bf[0:1, :],
                           start=True, stop=True))
        s_mskc.wait(t, at=1)
        for r in range(8):
            for ti in range(NOT):
                ins = t.matmul(pb[5][0:8, 32:33],
                               gath[:, r, 16 + 8 * ti:16 + 8 * (ti + 1)],
                               dev_scr[:, 16 * r + ti:16 * r + ti + 1],
                               start=(r == 0 and ti == 0),
                               stop=(r == 7 and ti == NOT - 1))
        s_affm.inc(ins)

    @blk.vector
    def _(v):
        s_agl.wait(v)
        v.tensor_scalar(bis_io[:], idx8[:], 1.0, None, OP.add)
        v.drain()
        v.tensor_reduce(scal[:, B_TB:B_TB + 1], gath[:, :, 0:16], AX.XY,
                        OP.max)
        v.drain()
        s_bhi.inc(v.tensor_copy(scal[:, B_TB2:B_TB2 + 1],
                                scal[:, B_TB:B_TB + 1]))
        s_bhi.wait(v, at=2)
        v.tensor_reduce(scal[0:1, B_TB2:B_TB2 + 1], pb[5][0:1, 0:128], AX.X,
                        OP.max)
        v.drain()
        s_bhi.inc(v.tensor_copy(scal[0:1, B_TB:B_TB + 1],
                                scal[0:1, B_TB2:B_TB2 + 1]))
        s_bhi.wait(v, at=4)
        for srch in range(2):
            v.tensor_copy(scal[:, B_HI[srch]:B_HI[srch] + 1], pb[5][:, 8:9])
            v.memset(scal[:, B_LO[srch]:B_LO[srch] + 1], 0.0)
        v.drain()
        for srch in range(2):
            v.tensor_scalar(scal[:, B_W[srch]:B_W[srch] + 1],
                            scal[:, B_HI[srch]:B_HI[srch] + 1], 0.125, None,
                            OP.mult)
        for r in range(BIS_R):
            for srch in range(2):
                LO = scal[:, B_LO[srch]:B_LO[srch] + 1]
                HI = scal[:, B_HI[srch]:B_HI[srch] + 1]
                W = scal[:, B_W[srch]:B_W[srch] + 1]
                NC = scal[:, B_NC[srch]:B_NC[srch] + 1]
                v.drain()
                v.tensor_scalar(bis_mid[:, srch, :], bis_io[:], W, LO,
                                OP.mult, OP.add)
                v.drain()
                for m in range(8):
                    v.tensor_scalar(bis_scr[:, m, :], gath[:, :, 0:16],
                                    bis_mid[:, srch, m:m + 1], None,
                                    OP.is_le, OP.add,
                                    accum_out=bis_cf[:, srch, m:m + 1])
                v.drain()
                ins = v.tensor_copy(bis_cb[:, srch, :], bis_cf[:, srch, :])
                s_bv[srch].inc(ins)
            for srch in range(2):
                LO = scal[:, B_LO[srch]:B_LO[srch] + 1]
                HI = scal[:, B_HI[srch]:B_HI[srch] + 1]
                W = scal[:, B_W[srch]:B_W[srch] + 1]
                NC = scal[:, B_NC[srch]:B_NC[srch] + 1]
                KV = K1 if srch == 0 else K2
                base = 16 if srch == 0 else 40
                reg = pb[5][:, base + 8 * (r % 2):base + 8 * (r % 2) + 8]
                TB = scal[:, B_TB:B_TB + 1]
                TB2 = scal[:, B_TB2:B_TB2 + 1]
                TMP = scal[:, SC_TMP:SC_TMP + 1]
                s_bt[srch].wait(v, at=r + 1)
                v.tensor_scalar(bis_scr[:, 0, 0:8], reg, float(KV), None,
                                OP.is_lt, OP.add, accum_out=NC)
                v.drain()
                v.tensor_scalar(TB, NC, 1.0, None, OP.add)
                v.drain()
                v.tensor_scalar(TB, TB, W, LO, OP.mult, OP.add)
                v.tensor_scalar(TB2, NC, 8.0, None, OP.is_ge)
                v.drain()
                v.tensor_tensor(TMP, HI, TB, OP.subtract)
                v.drain()
                v.scalar_tensor_tensor(HI, TMP, TB2, TB, OP.mult, OP.add)
                v.tensor_scalar(LO, NC, W, LO, OP.mult, OP.add)
                v.drain()
                v.tensor_tensor(W, HI, LO, OP.subtract)
                v.drain()
                ins = v.tensor_scalar(W, W, 0.125, None, OP.mult)
                if srch == 0:
                    s_bv[1].inc(ins)
        v.drain()
        v.tensor_copy(scal[:, SC_S1:SC_S1 + 1], scal[:, B_HI[0]:B_HI[0] + 1])
        v.tensor_copy(scal[:, SC_S2:SC_S2 + 1], scal[:, B_HI[1]:B_HI[1] + 1])
        v.drain()
        v.tensor_scalar(scal[:, SC_T1:SC_T1 + 1], scal[:, SC_S1:SC_S1 + 1],
                        float(_LW), None, OP.mult)
        v.tensor_scalar(scal[:, SC_T2:SC_T2 + 1], scal[:, SC_S2:SC_S2 + 1],
                        float(_HW), None, OP.mult)
        v.drain()
        v.tensor_tensor(scal[:, SC_THR:SC_THR + 1], scal[:, SC_T1:SC_T1 + 1],
                        scal[:, SC_T2:SC_T2 + 1], OP.add)
        v.drain()
        v.tensor_scalar(dev_scr[:], gath[:, :, 0:16], 1.0, None,
                        OP.mult, OP.add, accum_out=scal[:, SC_T1:SC_T1 + 1])
        v.drain()
        s_tempc.inc(v.tensor_copy(scal[:, SC_CNT:SC_CNT + 1],
                                  scal[:, SC_T1:SC_T1 + 1]))
        s_tempc.wait(v, at=2)
        v.tensor_scalar(scal[:, SC_TMP:SC_TMP + 1], pb[5][:, 4:5],
                        FP(1.0 / 16384.0), None, OP.mult)
        v.drain()
        v.tensor_scalar(dev_scr[:], gath[:, :, 0:16],
                        scal[:, SC_TMP:SC_TMP + 1], None, OP.subtract)
        v.drain()
        v.tensor_tensor(dev_scr[:], dev_scr[:], dev_scr[:], OP.mult)
        v.drain()
        v.tensor_scalar(sq2_scr[:], dev_scr[:], 1.0, None, OP.mult,
                        OP.add, accum_out=scal[:, SC_T2:SC_T2 + 1])
        v.drain()
        s_tempc.inc(v.tensor_copy(scal[:, SC_CNT:SC_CNT + 1],
                                  scal[:, SC_T2:SC_T2 + 1]))
        s_tempc.wait(v, at=4)
        s_tempc.inc(v.tensor_scalar(scal[:, SC_TMP2:SC_TMP2 + 1],
                                    pb[5][:, 5:6], FP(1.0 / 16384.0), None,
                                    OP.mult))
        s_tempc.wait(v, at=6)
        v.tensor_scalar(scal[:, SC_RTEMP:SC_RTEMP + 1],
                        scal[:, SC_TMP2:SC_TMP2 + 1], 0.1, 1.0, OP.mult,
                        OP.add)
        v.drain()
        v.reciprocal(scal[:, SC_RTEMP:SC_RTEMP + 1],
                     scal[:, SC_RTEMP:SC_RTEMP + 1])
        v.drain()
        v.tensor_scalar(scal[:, 32:48], xchg[:, 0:16],
                        scal[:, SC_THR:SC_THR + 1], None, OP.is_gt)
        v.drain()
        ins = v.tensor_scalar(scal[:, 32:48], scal[:, 32:48],
                              scal[:, SC_RTEMP:SC_RTEMP + 1], None, OP.mult)
        s_selw.inc(ins)
        s_rmb.wait(v, at=2)
        for r in range(8):
            v.tensor_scalar(dev_scr[:, 16 * r:16 * r + 16],
                            gath[:, r, 0:16], scal[:, SC_THR:SC_THR + 1],
                            None, OP.is_gt)
        v.drain()
        for r in range(8):
            ins = v.tensor_scalar(dev_scr[:, 16 * r:16 * r + 16],
                                  dev_scr[:, 16 * r:16 * r + 16],
                                  rm_bc[:, r:r + 1], None, OP.mult)
        s_mskc.inc(ins)

    @blk.scalar
    def _(s):
        s_tempc.wait(s, at=5)
        s_tempc.inc(s.activation(scal[:, SC_TMP2:SC_TMP2 + 1],
                                 scal[:, SC_TMP2:SC_TMP2 + 1], AF.Sqrt))
        s_rmb.wait(s, at=1)
        s_rmb.inc(s.copy(rm_bc[:], pb[4][:, 8:16]))
        s_affm.wait(s, at=1)
        s_affm.inc(s.copy(scal[0:8, SC_AFF:SC_AFF + 1], pb[5][0:8, 32:33]))

    esC.close()
    esL1b.close()
    esD0 = ExitStack()

    def sbD0(name, shape, dt):
        return esD0.enter_context(nc.sbuf_tensor(name, list(shape), dt))

    logits_tm = sbD0("logits_tm", (128, 16, 8), F32)
    w_tm = sbD0("w_tm", (128, 16, 8), F32)
    tk8a = sbD0("tk8a", (128, 16, 8), F32)
    tk8b = sbD0("tk8b", (128, 16, 8), F32)
    tk8c = sbD0("tk8c", (128, 16, 8), F32)
    tkm_a = sbD0("tkm_a", (128, 16), F32)
    tkm_b = sbD0("tkm_b", (128, 16), F32)
    tkm_c = sbD0("tkm_c", (128, 16), F32)
    srow = sbD0("srow", (1, 16), F32)
    e8c = sbD0("e8c", (8, 4), F32)
    n128 = sbD0("n128", (128, 1), F32)
    ones1f = sbD0("ones1f", (1, 128), F32)
    selflat = sbD0("selflat", (1, 2048), F32)

    esD = ExitStack()

    def sbD(name, shape, dt):
        return esD.enter_context(nc.sbuf_tensor(name, list(shape), dt))

    gw1_sb = sbD("gw1_sb", (128, 8, 1024), BF16)
    gw2_sb = sbD("gw2_sb", (128, 8, 512), BF16)
    gw3_sb = sbD("gw3_sb", (128, 4, 8), BF16)
    g1_sb = sbD("g1_sb", (128, 8, TOK), BF16)
    g2_sb = sbD("g2_sb", (128, 4, TOK), BF16)
    sqg_sb = sbD("sqg_sb", (128, 8, 512), BF16)
    gscr = sbD("gscr", (128, 512), F32)
    gscr2 = sbD("gscr2", (128, 512), F32)
    grows = sbD("grows", (1, 2048), F32)
    lrow = sbD("lrow", (8, 512), F32)

    s_sst = sem("s_sst")
    s_sse = sem("s_sse")
    s_ssb = sem("s_ssb")
    s_xn = sem("s_xn")
    s_gwx = sem("s_gwx")
    s_g1 = sem("s_g1")
    s_g1e = sem("s_g1e")
    s_gsq = sem("s_gsq")
    s_gl = sem("s_gl")
    s_g2 = sem("s_g2")
    s_g2e = sem("s_g2e")
    s_g3 = sem("s_g3")
    s_lgr = sem("s_lgr")
    s_ltt = sem("s_ltt")
    s_ltm = sem("s_ltm")
    s_ns = sem("s_ns")
    s_tk = sem("s_tk")
    s_tke = sem("s_tke")

    @blk.gpsimd
    def _(g):
        s_affm.wait(g, at=2)
        s_mskc.wait(g, at=1)
        s_tempc.wait(g, at=6)
        s_selw.wait(g, at=1)
        for j in range(2):
            s_ysp[j].wait(g)
        for j in range(3):
            s_x2[j].wait(g)
        s_agl.wait(g)
        s_utp.wait(g, at=NOT)
        s_ue.wait(g, at=NOT)
        s_gwx.inc(g.dma_start(
            out=gw1_sb[:, :, :],
            in_=D["gw1"].ap().rearrange("(c p) e -> p c e", p=128)), 16)
        s_gwx.inc(g.dma_start(
            out=gw2_sb[:, :, :],
            in_=D["gw2"].ap().rearrange("(c p) e -> p c e", p=128)), 16)
        s_gwx.inc(g.dma_start(
            out=gw3_sb[:, :, :],
            in_=D["gw3"].ap().rearrange("(c p) e -> p c e", p=128)), 16)
        for snm, ei in (("e0w2", 0), ("e1w3", 0), ("e2w2", 0), ("e0w2", 1),
                        ("e1w3", 1), ("e2w2", 1), ("e0w2", 2), ("e1w3", 2)):
            R = {"e0w2": 4096, "e1w3": 2048, "e2w2": 3072}[snm]
            for ch in range(R // 512):
                s_wcast.inc(g.dma_start(
                    out=D["c_" + snm].ap()[ei][512 * ch:512 * ch + 512, :],
                    in_=D[snm].ap()[ei][512 * ch:512 * ch + 512, :]), 16)

    @blk.vector
    def _(v):
        s_affm.wait(v, at=2)
        v.memset(ones1f[:], 1.0)
        v.tensor_scalar(srow[:, 5:13], usage_sb[:, :], 1.0, FP(1e-8),
                        OP.mult, OP.add)
        v.drain()
        v.tensor_reduce(srow[:, 4:5], srow[:, 5:13], AX.X, OP.min)
        v.tensor_scalar(srow[:, 5:13], srow[:, 5:13], FP(1e-18), None,
                        OP.max)
        v.drain()
        s_ns.inc(v.tensor_scalar(srow[:, 0:1], srow[:, 4:5], 0.0, None,
                                 OP.is_ge))

    @blk.scalar
    def _(s):
        s_ns.wait(s, at=1)
        s_ns.inc(s.activation(srow[:, 5:13], srow[:, 5:13], AF.Ln))

    @blk.vector
    def _(v):
        s_ns.wait(v, at=2)
        v.tensor_tensor(srow[:, 5:13], usage_sb[:, :], srow[:, 5:13], OP.mult)
        v.drain()
        v.tensor_reduce(srow[:, 3:4], srow[:, 5:13], AX.X, OP.add,
                        negate=True)
        v.drain()
        v.tensor_scalar(srow[:, 2:3], srow[:, 3:4], -0.01, 0.02,
                        OP.mult, OP.add)
        v.drain()
        v.tensor_scalar(srow[:, 2:3], srow[:, 2:3], FP(1e-4), None, OP.max)
        v.drain()
        v.tensor_tensor(srow[:, 1:2], srow[:, 2:3], srow[:, 0:1], OP.mult)
        v.drain()
        s_ns.inc(v.tensor_scalar(srow[:, 1:2], srow[:, 1:2], FP(1.0 / 1.5),
                                 None, OP.mult))

    @blk.tensor
    def _(t):
        s_ns.wait(t, at=3)
        t.matmul(pb[4][0:8, 0:1], ones1f[0:1, 0:8], srow[:, 0:1],
                 start=True, stop=True)
        t.matmul(pb[4][0:8, 1:2], ones1f[0:1, 0:8], srow[:, 1:2],
                 start=True, stop=True)
        s_ns.inc(t.matmul(pb[4][0:128, 2:3], ones1f[0:1, :], srow[:, 1:2],
                          start=True, stop=True))

    @blk.scalar
    def _(s):
        s_ns.wait(s, at=4)
        s.copy(e8c[0:8, 0:2], pb[4][0:8, 0:2])
        s_ns.inc(s.copy(n128[:, 0:1], pb[4][0:128, 2:3]))

    @blk.vector
    def _(v):
        s_ns.wait(v, at=5)
        v.tensor_scalar(e8c[0:8, 2:3], gb3_sb[0:8, 0:1], FP(1.0 / 1.5), None,
                        OP.mult)
        v.drain()
        v.scalar_tensor_tensor(e8c[0:8, 2:3], scal[0:8, SC_AFF:SC_AFF + 1],
                               FP(0.3 / 1.5), e8c[0:8, 2:3], OP.mult, OP.add)
        v.drain()
        v.tensor_tensor(e8c[0:8, 2:3], e8c[0:8, 2:3], e8c[0:8, 0:1], OP.mult)
        s_ns.inc(v.tensor_scalar(e8c[0:8, 3:4], e8c[0:8, 0:1], FP(1.0 / 1.5),
                                 None, OP.mult))

    @blk.tensor
    def _(t):
        s_selw.wait(t, at=1)
        s_qtm.wait(t, at=2)
        s_ns.wait(t, at=5)
        s_affm.wait(t, at=2)
        s_tempc.wait(t, at=5)
        s_rmb.wait(t, at=2)
        for ti in range(NOT):
            bank = pb[4 + ti % 2]
            if ti >= 2:
                s_sse.wait(t, at=ti - 1)
                s_xn.wait(t, at=ti - 1)
            s_sst.inc(t.transpose(bank[0:1, 0:128],
                                  scal[:, 32 + ti:33 + ti], identf[:]))
            s_sse.wait(t, at=ti + 1)
            s_ssb.inc(t.matmul(bank[:, 0:128], ones1f[0:1, :],
                               selflat[0:1, 128 * ti:128 * ti + 128],
                               start=True, stop=True))

    @blk.scalar
    def _(s):
        for ti in range(NOT):
            s_sst.wait(s, at=ti + 1)
            s_sse.inc(s.copy(selflat[0:1, 128 * ti:128 * ti + 128],
                             pb[4 + ti % 2][0:1, 0:128]))

    @blk.vector
    def _(v):
        for ti in range(NOT):
            s_ssb.wait(v, at=ti + 1)
            for j in range(8):
                ins = v.tensor_tensor(
                    u_fm[:, j, 128 * ti:128 * (ti + 1)],
                    u_fm[:, j, 128 * ti:128 * (ti + 1)],
                    pb[4 + ti % 2][:, 0:128], OP.mult)
            s_xn.inc(ins)

    @blk.tensor
    def _(t):
        s_gwx.wait(t, at=48)
        s_xn.wait(t, at=NOT)
        for oc in range(8):
            for tc in range(4):
                n = 4 * oc + tc
                bank = pb[n % 4]
                if n >= 4:
                    s_g1e.wait(t, at=n - 3)
                for k in range(8):
                    ins = t.matmul(bank[:], gw1_sb[:, k, 128 * oc:128 * oc + 128],
                                   u_fm[:, k, 512 * tc:512 * tc + 512],
                                   start=(k == 0), stop=(k == 7))
                s_g1.inc(ins)

    @blk.scalar
    def _(s):
        for oc in range(8):
            for tc in range(4):
                n = 4 * oc + tc
                s_g1.wait(s, at=n + 1)
                s_g1e.inc(s.activation(
                    g1_sb[:, oc, 512 * tc:512 * tc + 512], pb[n % 4][:],
                    AF.Silu, bias=biases[:, BOFF["gb1"] + oc:BOFF["gb1"] + oc + 1]))

    @blk.vector
    def _(v):
        for tc in range(4):
            s_g1e.wait(v, at=32)
            for j in range(8):
                ins = v.tensor_tensor(sqg_sb[:, j, :],
                                      g1_sb[:, j, 512 * tc:512 * tc + 512],
                                      g1_sb[:, j, 512 * tc:512 * tc + 512],
                                      OP.mult)
            s_gsq.inc(ins)
            s_gl.wait(v, at=5 * tc + 1)
            v.tensor_scalar(grows[0:1, 0:512], pb[4][0:1, 0:512],
                            FP(1.0 / 1024.0), None, OP.mult)
            v.tensor_scalar(grows[0:1, 512:1024], pb[5][0:1, 0:512],
                            FP(1.0 / 1024.0), None, OP.mult)
            v.drain()
            v.tensor_tensor(grows[0:1, 1024:1536], grows[0:1, 0:512],
                            grows[0:1, 0:512], OP.mult)
            v.drain()
            v.tensor_tensor(grows[0:1, 512:1024], grows[0:1, 512:1024],
                            grows[0:1, 1024:1536], OP.subtract)
            v.drain()
            v.tensor_scalar(grows[0:1, 512:1024], grows[0:1, 512:1024], 1.0,
                            LN_EPS, OP.mult, OP.add)
            v.drain()
            s_gl.inc(v.reciprocal(grows[0:1, 512:1024],
                                  grows[0:1, 512:1024]))
            s_gl.wait(v, at=5 * tc + 4)
            for j in range(8):
                v.tensor_tensor(gscr[:], g1_sb[:, j, 512 * tc:512 * tc + 512],
                                pb[4][:, 0:512], OP.subtract)
                v.drain()
                v.tensor_tensor(gscr2[:], gscr[:], pb[5][:, 0:512], OP.mult)
                v.drain()
                ins = v.tensor_scalar(
                    g1_sb[:, j, 512 * tc:512 * tc + 512], gscr2[:],
                    biases[:, BOFF["lnsc"] + j:BOFF["lnsc"] + j + 1],
                    biases[:, BOFF["lnof"] + j:BOFF["lnof"] + j + 1],
                    OP.mult, OP.add)
                v.drain()
            s_gl.inc(ins)

    @blk.scalar
    def _(s):
        for tc in range(4):
            s_gl.wait(s, at=5 * tc + 2)
            s_gl.inc(s.activation(grows[0:1, 512:1024],
                                  grows[0:1, 512:1024], AF.Sqrt))

    @blk.tensor
    def _(t):
        for tc in range(4):
            s_g1e.wait(t, at=32)
            s_gsq.wait(t, at=tc + 1)
            if tc >= 1:
                s_gl.wait(t, at=5 * tc)
            for j in range(8):
                t.matmul(pb[4][0:1, 0:512], onesc[:, 0:1],
                         g1_sb[:, j, 512 * tc:512 * tc + 512],
                         start=(j == 0), stop=(j == 7))
            for j in range(8):
                ins = t.matmul(pb[5][0:1, 0:512], onesc[:, 0:1],
                               sqg_sb[:, j, :], start=(j == 0), stop=(j == 7))
            s_gl.inc(ins)
            s_gl.wait(t, at=5 * tc + 3)
            t.matmul(pb[4][:, 0:512], ones1f[0:1, :], grows[0:1, 0:512],
                     start=True, stop=True)
            ins = t.matmul(pb[5][:, 0:512], ones1f[0:1, :],
                           grows[0:1, 512:1024], start=True, stop=True)
            s_gl.inc(ins)

    @blk.tensor
    def _(t):
        s_gl.wait(t, at=20)
        for oc in range(4):
            for tc in range(4):
                n = 4 * oc + tc
                bank = pb[n % 4]
                if n >= 4:
                    s_g2e.wait(t, at=n - 3)
                for k in range(8):
                    ins = t.matmul(bank[:],
                                   gw2_sb[:, k, 128 * oc:128 * oc + 128],
                                   g1_sb[:, k, 512 * tc:512 * tc + 512],
                                   start=(k == 0), stop=(k == 7))
                s_g2.inc(ins)

    @blk.scalar
    def _(s):
        for oc in range(4):
            for tc in range(4):
                n = 4 * oc + tc
                s_g2.wait(s, at=n + 1)
                s_g2e.inc(s.activation(
                    g2_sb[:, oc, 512 * tc:512 * tc + 512], pb[n % 4][:],
                    AF.Silu,
                    bias=biases[:, BOFF["gb2"] + oc:BOFF["gb2"] + oc + 1]))

    @blk.tensor
    def _(t):
        s_g2e.wait(t, at=16)
        s_ns.wait(t, at=6)
        for tc in range(4):
            bank = pb[tc % 2]
            if tc >= 2:
                s_lgr.wait(t, at=tc - 1)
            for k in range(4):
                ins = t.matmul(bank[0:8, 0:512], gw3_sb[:, k, :],
                               g2_sb[:, k, 512 * tc:512 * tc + 512],
                               start=(k == 0), stop=(k == 3))
            s_g3.inc(ins)
            s_lgr.wait(t, at=tc + 1)
            for q in range(4):
                ti = 4 * tc + q
                if ti >= 2:
                    s_ltm.wait(t, at=ti - 1)
                s_ltt.inc(t.transpose(
                    pb[2 + ti % 2][0:128, 0:8],
                    lrow[0:8, 128 * q:128 * q + 128], identf[0:8, 0:8]))

    @blk.vector
    def _(v):
        for tc in range(4):
            s_g3.wait(v, at=tc + 1)
            s_lgr.inc(v.tensor_scalar(lrow[0:8, :], pb[tc % 2][0:8, 0:512],
                                      e8c[0:8, 3:4], e8c[0:8, 2:3],
                                      OP.mult, OP.add))
            for q in range(4):
                ti = 4 * tc + q
                s_ltt.wait(v, at=ti + 1)
                s_ltm.inc(v.scalar_tensor_tensor(
                    logits_tm[:, ti, :], noise_sb[:, ti, :], n128[:, 0:1],
                    pb[2 + ti % 2][0:128, 0:8], OP.mult, OP.add))

    @blk.vector
    def _(v):
        s_ltm.wait(v, at=NOT)
        L3 = logits_tm
        v.drain()
        v.tensor_reduce(tkm_a[:], L3[:, :, :], AX.X, OP.max)
        v.drain()
        for e in range(8):
            v.tensor_tensor(tk8a[:, :, e:e + 1], L3[:, :, e:e + 1],
                            tkm_a[:], OP.is_equal)
        v.drain()
        for e in range(8):
            v.tensor_scalar(tk8b[:, :, e:e + 1], tk8a[:, :, e:e + 1],
                            float(100 - e), None, OP.mult)
        v.drain()
        for e in range(8):
            v.tensor_scalar(tk8b[:, :, e:e + 1], tk8b[:, :, e:e + 1],
                            -1.0, 100.0, OP.mult, OP.add)
        v.drain()
        v.tensor_reduce(tkm_b[:], tk8b[:, :, :], AX.X, OP.min)
        v.drain()
        for e in range(8):
            v.tensor_scalar(tk8a[:, :, e:e + 1], tkm_b[:], float(e), None,
                            OP.is_equal)
        v.drain()
        for e in range(8):
            v.scalar_tensor_tensor(tk8b[:, :, e:e + 1], tk8a[:, :, e:e + 1],
                                   -3e38, L3[:, :, e:e + 1],
                                   OP.mult, OP.add)
        v.drain()
        v.tensor_reduce(tkm_c[:], tk8b[:, :, :], AX.X, OP.max)
        v.drain()
        for e in range(8):
            v.tensor_tensor(tk8b[:, :, e:e + 1], tk8b[:, :, e:e + 1],
                            tkm_c[:], OP.is_equal)
        v.drain()
        for e in range(8):
            v.tensor_scalar(tk8b[:, :, e:e + 1], tk8b[:, :, e:e + 1],
                            float(100 - e), None, OP.mult)
        v.drain()
        for e in range(8):
            v.tensor_scalar(tk8b[:, :, e:e + 1], tk8b[:, :, e:e + 1],
                            -1.0, 100.0, OP.mult, OP.add)
        v.drain()
        v.tensor_reduce(tkm_c[:], tk8b[:, :, :], AX.X, OP.min)
        v.drain()
        for e in range(8):
            v.tensor_scalar(tk8b[:, :, e:e + 1], tkm_c[:], float(e), None,
                            OP.is_equal)
        v.drain()
        for e in range(8):
            v.tensor_tensor(tk8c[:, :, e:e + 1], tk8a[:, :, e:e + 1],
                            tk8b[:, :, e:e + 1], OP.add)
        for e in range(8):
            v.tensor_tensor(tk8b[:, :, e:e + 1], tk8b[:, :, e:e + 1],
                            L3[:, :, e:e + 1], OP.mult)
        v.drain()
        v.tensor_reduce(tkm_c[:], tk8b[:, :, :], AX.X, OP.add)
        v.drain()
        for e in range(8):
            v.tensor_tensor(tk8a[:, :, e:e + 1], L3[:, :, e:e + 1],
                            tkm_c[:], OP.is_ge)
        v.drain()
        for e in range(8):
            v.tensor_tensor(tk8b[:, :, e:e + 1], tk8a[:, :, e:e + 1],
                            L3[:, :, e:e + 1], OP.mult)
        for e in range(8):
            v.tensor_scalar(tk8a[:, :, e:e + 1], tk8a[:, :, e:e + 1],
                            FP(1e9), FP(-1e9), OP.mult, OP.add)
        v.drain()
        for e in range(8):
            v.tensor_tensor(tk8b[:, :, e:e + 1], tk8b[:, :, e:e + 1],
                            tk8a[:, :, e:e + 1], OP.add)
        v.drain()
        v.tensor_reduce(tkm_a[:], tk8b[:, :, :], AX.X, OP.max)
        v.drain()
        for e in range(8):
            ins = v.tensor_tensor(tk8b[:, :, e:e + 1], tk8b[:, :, e:e + 1],
                                  tkm_a[:], OP.subtract)
        s_tk.inc(ins)
        s_tke.wait(v, at=1)
        v.tensor_reduce(tkm_b[:], tk8a[:, :, :], AX.X, OP.add)
        v.drain()
        v.reciprocal(tkm_b[:], tkm_b[:])
        v.drain()
        for e in range(8):
            v.tensor_tensor(tk8a[:, :, e:e + 1], tk8a[:, :, e:e + 1],
                            tkm_b[:], OP.mult)
        v.drain()
        for e in range(8):
            ins = v.tensor_tensor(w_tm[:, :, e:e + 1], tk8a[:, :, e:e + 1],
                                  tk8c[:, :, e:e + 1], OP.mult)
        s_tk.inc(ins)

    @blk.scalar
    def _(s):
        s_tk.wait(s, at=1)
        s_tke.inc(s.activation(tk8a[:, :, :], tk8b[:, :, :], AF.Exp))

    if "gate" in debug:
        @blk.sync
        def _(sy):
            s_ltm.wait(sy)
            s_dbg.inc(sy.dma_start(out=dbg["d_logits"].ap(),
                                   in_=logits_tm[:]), 16)
            s_tk.wait(sy)
            s_dbg.inc(sy.dma_start(out=dbg["d_w"].ap(), in_=w_tm[:]), 16)
            s_dbg.inc(sy.dma_start(out=dbg["d_sel2"].ap(), in_=tk8c[:]), 16)
            s_dbg.inc(sy.dma_start(out=dbg["d_i1"].ap(), in_=tkm_b[:]), 16)
            s_dbg.inc(sy.dma_start(out=dbg["d_i2"].ap(), in_=tkm_c[:]), 16)
            s_xn.wait(sy)
            s_dbg.inc(sy.dma_start(out=dbg["d_xnorm"].ap(), in_=u_fm[:]), 16)

    esD.close()
    NSLOT = 256
    esE0 = ExitStack()

    def sbE0(name, shape, dt):
        return esE0.enter_context(nc.sbuf_tensor(name, list(shape), dt))

    grankm = sbE0("grankm", (128, 16), F32)
    ecst = sbE0("ecst", (128, 8, 8), F32)
    b2mx = sbE0("b2mx", (128, 8, 8), F32)
    econT = sbE0("econT", (8, 1024), F32)
    b2T = sbE0("b2T", (8, 1024), F32)
    wsl_tm = sbE0("wsl_tm", (128, 2, 8), F32)

    esEa = ExitStack()

    def sbEa(name, shape, dt):
        return esEa.enter_context(nc.sbuf_tensor(name, list(shape), dt))

    x_tm = sbEa("x_tm", (128, 16, 1024), BF16)
    m_bf = sbEa("m_bf", (128, 16), BF16)
    m_f = sbEa("m_f", (128, 16), F32)
    cnt8p = sbEa("cnt8p", (128, 8), F32)
    cnt8b = sbEa("cnt8b", (128, 8), BF16)
    cntc_f = sbEa("cntc_f", (8, 1), F32)
    cntrow = sbEa("cntrow", (1, 8), F32)
    totc = sbEa("totc", (16, 1), BF16)
    prefc = sbEa("prefc", (16, 1), F32)
    tioff = sbEa("tioff", (1, 16), F32)
    pre_sb = sbEa("pre_sb", (128, 16), F32)
    coff = sbEa("coff", (128, 1), F32)
    gsh8 = sbEa("gsh8", (128, 16), F32)
    gscr8 = sbEa("gscr8", (128, 16), F32)
    pbuf_b = [sbEa(f"pbuf_b{j}", (128, NSLOT), BF16) for j in range(2)]
    pbuf_f = [sbEa(f"pbuf_f{j}", (128, NSLOT), F32) for j in range(2)]
    partial = sbEa("partial", (128, 16, 1024), BF16)
    wpart = sbEa("wpart", (8, 8, NSLOT), F32)

    s_sp = SemAlias(s_bhi)
    s_xtt = SemAlias(s_sst)
    s_xte = SemAlias(s_sse)
    s_pb = SemAlias(s_bv[0])
    s_pc = SemAlias(s_bv[1])
    s_gev = SemAlias(s_bt[1])
    s_rsx = SemAlias(s_cload)
    s_rsd = sem("s_rsd")
    s_cc2 = sem("s_cc2")
    s_gxt = SemAlias(s_g3)
    s_ew1 = s_wq + [sem("s_ew1c"), sem("s_ew1d")]
    s_ew2 = s_wk + [sem("s_ew2c")]
    EW1B = [sm.n for sm in s_ew1]
    EW2B = [sm.n for sm in s_ew2]
    s_hx = sem("s_hx")
    s_he = SemAlias(s_lgr)
    s_h2v = SemAlias(s_ltt)
    s_whk = sem("s_whk")
    s_oc = SemAlias(s_tke)
    s_ecn = SemAlias(s_sq2)
    s_agd = sem("s_agd")
    s_rw = SemAlias(s_yv)
    s_scm = SemAlias(s_utp)
    s_y0m = SemAlias(s_ue)
    s_y0e = SemAlias(s_ave)
    s_qrw = SemAlias(s_op)
    s_acv = SemAlias(s_ev1)
    s_ftr = SemAlias(s_mskc)
    s_fot = SemAlias(s_impw)
    s_yr2 = s_ysp
    YR2B = [sm.n for sm in s_yr2]
    s_out2 = [sem(f"s_out2_{j}") for j in range(2)]

    ESP = {}
    for k in range(8):
        fam, ei = FAM[k]
        if fam == "e0":
            ESP[k] = dict(H=4096, w1=("c_e0w1", ei), b1=BOFF["e0b1"] + 32 * ei,
                          act=AF.Gelu_apprx_tanh, w2=("c_e0w2", ei),
                          b2=BOFF["e0b2"] + 8 * ei, mid=None, h2base=0,
                          nhk=32, e2=False)
        elif fam == "e1":
            ESP[k] = dict(H=2048, w1=("c_e1w1", ei), b1=BOFF["e1b1"] + 16 * ei,
                          act=AF.Silu, mid=("c_e1w2", ei),
                          bmid=BOFF["e1b2"] + 16 * ei, w2=("c_e1w3", ei),
                          b2=BOFF["e1b3"] + 8 * ei, h2base=16, nhk=16,
                          e2=False)
        else:
            ESP[k] = dict(H=3072, w1=("c_e2w1", ei), b1=BOFF["e2b1"] + 24 * ei,
                          act=None, w2=("c_e2w2", ei),
                          b2=BOFF["e2b2"] + 8 * ei, mid=None, h2base=0,
                          nhk=24, e2=True)

    A_SETS = []
    for k in range(8):
        sp = ESP[k]
        for q in range(sp["H"] // 1024):
            A_SETS.append((k, "q", q))
        if sp["mid"] is not None:
            for mp in range(4):
                A_SETS.append((k, "m", mp))
    NA = len(A_SETS)
    ECN_BASE = {}
    for k in range(8):
        ECN_BASE[k] = 3 * k
    ECN_TOT = 24
    NB_END = {}
    _nbe = 0
    for k in range(8):
        _nbe += ESP[k]["nhk"]
        NB_END[k] = _nbe
    HX_AT = []
    _hxl = 0
    for k in range(8):
        sp = ESP[k]
        for q in range(sp["H"] // 1024):
            _hxl += 8
            HX_AT.append(_hxl)
        if sp["mid"] is not None:
            for hh in range(2):
                _hxl += 8
                HX_AT.append(_hxl)
                HX_AT.append(_hxl)

    @blk.vector
    def _(v):
        s_tk.wait(v, at=2)
        s_xn.wait(v, at=NOT)
        v.tensor_scalar(m_bf[:], xchg[:, 0:16], scal[:, SC_THR:SC_THR + 1],
                        None, OP.is_gt)
        v.tensor_scalar(m_f[:], xchg[:, 0:16], scal[:, SC_THR:SC_THR + 1],
                        None, OP.is_gt)
        for r in range(8):
            v.tensor_scalar(gscr8[:], gath[:, r, 0:16],
                            scal[:, SC_THR:SC_THR + 1], None, OP.is_gt,
                            OP.add, accum_out=cnt8p[:, r:r + 1])
            v.drain()
        s_sp.inc(v.tensor_copy(cnt8b[:], cnt8p[:]))
        s_sp.wait(v, at=8)
        v.tensor_tensor(gscr8[:, 0:8], pb[2][:, 0:8], rltm_sb[:], OP.mult)
        v.drain()
        v.tensor_reduce(coff[:], gscr8[:, 0:8], AX.X, OP.add)
        v.drain()
        v.tensor_tensor(grankm[:], pre_sb[:], pb[4][:, 0:16], OP.add)
        v.drain()
        v.tensor_scalar(grankm[:], grankm[:], coff[:], None, OP.add)
        v.drain()
        v.tensor_tensor(grankm[:], grankm[:], m_f[:], OP.mult)
        v.tensor_scalar(gscr8[:], m_f[:], 1.0, -1.0, OP.mult, OP.add)
        v.drain()
        ins = v.tensor_tensor(grankm[:], grankm[:], gscr8[:], OP.add)
        s_sp.inc(ins)

    @blk.tensor
    def _(t):
        s_tk.wait(t, at=2)
        s_gl.wait(t, at=20)
        s_sp.wait(t, at=1)
        t.matmul(pb[0][:, 0:16], ltri_b[:], m_bf[:], start=True, stop=True)
        t.matmul(pb[1][0:16, 0:1], m_bf[:], onesc[:, 0:1], start=True,
                 stop=True)
        ins = t.matmul(pb[1][0:8, 2:3], cnt8b[:], onesc[:, 0:1], start=True,
                       stop=True)
        s_sp.inc(ins)
        s_sp.wait(t, at=3)
        t.matmul(pb[3][0:16, 0:1], ltri_b[0:16, 0:16], totc[:, 0:1],
                 start=True, stop=True)
        ins = t.transpose(pb[3][0:1, 4:12], cntc_f[:, 0:1],
                          identf[0:8, 0:8])
        s_sp.inc(ins)
        s_sp.wait(t, at=5)
        t.transpose(pb[3][0:1, 16:32], prefc[:, 0:1], identf[0:16, 0:16])
        ins = t.matmul(pb[2][:, 0:8], ones1f[0:1, :], cntrow[0:1, :],
                       start=True, stop=True)
        s_sp.inc(ins)
        s_sp.wait(t, at=7)
        ins = t.matmul(pb[4][:, 0:16], ones1f[0:1, :], tioff[0:1, :],
                       start=True, stop=True)
        s_sp.inc(ins)

    @blk.scalar
    def _(s):
        s_sp.wait(s, at=2)
        s.copy(pre_sb[:], pb[0][:, 0:16])
        s.copy(totc[:, 0:1], pb[1][0:16, 0:1])
        ins = s.copy(cntc_f[:, 0:1], pb[1][0:8, 2:3])
        s_sp.inc(ins)
        s_sp.wait(s, at=4)
        s.copy(prefc[:, 0:1], pb[3][0:16, 0:1])
        ins = s.copy(cntrow[0:1, :], pb[3][0:1, 4:12])
        s_sp.inc(ins)
        s_sp.wait(s, at=6)
        ins = s.copy(tioff[0:1, :], pb[3][0:1, 16:32])
        s_sp.inc(ins)

    @blk.tensor
    def _(t):
        s_xn.wait(t, at=NOT)
        s_tk.wait(t, at=2)
        for ti in range(NOT):
            ptbank = pt0 if ti % 2 == 0 else pt1
            if ti >= 2:
                s_xte.wait(t, at=ti - 1)
            for k in range(8):
                ins = t.transpose(ptbank[:, 128 * k:128 * k + 128],
                                  u_fm[:, k, 128 * ti:128 * ti + 128],
                                  identb[:])
            s_xtt.inc(ins)

    @blk.scalar
    def _(s):
        for ti in range(NOT):
            ptbank = pt0 if ti % 2 == 0 else pt1
            s_xtt.wait(s, at=ti + 1)
            s_xte.inc(s.copy(x_tm[:, ti, :], ptbank[:, 0:1024]))

    @blk.vector
    def _(v):
        s_sp.wait(v, at=9)
        n = 0
        for g8 in range(8):
            v.drain()
            v.tensor_scalar(gsh8[:], grankm[:], float(-g8), None, OP.add)
            v.drain()
            for ti in range(NOT):
                j = n % 2
                if n >= 2:
                    s_pc.wait(v, at=n - 1)
                v.tensor_scalar(pbuf_b[j][:], r2560[:], gsh8[:, ti:ti + 1],
                                None, OP.is_equal)
                v.drain()
                ins = v.tensor_copy(pbuf_f[j][:], pbuf_b[j][:])
                s_pb.inc(ins)
                n += 1

    @blk.tensor
    def _(t):
        s_xte.wait(t, at=NOT)
        n = 0
        for g8 in range(8):
            if g8 >= 1:
                s_gev.wait(t, at=5 * g8)
            for ti in range(NOT):
                j = n % 2
                s_pb.wait(t, at=n + 1)
                t.matmul(pb[4][0:8, 0:256], w_tm[:, ti, :], pbuf_f[j][:],
                         start=(ti == 0), stop=(ti == NOT - 1))
                for stl in range(2):
                    for fb in range(2):
                        ins = t.matmul(
                            pb[2 * stl + fb][:],
                            pbuf_b[j][:, 128 * stl:128 * stl + 128],
                            x_tm[:, ti, 512 * fb:512 * fb + 512],
                            start=(ti == 0), stop=(ti == NOT - 1))
                s_pc.inc(ins)
                n += 1

    @blk.scalar
    def _(s):
        for g8 in range(8):
            s_pc.wait(s, at=16 * (g8 + 1))
            for stl in range(2):
                for fb in range(2):
                    s_gev.inc(s.copy(
                        partial[:, 2 * g8 + stl, 512 * fb:512 * fb + 512],
                        pb[2 * stl + fb][:]))
            s_gev.inc(s.copy(wpart[:, g8, :], pb[4][0:8, 0:256]))

    @blk.sync
    def _(sy):
        for g8 in range(8):
            s_gev.wait(sy, at=5 * (g8 + 1))
            s_rsx.inc(sy.dma_start(
                out=D["rsin"].ap()[g8],
                in_=partial[:, 2 * g8:2 * g8 + 2, :]), 16)
        s_rsx.inc(sy.dma_start(
            out=D["wrsin"].ap().rearrange("c e j -> e c j"),
            in_=wpart[:, :, :]), 16)

    esEa.close()
    esEb = ExitStack()

    def sbEb(name, shape, dt):
        return esEb.enter_context(nc.sbuf_tensor(name, list(shape), dt))

    gx_tm = sbEb("gx_tm", (128, 2, 1024), BF16)
    gxf = sbEb("gxf", (128, 8, NSLOT), BF16)
    wsl_raw = sbEb("wsl_raw", (8, NSLOT), F32)
    h_sb = sbEb("h_sb", (128, 32, NSLOT), BF16)
    bscr = [sbEb(f"bscr{j}", (128, NSLOT), BF16) for j in range(4)]
    wq1 = [sbEb(f"wq1_{j}", (128, 8, 1024), BF16) for j in range(4)]
    wq2 = [sbEb(f"wq2_{j}", (128, 1024), BF16) for j in range(3)]
    comb = sbEb("comb", (128, 2, 1024), F32)
    comb_b = sbEb("comb_b", (128, 2, 1024), BF16)
    erow = sbEb("erow", (1, 1024), F32)

    @blk.gpsimd
    def _(g):
        s_rsx.wait(g, at=16 * 9)
        g.collective_compute(
            "ReduceScatter", OP.add, replica_groups=[list(range(8))],
            ins=[D["rsin"].ap()], outs=[D["rsout"].ap()],
        ).then_inc(s_cc2.h, 1)
        s_cc2.n += 1
        g.collective_compute(
            "ReduceScatter", OP.add, replica_groups=[list(range(8))],
            ins=[D["wrsin"].ap()], outs=[D["wrsout"].ap()],
        ).then_inc(s_cc2.h, 1)
        s_cc2.n += 1
        g.wait_ge(s_cc2.h, 2)
        s_rsd.inc(g.dma_start(out=gx_tm[:, :, :], in_=D["rsout"].ap()), 16)
        s_rsd.inc(g.dma_start(out=wsl_raw[:, :], in_=D["wrsout"].ap()), 16)

    @blk.tensor
    def _(t):
        s_rsd.wait(t, at=32)
        for sl in range(2):
            ptbank = pt0 if sl == 0 else pt1
            for k in range(8):
                ins = t.transpose(ptbank[:, 128 * k:128 * k + 128],
                                  gx_tm[:, sl, 128 * k:128 * k + 128],
                                  identb[:])
            s_gxt.inc(ins)
        s_gxt.wait(t, at=4)
        t.transpose(pb[5][:, 0:8], wsl_raw[0:8, 0:128], identf[0:8, 0:8])
        ins = t.transpose(pb[5][:, 8:16], wsl_raw[0:8, 128:256],
                          identf[0:8, 0:8])
        s_gxt.inc(ins)

    @blk.scalar
    def _(s):
        for sl in range(2):
            s_gxt.wait(s, at=2 * sl + 1)
            ptbank = pt0 if sl == 0 else pt1
            for k in range(8):
                ins = s.copy(gxf[:, k, 128 * sl:128 * sl + 128],
                             ptbank[:, 128 * k:128 * k + 128])
            s_gxt.inc(ins)
        s_gxt.wait(s, at=5)
        s.copy(wsl_tm[:, 0, :], pb[5][:, 0:8])
        ins = s.copy(wsl_tm[:, 1, :], pb[5][:, 8:16])
        s_gxt.inc(ins)

    @blk.gpsimd
    def _(g):
        s_wcast.wait(g)
        s_ltm.wait(g, at=NOT)
        s_tk.wait(g, at=2)
        s_xn.wait(g, at=NOT)
        s_gl.wait(g, at=20)
        s_g2e.wait(g, at=16)
        s_g1e.wait(g, at=32)
        n = 0
        nb = 0
        for k in range(8):
            sp = ESP[k]
            nsets = sp["H"] // 1024 + (4 if sp["mid"] is not None else 0)
            for si in range(nsets):
                kind = "q" if si < sp["H"] // 1024 else "m"
                idx = si if kind == "q" else si - sp["H"] // 1024
                j = n % 4
                if n >= 4:
                    g.wait_ge(s_hx.h, HX_AT[n - 4])
                if kind == "q":
                    nm, ei = sp["w1"]
                    srcw = D[nm].ap()[ei][:, 1024 * idx:1024 * idx + 1024]
                else:
                    nm, ei = sp["mid"]
                    hh, prt = idx // 2, idx % 2
                    srcw = D[nm].ap()[ei][1024 * prt:1024 * prt + 1024,
                                          1024 * hh:1024 * hh + 1024]
                s_ew1[j].inc(g.dma_start(
                    out=wq1[j][:, 0:8, :],
                    in_=srcw.rearrange("(c p) f -> p c f", p=128)), 16)
                n += 1
            nm, ei = sp["w2"]
            for hk in range(sp["nhk"]):
                j = nb % 3
                if nb >= 3:
                    g.wait_ge(s_whk.h, nb - 2)
                srcw = D[nm].ap()[ei][128 * hk:128 * hk + 128, :]
                s_ew2[j].inc(g.dma_start(out=wq2[j][:, :], in_=srcw), 16)
                nb += 1

    @blk.tensor
    def _(t):
        s_gxt.wait(t, at=6)
        n = 0
        nb = 0
        hx = 0
        h2v = 0
        for k in range(8):
            sp = ESP[k]
            eb = ECN_BASE[k]
            nq = sp["H"] // 1024
            for q in range(nq):
                j = n % 4
                s_ew1[j].wait(t, at=EW1B[j] + 16 * (n // 4 + 1))
                if k >= 1 and q == 0:
                    s_oc.wait(t, at=4 * k)
                    s_ecn.wait(t, at=eb)
                for hc in range(8):
                    reg = pb[hc % 4][:, 256 * (hc // 4):256 * (hc // 4)
                                     + 256]
                    if hx >= 4:
                        s_he.wait(t, at=hx - 3)
                    for kk in range(8):
                        ins = t.matmul(
                            reg, wq1[j][:, kk, 128 * hc:128 * hc + 128],
                            gxf[:, kk, :], start=(kk == 0), stop=(kk == 7))
                    s_hx.inc(ins)
                    hx += 1
                    if sp["e2"]:
                        h2v += 1
                n += 1
            if sp["mid"] is not None:
                s_he.wait(t, at=hx)
                for hh in range(2):
                    n0 = n
                    for jp in range(2):
                        jj = (n0 + jp) % 4
                        s_ew1[jj].wait(
                            t, at=EW1B[jj] + 16 * ((n0 + jp) // 4 + 1))
                    for hc in range(8):
                        reg = pb[hc % 4][:, 256 * (hc // 4):
                                         256 * (hc // 4) + 256]
                        if hx >= 4:
                            s_he.wait(t, at=hx - 3)
                        for h1c in range(16):
                            jj = (n0 + h1c // 8) % 4
                            ins = t.matmul(
                                reg,
                                wq1[jj][:, h1c % 8, 128 * hc:128 * hc + 128],
                                h_sb[:, h1c, :], start=(h1c == 0),
                                stop=(h1c == 15))
                        s_hx.inc(ins)
                        hx += 1
                    n += 2
            s_he.wait(t, at=hx)
            if sp["e2"]:
                s_h2v.wait(t, at=h2v)
            for hk in range(sp["nhk"]):
                j = nb % 3
                s_ew2[j].wait(t, at=EW2B[j] + 16 * (nb // 3 + 1))
                if k >= 1 and hk == 0:
                    s_oc.wait(t, at=4 * k)
                for stl in range(2):
                    for fb in range(2):
                        ins = t.matmul(
                            pb[2 * stl + fb][:],
                            h_sb[:, sp["h2base"] + hk,
                                 128 * stl:128 * stl + 128],
                            wq2[j][:, 512 * fb:512 * fb + 512],
                            start=(hk == 0), stop=(hk == sp["nhk"] - 1))
                s_whk.inc(ins)
                nb += 1
            s_ecn.wait(t, at=eb + 1)
            for fc in range(8):
                ins = t.transpose(pb[4][:, fc:fc + 1],
                                  erow[0:1, 128 * fc:128 * fc + 128],
                                  identf[0:1, 0:1])
            s_ecn.inc(ins)

    @blk.scalar
    def _(s):
        s_gxt.wait(s, at=6)
        hx = 0
        h2v = 0
        for k in range(8):
            sp = ESP[k]
            eb = ECN_BASE[k]
            nq = sp["H"] // 1024
            for q in range(nq):
                for hc in range(8):
                    reg = pb[hc % 4][:, 256 * (hc // 4):256 * (hc // 4)
                                     + 256]
                    s_hx.wait(s, at=hx + 1)
                    hr = 8 * q + hc
                    bias = biases[:, sp["b1"] + hr:sp["b1"] + hr + 1]
                    if sp["e2"]:
                        if h2v >= 2:
                            s_h2v.wait(s, at=h2v - 1)
                        s.activation(bscr[2 * (h2v % 2)][:], reg, AF.Silu,
                                     bias=bias)
                        s_he.inc(s.activation(bscr[2 * (h2v % 2) + 1][:],
                                              reg, AF.Sigmoid, bias=bias))
                        h2v += 1
                    else:
                        s_he.inc(s.activation(h_sb[:, hr, :], reg,
                                              sp["act"], bias=bias))
                    hx += 1
            if sp["mid"] is not None:
                for hh in range(2):
                    for hc in range(8):
                        reg = pb[hc % 4][:, 256 * (hc // 4):
                                         256 * (hc // 4) + 256]
                        s_hx.wait(s, at=hx + 1)
                        hr = 8 * hh + hc
                        bias = biases[:, sp["bmid"] + hr:sp["bmid"] + hr + 1]
                        s_he.inc(s.activation(h_sb[:, 16 + hr, :], reg,
                                              AF.Silu, bias=bias))
                        hx += 1
            s_whk.wait(s, at=NB_END[k])
            s.copy(erow[0:1, 0:512], pb[2][64:65, 0:512])
            ins = s.copy(erow[0:1, 512:1024], pb[3][64:65, 0:512])
            s_ecn.inc(ins)
            s_ecn.wait(s, at=eb + 2)
            for fc in range(8):
                s.copy(ecst[:, fc, k:k + 1], pb[4][:, fc:fc + 1])
            for fc in range(8):
                ins = s.copy(b2mx[:, fc, k:k + 1],
                             biases[:, sp["b2"] + fc:sp["b2"] + fc + 1])
            s_ecn.inc(ins)

    @blk.vector
    def _(v):
        s_gxt.wait(v, at=6)
        v.memset(comb[:], 0.0)
        v.drain()
        hx = 0
        h2v = 0
        ox = 0
        for k in range(8):
            sp = ESP[k]
            nq = sp["H"] // 1024
            for q in range(nq):
                for hc in range(8):
                    if sp["e2"]:
                        s_he.wait(v, at=hx + 1)
                        hr = 8 * q + hc
                        s_h2v.inc(v.tensor_tensor(
                            h_sb[:, hr, :], bscr[2 * (h2v % 2)][:],
                            bscr[2 * (h2v % 2) + 1][:], OP.mult))
                        h2v += 1
                    hx += 1
            if sp["mid"] is not None:
                hx += 16
            s_whk.wait(v, at=NB_END[k])
            for stl in range(2):
                for fb in range(2):
                    ins = v.scalar_tensor_tensor(
                        comb[:, stl, 512 * fb:512 * fb + 512],
                        pb[2 * stl + fb][:], wsl_tm[:, stl, k:k + 1],
                        comb[:, stl, 512 * fb:512 * fb + 512],
                        OP.mult, OP.add)
                    s_oc.inc(ins)
                    ox += 1
        v.drain()
        ins = v.tensor_copy(comb_b[:], comb[:])
        s_oc.inc(ins)

    @blk.sync
    def _(sy):
        s_oc.wait(sy, at=33)
        s_rsx.inc(sy.dma_start(
            out=D["agin2"].ap().rearrange("s p f -> p s f"),
            in_=comb_b[:, :, :]), 16)

    esEb.close()
    esEc = ExitStack()

    def sbEc(name, shape, dt):
        return esEc.enter_context(nc.sbuf_tensor(name, list(shape), dt))

    agx = sbEc("agx", (128, 16, 1024), BF16)
    grow = sbEc("grow", (1, 2048), F32)
    grbcq = sbEc("grbcq", (128, 512), F32)
    w_em = sbEc("w_em", (8, 2048), F32)
    wns_scr = sbEc("wns_scr", (8, 512), F32)
    nmem_scr = sbEc("nmem_scr", (8, 512), F32)
    pth = sbEc("pth", (128, 16, 512), BF16)
    yscrE = sbEc("yscrE", (128, 512), F32)
    acc = sbEc("acc", (128, 8, 512), F32)
    yrld2 = [sbEc(f"yrld2_{j}", (128, EMB), F32) for j in range(2)]
    out_sb = [sbEc(f"out_sb{j}", (128, EMB), F32) for j in range(2)]

    @blk.gpsimd
    def _(g):
        s_rsx.wait(g, at=16 * 10)
        g.collective_compute(
            "AllGather", OP.bypass, replica_groups=[list(range(8))],
            ins=[D["agin2"].ap()], outs=[D["agout2"].ap()],
        ).then_inc(s_cc2.h, 1)
        s_cc2.n += 1
        g.wait_ge(s_cc2.h, 3)
        s_agd.inc(g.dma_start(
            out=agx[:, :, :],
            in_=D["agout2"].ap().rearrange("c s p f -> p (c s) f")), 16)

    @blk.tensor
    def _(t):
        s_oc.wait(t, at=33)
        for ti in range(NOT):
            if ti >= 1:
                s_rw.wait(t, at=2 * ti)
            t.transpose(pb[0][0:1, 0:128], grankm[:, ti:ti + 1], identf[:])
            ins = t.transpose(pb[0][0:8, 128:256], w_tm[:, ti, :],
                              identf[:])
            s_rw.inc(ins)
        s_ecn.wait(t, at=ECN_TOT)
        for fc in range(8):
            s_rw.wait(t, at=32 + 2 * fc)
            t.transpose(pb[2][0:8, 0:128], ecst[:, fc, :], identf[:])
            ins = t.transpose(pb[3][0:8, 0:128], b2mx[:, fc, :], identf[:])
            s_rw.inc(ins)

    @blk.scalar
    def _(s):
        for ti in range(NOT):
            s_rw.wait(s, at=2 * ti + 1)
            s.copy(grow[0:1, 128 * ti:128 * ti + 128], pb[0][0:1, 0:128])
            ins = s.copy(w_em[0:8, 128 * ti:128 * ti + 128],
                         pb[0][0:8, 128:256])
            s_rw.inc(ins)
        for fc in range(8):
            s_rw.wait(s, at=33 + 2 * fc)
            s.copy(econT[0:8, 128 * fc:128 * fc + 128], pb[2][0:8, 0:128])
            ins = s.copy(b2T[0:8, 128 * fc:128 * fc + 128],
                         pb[3][0:8, 0:128])
            s_rw.inc(ins)


    @blk.vector
    def _(v):
        for qr in range(4):
            s_qrw.wait(v, at=3 * qr + 2)
            if qr >= 1:
                s_scm.wait(v, at=8 * qr)
            for st in range(16):
                v.tensor_scalar(pth[:, st, :], grbcq[:, :],
                                gc16_sb[:, st:st + 1], None, OP.is_equal)
            v.tensor_scalar(nmem_scr[0:8, :], grbcq[0:8, :], 0.0, None,
                            OP.is_lt)
            v.drain()
            ins = v.tensor_tensor(wns_scr[0:8, :],
                                  w_em[0:8, 512 * qr:512 * qr + 512],
                                  nmem_scr[0:8, :], OP.mult)
            s_qrw.inc(ins)
            for fc in range(8):
                u = 8 * qr + fc
                s_scm.wait(v, at=u + 1)
                s_y0e.wait(v, at=u + 1)
                v.tensor_tensor(acc[:, fc, :], pb[fc % 2][:], yscrE[:],
                                OP.add)
                v.drain()
                ins = v.scalar_tensor_tensor(
                    acc[:, fc, :], u_fm[:, fc, 512 * qr:512 * qr + 512],
                    0.1, acc[:, fc, :], OP.mult, OP.add)
                s_acv.inc(ins)
            for tj in range(4):
                ti = 4 * qr + tj
                s_ftr.wait(v, at=ti + 1)
                s_yr2[ti % 2].wait(v, at=YR2B[ti % 2] + 16 * (ti // 2 + 1))
                if ti >= 2:
                    s_out2[ti % 2].wait(v, at=16 * ((ti - 2) // 2 + 1))
                v.tensor_tensor(out_sb[ti % 2][:, 0:512],
                                yrld2[ti % 2][:, 0:512], pb[4][:], OP.add)
                ins = v.tensor_tensor(out_sb[ti % 2][:, 512:1024],
                                      yrld2[ti % 2][:, 512:1024], pb[5][:],
                                      OP.add)
                s_fot.inc(ins)

    @blk.tensor
    def _(t):
        s_agd.wait(t, at=16)
        s_rw.wait(t, at=48)
        for qr in range(4):
            if qr >= 1:
                s_acv.wait(t, at=8 * qr)
            ins = t.matmul(pb[1][:, 0:512], ones1f[0:1, :],
                           grow[0:1, 512 * qr:512 * qr + 512], start=True,
                           stop=True)
            s_qrw.inc(ins)
            s_qrw.wait(t, at=3 * qr + 3)
            for fc in range(8):
                u = 8 * qr + fc
                if u >= 2:
                    s_acv.wait(t, at=u - 1)
                ba = pb[fc % 2]
                for st in range(16):
                    ins = t.matmul(
                        ba[:], agx[:, st, 128 * fc:128 * fc + 128],
                        pth[:, st, :], start=(st == 0), stop=(st == 15))
                s_scm.inc(ins)
                by = pb[2 + fc % 2]
                if u >= 2:
                    s_y0e.wait(t, at=u - 1)
                t.matmul(by[:], econT[0:8, 128 * fc:128 * fc + 128],
                         wns_scr[0:8, :], start=True, stop=False)
                ins = t.matmul(by[:], b2T[0:8, 128 * fc:128 * fc + 128],
                               w_em[0:8, 512 * qr:512 * qr + 512],
                               start=False, stop=True)
                s_y0m.inc(ins)
            s_acv.wait(t, at=8 * (qr + 1))
            for tj in range(4):
                ti = 4 * qr + tj
                if ti >= 1:
                    s_fot.wait(t, at=ti)
                for jc in range(8):
                    bank = pb[4 + jc // 4]
                    ins = t.transpose(
                        bank[:, 128 * (jc % 4):128 * (jc % 4) + 128],
                        acc[:, jc, 128 * tj:128 * tj + 128], identf[:])
                s_ftr.inc(ins)

    @blk.scalar
    def _(s):
        for qr in range(4):
            s_qrw.wait(s, at=3 * qr + 1)
            ins = s.copy(grbcq[:, :], pb[1][:, 0:512])
            s_qrw.inc(ins)
            for fc in range(8):
                u = 8 * qr + fc
                s_y0m.wait(s, at=u + 1)
                if u >= 1:
                    s_acv.wait(s, at=u)
                ins = s.copy(yscrE[:], pb[2 + fc % 2][:])
                s_y0e.inc(ins)

    @blk.sync
    def _(sy):
        for j in range(2):
            s_ysp[j].wait(sy)
        for ti in range(NOT + 1):
            if ti < NOT:
                if ti >= 2:
                    s_fot.wait(sy, at=ti - 1)
                s_yr2[ti % 2].inc(sy.dma_start(
                    out=yrld2[ti % 2][:],
                    in_=D["yspill"].ap()[128 * ti:128 * (ti + 1), :]), 16)
            if ti >= 1:
                tj = ti - 1
                s_fot.wait(sy, at=tj + 1)
                s_out2[tj % 2].inc(sy.dma_start(
                    out=D["out"].ap()[128 * tj:128 * (tj + 1), :],
                    in_=out_sb[tj % 2][:]), 16)

    if "sp" in debug:
        @blk.sync
        def _(sy):
            s_sp.wait(sy)
            s_dbg.inc(sy.dma_start(out=dbg["d_grank"].ap(), in_=grankm[:]),
                      16)
            s_rsd.wait(sy)
            s_dbg.inc(sy.dma_start(out=dbg["d_gx"].ap(), in_=gx_tm[:]), 16)
            s_dbg.inc(sy.dma_start(out=dbg["d_wsl"].ap(), in_=wsl_raw[:]),
                      16)
            s_oc.wait(sy)
            s_dbg.inc(sy.dma_start(out=dbg["d_comb"].ap(), in_=comb[:]), 16)

    @blk.gpsimd
    def _(g):
        s_sp.wait(g)
        s_agd.wait(g)
        s_hx.wait(g)
        s_y0m.wait(g)
        s_xtt.wait(g)
        s_xte.wait(g)
        s_pb.wait(g)
        s_pc.wait(g)
        s_gev.wait(g)
        s_he.wait(g)
        s_h2v.wait(g)
        s_whk.wait(g)
        s_ecn.wait(g)
        s_rw.wait(g)
        s_qrw.wait(g)
        s_scm.wait(g)
        s_y0e.wait(g)
        s_acv.wait(g)
        s_fot.wait(g)
        for j in range(2):
            s_ysp[j].wait(g)
        s_dbg.wait(g)
        for j in range(2):
            s_out2[j].wait(g)

    esEc.close()
    esE0.close()

    esD0.close()
    es.close()
    return nc, D


def kernel(**inputs):
    noise = _noise_table()
    nc, D = build_nc()
    in_maps = [_core_inputs(c, inputs, noise) for c in range(8)]
    res = run_bass_kernel_spmd(nc, in_maps, core_ids=list(range(8)))
    out = np.zeros((4, S, EMB), np.float32)
    for c in range(8):
        b, h = c // 2, c % 2
        out[b, 2048 * h:2048 * (h + 1)] = np.asarray(res.results[c]["out"],
                                                     np.float32)
    return out


# version 12
# speedup vs baseline: 1.8522x; 1.0294x over previous
import numpy as np
import ml_dtypes
import concourse.bass as bass
import concourse.mybir as mybir
from concourse.alu_op_type import AluOpType
from concourse.bass_utils import run_bass_kernel_spmd
from contextlib import ExitStack

F32 = mybir.dt.float32
BF16 = mybir.dt.bfloat16
I32 = mybir.dt.int32
AF = mybir.ActivationFunctionType
AX = mybir.AxisListType
OP = AluOpType

EMB = 1024
HEADS = 16
HD = 64
S = 4096
TOK = 2048
XROWS = 2304
NXT = 18
NOT = 16
MASKRAW = -2.0 ** 33
LN_EPS = float(np.float32(1e-5))
FP = lambda x: float(np.float32(x))
BF = ml_dtypes.bfloat16

_QPOS = np.float32(np.float32(0.9) * np.float32(16383.0))
_LOI = int(np.floor(_QPOS))
_HW = np.float32(_QPOS - np.float32(_LOI))
_LW = np.float32(np.float32(1.0) - _HW)
K1 = float(_LOI + 1)
K2 = float(_LOI + 2)


class SemAlias:

    def __init__(self, base_sem):
        self.h = base_sem.h
        self._s = base_sem
        self.b = base_sem.n

    @property
    def n(self):
        return self._s.n - self.b

    @n.setter
    def n(self, v):
        self._s.n = self.b + v

    def inc(self, instr, k=1):
        instr.then_inc(self.h, k)
        self._s.n += k
        return instr

    def wait(self, eng, at=None):
        eng.wait_ge(self.h, self._s.n if at is None else self.b + at)


class Sem:

    def __init__(self, nc, es, name):
        self.h = es.enter_context(nc.semaphore(name))
        self.n = 0

    def inc(self, instr, k=1):
        instr.then_inc(self.h, k)
        self.n += k
        return instr

    def wait(self, eng, at=None):
        eng.wait_ge(self.h, self.n if at is None else at)


def _host_consts():
    c = {}
    c["identf"] = np.eye(128, dtype=np.float32)
    ii = np.arange(128)
    band = np.where(ii[None, :] >= ii[:, None], 0.0, MASKRAW).astype(np.float32)
    diag = np.where(ii[None, :] <= ii[:, None], 0.0, MASKRAW).astype(np.float32)
    c["mband"] = band
    c["mdiag"] = diag
    R = np.zeros((HD, HD), np.float32)
    for i in range(0, HD, 2):
        R[i, i + 1] = -1.0
        R[i + 1, i] = 1.0
    R2 = np.zeros((128, 128), np.float32)
    R2[0:64, 0:64] = R
    R2[64:128, 64:128] = R
    c["rperm"] = np.ascontiguousarray(R2.T)
    c["idx8"] = np.broadcast_to(np.arange(8, dtype=np.float32), (128, 8)).copy()
    c["negbig"] = np.full((128, 8), -3e38, np.float32)
    c["neg1e9"] = np.full((128, 8), -1e9, np.float32)
    inv_freq = 1.0 / (10000.0 ** (np.arange(0, HD, 2, dtype=np.float32) / HD))
    pos = np.arange(S, dtype=np.float32)
    fr = np.outer(pos, inv_freq)
    emb = np.concatenate([fr, fr], -1)
    c["cosS"] = np.cos(emb).astype(np.float32)
    c["sinS"] = np.sin(emb).astype(np.float32)
    return c


_CONSTS = _host_consts()


def _noise_table():
    import jax
    with jax.default_device(jax.devices("cpu")[0]):
        n = np.asarray(jax.random.normal(jax.random.key(1), (4, S, 8)),
                       np.float32)
    return n


def _core_inputs(c, inp, noise):
    b, h = c // 2, c % 2
    p0 = 2048 * h
    x = inp["L0_attn_x"][b]
    xa = np.zeros((XROWS, EMB), np.float32)
    if h == 1:
        xa[0:128] = x[p0 - 128:p0]
    xa[128:128 + TOK] = x[p0:p0 + TOK]
    xa[128 + TOK:128 + TOK + 4] = x[0:4]
    m = {"xa": xa}
    posn = np.zeros(XROWS, np.int64)
    posn[0:128] = np.arange(p0 - 128, p0) if h == 1 else 0
    posn[128:128 + TOK] = np.arange(p0, p0 + TOK)
    posn[128 + TOK:128 + TOK + 4] = np.arange(4)
    cos1 = _CONSTS["cosS"][posn]
    sin1 = _CONSTS["sinS"][posn]
    m["cosb"] = np.concatenate([cos1, cos1], 1).T.copy()
    m["sinb"] = np.concatenate([sin1, sin1], 1).T.copy()
    m["mA0"] = (np.full((128, 128), MASKRAW, np.float32) if h == 0
                else _CONSTS["mband"])
    m["gmask0"] = np.full((128, 4), MASKRAW if h == 0 else 0.0, np.float32)
    m["qkvwT"] = np.ascontiguousarray(inp["L0_attn_qkv_w"].T)
    m["outwT"] = np.ascontiguousarray(inp["L0_attn_out_w"].T)
    m["noise"] = np.ascontiguousarray(
        noise[b, p0:p0 + TOK, :].reshape(16, 128, 8).transpose(1, 0, 2))
    for k in ("identf", "mband", "mdiag", "rperm", "idx8", "negbig",
              "neg1e9"):
        m[k] = _CONSTS[k]
    m["gw1"] = inp["L0_ffn_gate_w1"]
    m["gw2"] = inp["L0_ffn_gate_w2"]
    m["gw3"] = inp["L0_ffn_gate_w3"]
    m["affw"] = inp["L0_ffn_aff_w"]
    m["gb1"] = inp["L0_ffn_gate_b1"].reshape(8, 128).T.copy()
    m["gb2"] = inp["L0_ffn_gate_b2"].reshape(4, 128).T.copy()
    m["lnsc"] = inp["L0_ffn_ln_scale"].reshape(8, 128).T.copy()
    m["lnof"] = inp["L0_ffn_ln_offset"].reshape(8, 128).T.copy()
    m["gb3"] = inp["L0_ffn_gate_b3"].reshape(8, 1).copy()
    m["affb"] = inp["L0_ffn_aff_b"].reshape(8, 1).copy()
    m["usage"] = inp["L0_ffn_expert_usage"].reshape(1, 8).copy()
    sel = np.zeros((1, 8), np.float32)
    sel[0, 2 * b] = 1.0
    sel[0, 2 * b + 1] = 1.0
    m["rankmask"] = sel
    m["e0w1"] = inp["L0_ffn_e0_w1"]
    m["e0w2"] = inp["L0_ffn_e0_w2"]
    m["e1w1"] = inp["L0_ffn_e1_w1"]
    m["e1w2"] = inp["L0_ffn_e1_w2"]
    m["e1w3"] = inp["L0_ffn_e1_w3"]
    m["e2w1"] = inp["L0_ffn_e2_w1"]
    m["e2w2"] = inp["L0_ffn_e2_w2"]
    m["e0b1"] = inp["L0_ffn_e0_b1"].reshape(3, 32, 128).transpose(0, 2, 1).copy()
    m["e0b2"] = inp["L0_ffn_e0_b2"].reshape(3, 8, 128).transpose(0, 2, 1).copy()
    m["e1b1"] = inp["L0_ffn_e1_b1"].reshape(3, 16, 128).transpose(0, 2, 1).copy()
    m["e1b2"] = inp["L0_ffn_e1_b2"].reshape(3, 16, 128).transpose(0, 2, 1).copy()
    m["e1b3"] = inp["L0_ffn_e1_b3"].reshape(3, 8, 128).transpose(0, 2, 1).copy()
    m["e2b1"] = inp["L0_ffn_e2_b1"].reshape(2, 24, 128).transpose(0, 2, 1).copy()
    m["e2b2"] = inp["L0_ffn_e2_b2"].reshape(2, 8, 128).transpose(0, 2, 1).copy()
    q = np.arange(128)
    m["ltri"] = (q[:, None] < q[None, :]).astype(np.float32)
    m["r256"] = np.broadcast_to(8.0 * np.arange(256, dtype=np.float32),
                                (128, 256)).copy()
    st = np.arange(16)
    m["gc16"] = (8.0 * (128.0 * (st[None, :] % 2) + q[:, None])
                 + st[None, :] // 2).astype(np.float32)
    m["rltm"] = np.broadcast_to(
        (np.arange(8)[None, :] < c).astype(np.float32), (128, 8)).copy()
    return m


FAM = {0: ("e0", 0), 1: ("e1", 0), 2: ("e2", 0), 3: ("e0", 1),
       4: ("e1", 1), 5: ("e2", 1), 6: ("e0", 2), 7: ("e1", 2)}


def build_nc(debug=()):
    nc = bass.Bass("TRN2", target_bir_lowering=False)
    D = {}

    def din(name, shape, dt=F32):
        D[name] = nc.dram_tensor(name, list(shape), dt, kind="ExternalInput")
        return D[name]

    def dout(name, shape, dt=F32):
        D[name] = nc.dram_tensor(name, list(shape), dt, kind="ExternalOutput")
        return D[name]

    din("xa", (XROWS, EMB))
    din("cosb", (128, XROWS)); din("sinb", (128, XROWS))
    din("mA0", (128, 128)); din("gmask0", (128, 4))
    din("mband", (128, 128)); din("mdiag", (128, 128))
    din("identf", (128, 128)); din("rperm", (128, 128))
    din("idx8", (128, 8)); din("negbig", (128, 8)); din("neg1e9", (128, 8))
    din("qkvwT", (EMB, 3 * EMB)); din("outwT", (EMB, EMB))
    din("noise", (128, 16, 8))
    din("gw1", (1024, 1024)); din("gw2", (1024, 512)); din("gw3", (512, 8))
    din("affw", (1024, 8))
    din("gb1", (128, 8)); din("gb2", (128, 4)); din("lnsc", (128, 8))
    din("lnof", (128, 8)); din("gb3", (8, 1)); din("affb", (8, 1))
    din("usage", (1, 8)); din("rankmask", (1, 8))
    din("e0w1", (3, 1024, 4096)); din("e0w2", (3, 4096, 1024))
    din("e1w1", (3, 1024, 2048)); din("e1w2", (3, 2048, 2048))
    din("e1w3", (3, 2048, 1024))
    din("e2w1", (2, 1024, 3072)); din("e2w2", (2, 3072, 1024))
    din("e0b1", (3, 128, 32)); din("e0b2", (3, 128, 8))
    din("e1b1", (3, 128, 16)); din("e1b2", (3, 128, 16))
    din("e1b3", (3, 128, 8))
    din("e2b1", (2, 128, 24)); din("e2b2", (2, 128, 8))
    din("ltri", (128, 128)); din("r256", (128, 256))
    din("gc16", (128, 16)); din("rltm", (128, 8))
    for nm, sh in (("c_e0w1", (3, 1024, 4096)), ("c_e0w2", (3, 4096, 1024)),
                   ("c_e1w1", (3, 1024, 2048)), ("c_e1w2", (3, 2048, 2048)),
                   ("c_e1w3", (3, 2048, 1024)), ("c_e2w1", (2, 1024, 3072)),
                   ("c_e2w2", (2, 3072, 1024))):
        D[nm] = nc.dram_tensor(nm, list(sh), BF16, kind="Internal")
    D["rsin"] = nc.dram_tensor("rsin", [8, 128, 2, 1024], BF16,
                               kind="Internal")
    D["rsout"] = nc.dram_tensor("rsout", [128, 2, 1024], BF16,
                                kind="Internal")
    D["wrsin"] = nc.dram_tensor("wrsin", [8, 8, 256], F32, kind="Internal")
    D["wrsout"] = nc.dram_tensor("wrsout", [8, 256], F32, kind="Internal")
    D["agin2"] = nc.dram_tensor("agin2", [2, 128, 1024], BF16,
                                kind="Internal")
    D["agout2"] = nc.dram_tensor("agout2", [8, 2, 128, 1024], BF16,
                                 kind="Internal")

    dout("out", (TOK, EMB))

    D["yspill"] = nc.dram_tensor("yspill", [TOK, EMB], F32)
    D["agin"] = nc.dram_tensor("agin", [128, 144], F32)
    D["agout"] = nc.dram_tensor("agout", [1024, 144], F32)

    dbg = {}
    if "ln1" in debug:
        dbg["d_xlnfm"] = dout("d_xlnfm", (128, 8, XROWS), BF16)
    if "attn" in debug:
        dbg["d_ofm"] = dout("d_ofm", (128, 8, TOK), BF16)
    if "attn2" in debug:
        dbg["d_qrot"] = dout("d_qrot", (128, TOK), BF16)
        dbg["d_krot"] = dout("d_krot", (128, XROWS), BF16)
        dbg["d_qa"] = dout("d_qa", (128, TOK), BF16)
        dbg["d_vtm"] = dout("d_vtm", (2, 128, 18, 128), BF16)
    if "y" in debug:
        dbg["d_y"] = dout("d_y", (TOK, EMB))
        dbg["d_u"] = dout("d_u", (128, 8, TOK), BF16)
        dbg["d_xchg"] = dout("d_xchg", (128, 144))
    if "thr" in debug:
        dbg["d_thr"] = dout("d_thr", (128, 16))
    if "sp" in debug:
        dbg["d_grank"] = dout("d_grank", (128, 16))
        dbg["d_gx"] = dout("d_gx", (128, 2, 1024), BF16)
        dbg["d_wsl"] = dout("d_wsl", (8, 256))
        dbg["d_comb"] = dout("d_comb", (128, 2, 1024))
    if "gate" in debug:
        dbg["d_logits"] = dout("d_logits", (128, 16, 8))
        dbg["d_w"] = dout("d_w", (128, 16, 8))
        dbg["d_xnorm"] = dout("d_xnorm", (128, 8, TOK), BF16)
        dbg["d_sel2"] = dout("d_sel2", (128, 16, 8))
        dbg["d_i1"] = dout("d_i1", (128, 16))
        dbg["d_i2"] = dout("d_i2", (128, 16))

    es = ExitStack()

    def sb(name, shape, dt):
        return es.enter_context(nc.sbuf_tensor(name, list(shape), dt))

    def ps(name, shape, dt=F32):
        return es.enter_context(nc.psum_tensor(name, list(shape), dt))

    def sem(name):
        return Sem(nc, es, name)

    identf = sb("identf_s", (128, 128), F32)
    identb = sb("identb", (128, 128), BF16)
    ones1 = sb("ones1", (1, 128), BF16)
    onesc = sb("onesc", (128, 1), BF16)
    masks = sb("masks", (128, 4 * 128 + 4), BF16)
    idx8 = sb("idx8_s", (128, 8), F32)
    ltri_b = sb("ltri_b", (128, 128), BF16)
    r2560 = sb("r2560", (128, 256), F32)
    gc16_sb = sb("gc16_sb", (128, 16), F32)
    rltm_sb = sb("rltm_sb", (128, 8), F32)
    negbig = sb("negbig_s", (128, 8), F32)
    neg1e9 = sb("neg1e9_s", (128, 8), F32)
    xchg = sb("xchg", (128, 144), F32)
    gath = sb("gath", (128, 8, 144), F32)
    u_fm = sb("u_fm", (128, 8, TOK), BF16)
    scal = sb("scal", (128, 64), F32)
    SC_THR = 0
    SC_RTEMP = 1
    SC_NSC = 2
    SC_B3P = 3
    SC_LO = 4; SC_HI = 5; SC_MID = 6; SC_PRED = 7; SC_CNT = 8
    SC_T1 = 9; SC_T2 = 10; SC_S1 = 11; SC_S2 = 12
    SC_AFF = 13
    SC_TMP = 14; SC_TMP2 = 15
    gcols = sb("gcols", (128, 16 * 10), F32)
    biases = sb("biases", (128, 332), F32)
    BOFF = {}
    _bo = 0
    for nm, w in (("gb1", 8), ("gb2", 4), ("lnsc", 8), ("lnof", 8),
                  ("e0b1", 32 * 3), ("e0b2", 8 * 3), ("e1b1", 16 * 3),
                  ("e1b2", 16 * 3), ("e1b3", 8 * 3), ("e2b1", 24 * 2),
                  ("e2b2", 8 * 2)):
        BOFF[nm] = _bo
        _bo += w
    assert _bo <= 332, _bo
    noise_sb = sb("noise_sb", (128, 16, 8), F32)
    usage_sb = sb("usage_sb", (1, 8), F32)
    rankm_sb = sb("rankm_sb", (1, 8), F32)
    gb3_sb = sb("gb3_sb", (8, 1), F32)
    affb_sb = sb("affb_sb", (8, 1), F32)

    pb = [ps(f"pb{i}", (128, 512), F32) for i in range(6)]
    pt0 = ps("pt0", (128, 1024), BF16)
    pt1 = ps("pt1", (128, 1024), BF16)

    s_xload = [sem(f"s_xload{j}") for j in range(3)]
    s_ln1 = sem("s_ln1")
    s_sq1 = sem("s_sq1")
    s_ap1 = sem("s_ap1")
    s_tp1 = sem("s_tp1")
    s_ev1 = sem("s_ev1")
    s_cload = sem("s_cload")
    s_wcast = sem("s_wcast")
    s_gload = sem("s_gload")
    s_sc = sem("s_sc")
    s_exp = sem("s_exp")
    s_pn = sem("s_pn")
    s_pt = sem("s_pt")
    s_pte = sem("s_pte")
    s_av = sem("s_av")
    s_ave = sem("s_ave")
    s_op = sem("s_op")
    s_utp = sem("s_utp")
    s_ue = sem("s_ue")
    s_qtm = sem("s_qtm")
    s_ag = sem("s_ag")
    s_agl = sem("s_agl")
    s_gwl = sem("s_gwl")
    s_out = sem("s_out")
    s_dbg = sem("s_dbg")

    blk = es.enter_context(nc.Block())

    esL1b = ExitStack()
    O_fm = esL1b.enter_context(nc.sbuf_tensor("O_fm", [128, 8, TOK], BF16))
    outw_sb = esL1b.enter_context(nc.sbuf_tensor("outw_sb", [128, 8, EMB], BF16))

    esA = ExitStack()

    def sbA(name, shape, dt):
        return esA.enter_context(nc.sbuf_tensor(name, list(shape), dt))

    xln_fm = sbA("xln_fm", (128, 8, XROWS), BF16)
    cos_sb = sbA("cos_sb", (128, XROWS), BF16)
    sin_sb = sbA("sin_sb", (128, XROWS), BF16)

    @blk.sync
    def _(sy):
        s_cload.inc(sy.dma_start(out=identf[:], in_=D["identf"].ap()), 16)
        s_cload.inc(sy.dma_start(out=idx8[:], in_=D["idx8"].ap()), 16)
        s_cload.inc(sy.dma_start(out=negbig[:], in_=D["negbig"].ap()), 16)
        s_cload.inc(sy.dma_start(out=neg1e9[:], in_=D["neg1e9"].ap()), 16)
        s_cload.inc(sy.dma_start(out=noise_sb[:, :, :], in_=D["noise"].ap()), 16)
        s_cload.inc(sy.dma_start(out=usage_sb[:], in_=D["usage"].ap()), 16)
        s_cload.inc(sy.dma_start(out=rankm_sb[:], in_=D["rankmask"].ap()), 16)
        s_cload.inc(sy.dma_start(out=gb3_sb[:], in_=D["gb3"].ap()), 16)
        s_cload.inc(sy.dma_start(out=affb_sb[:], in_=D["affb"].ap()), 16)

    @blk.gpsimd
    def _(g):
        s_gload.inc(g.dma_start(out=identb[:], in_=D["identf"].ap()), 16)
        s_gload.inc(g.dma_start(out=masks[:, 0:128], in_=D["mband"].ap()), 16)
        s_gload.inc(g.dma_start(out=masks[:, 128:256], in_=D["mdiag"].ap()), 16)
        s_gload.inc(g.dma_start(out=masks[:, 256:384], in_=D["mA0"].ap()), 16)
        s_gload.inc(g.dma_start(out=masks[:, 384:512], in_=D["rperm"].ap()), 16)
        s_gload.inc(g.dma_start(out=masks[:, 512:516], in_=D["gmask0"].ap()), 16)
        s_gload.inc(g.dma_start(out=cos_sb[:], in_=D["cosb"].ap()), 16)
        s_gload.inc(g.dma_start(out=sin_sb[:], in_=D["sinb"].ap()), 16)
        s_gload.inc(g.dma_start(out=ltri_b[:], in_=D["ltri"].ap()), 16)
        s_gload.inc(g.dma_start(out=r2560[:], in_=D["r256"].ap()), 16)
        s_gload.inc(g.dma_start(out=gc16_sb[:], in_=D["gc16"].ap()), 16)
        s_gload.inc(g.dma_start(out=rltm_sb[:], in_=D["rltm"].ap()), 16)
        for nm in ("gb1", "gb2", "lnsc", "lnof"):
            w = {"gb1": 8, "gb2": 4, "lnsc": 8, "lnof": 8}[nm]
            s_gload.inc(g.dma_start(
                out=biases[:, BOFF[nm]:BOFF[nm] + w], in_=D[nm].ap()), 16)
        for nm, per in (("e0b1", 32), ("e0b2", 8), ("e1b1", 16), ("e1b2", 16),
                        ("e1b3", 8)):
            for e in range(3):
                s_gload.inc(g.dma_start(
                    out=biases[:, BOFF[nm] + per * e:BOFF[nm] + per * (e + 1)],
                    in_=D[nm].ap()[e]), 16)
        for nm, per in (("e2b1", 24), ("e2b2", 8)):
            for e in range(2):
                s_gload.inc(g.dma_start(
                    out=biases[:, BOFF[nm] + per * e:BOFF[nm] + per * (e + 1)],
                    in_=D[nm].ap()[e]), 16)

    GLOAD_EARLY = 16 * 12

    @blk.vector
    def _(v):
        s_cload.wait(v)
        s_gload.wait(v, at=GLOAD_EARLY)
        v.memset(ones1[:], 1.0)
        v.memset(scal[:], 0.0)
        v.memset(gcols[:], 0.0)
        s_ap1.inc(v.memset(onesc[:], 1.0))

    AP1_BASE = s_ap1.n

    mband = masks[:, 0:128]
    mdiag = masks[:, 128:256]
    mA0 = masks[:, 256:384]
    rpermb = masks[:, 384:512]
    gmask0 = masks[:, 512:516]

    esA1 = ExitStack()
    xr = [esA1.enter_context(nc.sbuf_tensor(f"xr{j}", [128, EMB], F32))
          for j in range(3)]
    xlt = [esA1.enter_context(nc.sbuf_tensor(f"xlt{j}", [128, EMB], BF16))
           for j in range(2)]
    sq_scr = esA1.enter_context(nc.sbuf_tensor("sq_scr", [128, EMB], F32))
    lncols = esA1.enter_context(nc.sbuf_tensor("lncols", [128, 24], F32))

    @blk.sync
    def _(sy):
        for i in range(NXT):
            if i >= 3:
                s_ap1.wait(sy, at=AP1_BASE + (i - 2))
            s_xload[i % 3].inc(sy.dma_start(
                out=xr[i % 3][:], in_=D["xa"].ap()[128 * i:128 * (i + 1), :]),
                16)

    def _ln_stats(v, x, cols):
        mn = cols[:, 0:1]; rstd = cols[:, 1:2]
        ssq = cols[:, 2:3]; t1 = cols[:, 3:4]; t2 = cols[:, 4:5]
        v.tensor_reduce(mn, x, AX.X, OP.add, negate=True)
        v.tensor_tensor(sq_scr[:], x, x, OP.mult)
        v.drain()
        v.tensor_reduce(ssq, sq_scr[:], AX.X, OP.add)
        v.tensor_tensor(t1, mn, mn, OP.mult)
        v.drain()
        v.scalar_tensor_tensor(ssq, t1, FP(-1.0 / EMB), ssq, OP.mult, OP.add)
        v.drain()
        v.tensor_scalar(t2, ssq, FP(1.0 / EMB), LN_EPS,
                        OP.mult, OP.add)
        v.drain()
        return v.reciprocal(rstd, t2)

    @blk.vector
    def _(v):
        for i in range(NXT):
            cols = lncols[:, 6 * (i % 4):6 * (i % 4) + 6]
            s_xload[i % 3].wait(v, at=16 * (i // 3 + 1))
            s_ln1.inc(_ln_stats(v, xr[i % 3][:], cols))
            s_sq1.wait(v, at=i + 1)
            if i >= 2:
                s_tp1.wait(v, at=i - 1)
            mn = cols[:, 0:1]; rstd = cols[:, 1:2]; t1 = cols[:, 3:4]
            v.tensor_scalar(t1, mn, FP(1.0 / EMB), None, OP.mult)
            v.drain()
            s_ap1.inc(v.tensor_scalar(xlt[i % 2][:], xr[i % 3][:], t1, rstd,
                                      OP.add, OP.mult))

    @blk.scalar
    def _(s):
        for i in range(NXT + 2):
            if i < NXT:
                cols = lncols[:, 6 * (i % 4):6 * (i % 4) + 6]
                s_ln1.wait(s, at=i + 1)
                s_sq1.inc(s.activation(cols[:, 1:2], cols[:, 1:2], AF.Sqrt))
            if i >= 2:
                j = i - 2
                s_tp1.wait(s, at=j + 1)
                bank = (pt0 if j % 2 == 0 else pt1)
                s_ev1.inc(s.copy(xln_fm[:, :, 128 * j:128 * (j + 1)],
                                 bank[:].rearrange("p (c j) -> p c j", c=8)))

    @blk.tensor
    def _(t):
        s_cload.wait(t)
        s_gload.wait(t, at=GLOAD_EARLY)
        for i in range(NXT):
            t.wait_ge(s_ap1.h, AP1_BASE + i + 1)
            bank = (pt0 if i % 2 == 0 else pt1)
            if i >= 2:
                s_ev1.wait(t, at=i - 1)
            for j in range(8):
                ins = t.transpose(bank[:, 128 * j:128 * (j + 1)],
                                  xlt[i % 2][:, 128 * j:128 * (j + 1)],
                                  identb[:])
            s_tp1.inc(ins)


    esA1.close()
    esB = ExitStack()

    def sbB(name, shape, dt):
        return esB.enter_context(nc.sbuf_tensor(name, list(shape), dt))

    wq = [sbB(f"wq{j}", (128, 8, 128), BF16) for j in range(2)]
    wk = [sbB(f"wk{j}", (128, 8, 128), BF16) for j in range(2)]
    wv = [sbB(f"wv{j}", (128, 8, 128), BF16) for j in range(2)]
    qa_sb = sbB("qa_sb", (128, TOK), BF16)
    ka_sb = sbB("ka_sb", (128, XROWS), BF16)
    qrot = sbB("qrot", (128, TOK), BF16)
    krot = sbB("krot", (128, XROWS), BF16)
    rsc1 = sbB("rsc1", (128, 512), F32)
    rsc2 = sbB("rsc2", (128, 512), F32)
    vtmg = [sbB(f"vtmg{j}", (128, 18, 128), BF16) for j in range(2)]
    p_sb = [sbB(f"p_sb{j}", (128, 260), BF16) for j in range(2)]
    pn_sb = [sbB(f"pn_sb{j}", (128, 260), BF16) for j in range(2)]
    pt_sb = [sbB(f"pt_sb{j}", (128, 256), BF16) for j in range(2)]
    ptg_sb = [sbB(f"ptg_sb{j}", (4, 128), BF16) for j in range(2)]
    acols = sbB("acols", (128, 8), F32)

    s_wq = [sem(f"s_wq{j}") for j in range(2)]
    s_wk = [sem(f"s_wk{j}") for j in range(2)]
    s_wv = [sem(f"s_wv{j}") for j in range(2)]
    s_qkq = sem("s_qkq")
    s_qkk = sem("s_qkk")
    s_qkv = sem("s_qkv")
    s_qaev = sem("s_qaev")
    s_kaev = sem("s_kaev")
    s_vev = sem("s_vev")
    s_pmq = sem("s_pmq")
    s_pmk = sem("s_pmk")
    s_rpq = sem("s_rpq")
    s_rpk = sem("s_rpk")
    s_rmax = sem("s_rmax")

    NI = 32
    KCH = [(0, 512), (512, 512), (1024, 512), (1536, 512), (2048, 256)]
    QCH = [(128, 512), (640, 512), (1152, 512), (1664, 512)]

    def hq(i):
        return (i % 2, i // 2)

    CAST_CHUNKS = []
    for snm, n_e, R, C in (("e0w1", 3, 1024, 4096), ("e1w1", 3, 1024, 2048),
                           ("e1w2", 3, 2048, 2048), ("e2w1", 2, 1024, 3072)):
        rc = 128 if C >= 3072 else 256
        for ei in range(n_e):
            for ch in range(R // rc):
                CAST_CHUNKS.append((snm, ei, rc * ch, rc))
    NCAST = len(CAST_CHUNKS)

    @blk.gpsimd
    def _(g):
        s_tp1.wait(g, at=NXT)
        ci = 0
        for gi in range(8):
            j = gi % 2
            if gi >= 2:
                s_qkq.wait(g, at=4 * (gi - 1))
                s_qkk.wait(g, at=5 * (gi - 1))
                s_qkv.wait(g, at=5 * (gi - 1))
            qsrc = D["qkvwT"].ap()[:, 128 * gi:128 * (gi + 1)]
            ksrc = D["qkvwT"].ap()[:, 1024 + 128 * gi:1024 + 128 * (gi + 1)]
            vsrc = D["qkvwT"].ap()[:, 2048 + 128 * gi:2048 + 128 * (gi + 1)]
            s_wq[j].inc(g.dma_start(
                out=wq[j][:, :, :],
                in_=qsrc.rearrange("(c p) e -> p c e", p=128)), 16)
            s_wk[j].inc(g.dma_start(
                out=wk[j][:, :, :],
                in_=ksrc.rearrange("(c p) e -> p c e", p=128)), 16)
            s_wv[j].inc(g.dma_start(
                out=wv[j][:, :, :],
                in_=vsrc.rearrange("(c p) e -> p c e", p=128)), 16)
            if gi >= 1:
                ntake = NCAST * gi // 7 - ci if gi < 7 else NCAST - ci
                for _ in range(ntake):
                    snm, ei, r0, rc = CAST_CHUNKS[ci]
                    s_wcast.inc(g.dma_start(
                        out=D["c_" + snm].ap()[ei][r0:r0 + rc, :],
                        in_=D[snm].ap()[ei][r0:r0 + rc, :]), 16)
                    ci += 1
        wsrc = D["outwT"].ap().rearrange("(c p) e -> p c e", p=128)
        s_gload.inc(g.dma_start(out=outw_sb[:, :, :], in_=wsrc), 16)

    class BankPair:

        def __init__(self):
            self.uses = []

        def acquire(self, eng):
            if len(self.uses) >= 2:
                sem, cnt = self.uses[-2]
                sem.wait(eng, at=cnt)

        def commit(self, release_sem, release_cnt):
            self.uses.append((release_sem, release_cnt))

    def emit_pt(t, io):
        ptbank = pt0 if io % 2 == 0 else pt1
        s_pn.wait(t, at=io + 1)
        if io >= 2:
            s_pte.wait(t, at=io - 1)
        t.transpose(ptbank[:, 0:128], pn_sb[io % 2][:, 0:128], identb[:])
        t.transpose(ptbank[:, 128:256], pn_sb[io % 2][:, 128:256], identb[:])
        s_pt.inc(t.transpose(ptbank[0:4, 256:384],
                             pn_sb[io % 2][:, 256:260], identb[:]))

    def emit_av(t, io):
        go = io // NI
        ho, qto = hq(io % NI)
        bank2 = pb[4 + io % 2]
        s_pte.wait(t, at=io + 1)
        if io >= 2:
            s_ave.wait(t, at=io - 1)
        vt = vtmg[go % 2]
        t.matmul(bank2[0:64, 0:128], vt[:, qto, 64 * ho:64 * ho + 64],
                 pt_sb[io % 2][:, 0:128], start=True, stop=False)
        t.matmul(bank2[0:64, 0:128], vt[:, qto + 1, 64 * ho:64 * ho + 64],
                 pt_sb[io % 2][:, 128:256], start=False, stop=False)
        s_av.inc(t.matmul(bank2[0:64, 0:128],
                          vt[0:4, 17, 64 * ho:64 * ho + 64],
                          ptg_sb[io % 2][:, :], start=False, stop=True))


    bp01 = BankPair()
    bp23 = BankPair()

    @blk.tensor
    def _(t):
        s_ev1.wait(t)
        for gi in range(8):
            j = gi % 2
            s_wq[j].wait(t, at=16 * (gi // 2 + 1))
            for tc, (o, w) in enumerate(QCH):
                bank = pb[tc % 2]
                bp01.acquire(t)
                for k in range(8):
                    ins = t.matmul(bank[:, 0:w], wq[j][:, k, :],
                                   xln_fm[:, k, o:o + w],
                                   start=(k == 0), stop=(k == 7))
                s_qkq.inc(ins)
                bp01.commit(s_qaev, gi * 4 + tc + 1)
            s_wk[j].wait(t, at=16 * (gi // 2 + 1))
            for tc, (o, w) in enumerate(KCH):
                bank = pb[tc % 2]
                bp01.acquire(t)
                for k in range(8):
                    ins = t.matmul(bank[:, 0:w], wk[j][:, k, :],
                                   xln_fm[:, k, o:o + w],
                                   start=(k == 0), stop=(k == 7))
                s_qkk.inc(ins)
                bp01.commit(s_kaev, gi * 5 + tc + 1)
            s_qaev.wait(t, at=4 * (gi + 1))
            for tc, (o, w) in enumerate(QCH):
                bank = pb[2 + tc % 2]
                bp23.acquire(t)
                s_pmq.inc(t.matmul(bank[:, 0:w], rpermb,
                                   qa_sb[:, o - 128:o - 128 + w],
                                   start=True, stop=True))
                bp23.commit(s_rpq, gi * 4 + tc + 1)
            s_kaev.wait(t, at=5 * (gi + 1))
            for tc, (o, w) in enumerate(KCH):
                bank = pb[2 + tc % 2]
                bp23.acquire(t)
                s_pmk.inc(t.matmul(bank[:, 0:w], rpermb, ka_sb[:, o:o + w],
                                   start=True, stop=True))
                bp23.commit(s_rpk, gi * 5 + tc + 1)
            s_wv[j].wait(t, at=16 * (gi // 2 + 1))
            for bg in range(5):
                bank = pb[bg % 2]
                r0 = 4 * bg
                nt = min(4, 18 - r0)
                bp01.acquire(t)
                for r in range(r0, r0 + nt):
                    for k in range(8):
                        ins = t.matmul(
                            bank[:, 128 * (r - r0):128 * (r - r0) + 128],
                            xln_fm[:, k, 128 * r:128 * (r + 1)],
                            wv[j][:, k, :],
                            start=(k == 0 and r == r0),
                            stop=(k == 7 and r == r0 + nt - 1))
                s_qkv.inc(ins)
                bp01.commit(s_vev, gi * 5 + bg + 1)
            s_rpq.wait(t, at=4 * (gi + 1))
            s_rpk.wait(t, at=5 * (gi + 1))
            s_vev.wait(t, at=5 * (gi + 1))
            for i in range(NI):
                h, qt = hq(i)
                ii = gi * NI + i
                bank = pb[2 + i % 2]
                bp23.acquire(t)
                q_sl = qrot[64 * h:64 * h + 64, 128 * qt:128 * qt + 128]
                t.matmul(bank[:, 0:128], q_sl,
                         krot[64 * h:64 * h + 64, 128 * qt:128 * qt + 128],
                         start=True, stop=False)
                t.matmul(bank[:, 128:256], q_sl,
                         krot[64 * h:64 * h + 64,
                              128 * qt + 128:128 * qt + 256],
                         start=False, stop=False)
                t.matmul(bank[:, 256:260], q_sl,
                         krot[64 * h:64 * h + 64, 2176:2180],
                         start=False, stop=False)
                masks_todo = [(slice(128, 256), mdiag)]
                if qt % 2 == 0:
                    masks_todo.append((slice(0, 128), mA0 if qt == 0 else mband))
                if qt in (0, 1):
                    masks_todo.append((slice(256, 260), gmask0))
                for mi, (sl, mt) in enumerate(masks_todo):
                    ins = t.matmul(bank[:, sl], identb[:], mt,
                                   start=False, stop=(mi == len(masks_todo) - 1))
                s_sc.inc(ins)
                bp23.commit(s_exp, ii + 1)
                if ii >= 1:
                    emit_pt(t, ii - 1)
                if ii >= 2:
                    emit_av(t, ii - 2)
        NTOT = 8 * NI
        emit_pt(t, NTOT - 1)
        emit_av(t, NTOT - 2)
        emit_av(t, NTOT - 1)

    @blk.scalar
    def _(s):
        for gi in range(8):
            for tc, (o, w) in enumerate(QCH):
                s_qkq.wait(s, at=gi * 4 + tc + 1)
                if gi > 0 and tc == 0:
                    s_rpq.wait(s, at=4 * gi)
                    s_pmq.wait(s, at=4 * gi)
                s_qaev.inc(s.copy(qa_sb[:, o - 128:o - 128 + w],
                                  pb[tc % 2][:, 0:w]))
            for tc, (o, w) in enumerate(KCH):
                s_qkk.wait(s, at=gi * 5 + tc + 1)
                if gi > 0 and tc == 0:
                    s_rpk.wait(s, at=5 * gi)
                    s_pmk.wait(s, at=5 * gi)
                s_kaev.inc(s.copy(ka_sb[:, o:o + w], pb[tc % 2][:, 0:w]))
            for bg in range(5):
                r0 = 4 * bg
                nt = min(4, 18 - r0)
                s_qkv.wait(s, at=gi * 5 + bg + 1)
                if gi >= 2 and bg == 0:
                    s_av.wait(s, at=(gi - 1) * NI)
                s_vev.inc(s.copy(
                    vtmg[gi % 2][:, r0:r0 + nt, :],
                    pb[bg % 2][:, 0:128 * nt].rearrange(
                        "p (r j) -> p r j", r=nt)))
            for i in range(NI):
                h, qt = hq(i)
                ii = gi * NI + i
                bank = pb[2 + i % 2]
                s_rmax.wait(s, at=ii + 1)
                s_exp.inc(s.activation(
                    p_sb[i % 2][:, 0:260], bank[:, 0:260], AF.Exp,
                    bias=acols[:, 4 + i % 2:5 + i % 2], scale=0.125,
                    accum_out=acols[:, 6 + i % 2:7 + i % 2]))
                if ii >= 1:
                    io = ii - 1
                    ptbank = pt0 if io % 2 == 0 else pt1
                    s_pt.wait(s, at=io + 1)
                    if io >= 2:
                        s_av.wait(s, at=io - 1)
                    s.copy(pt_sb[io % 2][:, 0:256], ptbank[:, 0:256])
                    s_pte.inc(s.copy(ptg_sb[io % 2][:, :],
                                     ptbank[0:4, 256:384]))
                if ii >= 2:
                    io = ii - 2
                    go = io // NI
                    ho, qto = hq(io % NI)
                    s_av.wait(s, at=io + 1)
                    s_ave.inc(s.copy(
                        O_fm[64 * ho:64 * ho + 64, go, 128 * qto:128 * qto + 128],
                        pb[4 + io % 2][0:64, 0:128]))
        NTOT = 8 * NI
        for io in (NTOT - 1,):
            ptbank = pt0 if io % 2 == 0 else pt1
            s_pt.wait(s, at=io + 1)
            s_av.wait(s, at=io - 1)
            s.copy(pt_sb[io % 2][:, 0:256], ptbank[:, 0:256])
            s_pte.inc(s.copy(ptg_sb[io % 2][:, :], ptbank[0:4, 256:384]))
        for io in (NTOT - 2, NTOT - 1):
            go = io // NI
            ho, qto = hq(io % NI)
            s_av.wait(s, at=io + 1)
            s_ave.inc(s.copy(
                O_fm[64 * ho:64 * ho + 64, go, 128 * qto:128 * qto + 128],
                pb[4 + io % 2][0:64, 0:128]))

    @blk.vector
    def _(v):
        for gi in range(8):
            for tc, (o, w) in enumerate(QCH):
                bank = pb[2 + tc % 2]
                s_pmq.wait(v, at=gi * 4 + tc + 1)
                if gi > 0 and tc == 0:
                    s_sc.wait(v, at=NI * gi)
                v.drain()
                v.tensor_tensor(rsc1[:, 0:w], qa_sb[:, o - 128:o - 128 + w],
                                cos_sb[:, o:o + w], OP.mult)
                v.tensor_tensor(rsc2[:, 0:w], bank[:, 0:w],
                                sin_sb[:, o:o + w], OP.mult)
                v.drain()
                s_rpq.inc(v.tensor_tensor(qrot[:, o - 128:o - 128 + w],
                                          rsc1[:, 0:w], rsc2[:, 0:w], OP.add))
            for tc, (o, w) in enumerate(KCH):
                bank = pb[2 + tc % 2]
                s_pmk.wait(v, at=gi * 5 + tc + 1)
                if gi > 0 and tc == 0:
                    s_sc.wait(v, at=NI * gi)
                v.drain()
                v.tensor_tensor(rsc1[:, 0:w], ka_sb[:, o:o + w],
                                cos_sb[:, o:o + w], OP.mult)
                v.tensor_tensor(rsc2[:, 0:w], bank[:, 0:w],
                                sin_sb[:, o:o + w], OP.mult)
                v.drain()
                s_rpk.inc(v.tensor_tensor(krot[:, o:o + w],
                                          rsc1[:, 0:w], rsc2[:, 0:w], OP.add))
            for i in range(NI):
                ii = gi * NI + i
                bank = pb[2 + i % 2]
                s_sc.wait(v, at=ii + 1)
                v.tensor_reduce(acols[:, 0 + i % 2:1 + i % 2], bank[:, 0:260],
                                AX.X, OP.max, negate=True)
                v.drain()
                s_rmax.inc(v.tensor_scalar(acols[:, 4 + i % 2:5 + i % 2],
                                           acols[:, 0 + i % 2:1 + i % 2],
                                           0.125, None, OP.mult))
                s_exp.wait(v, at=ii + 1)
                v.reciprocal(acols[:, 2 + i % 2:3 + i % 2],
                             acols[:, 6 + i % 2:7 + i % 2])
                v.drain()
                if ii >= 1:
                    s_pt.wait(v, at=ii - 1)
                s_pn.inc(v.tensor_scalar(pn_sb[i % 2][:, 0:260],
                                         p_sb[i % 2][:, 0:260],
                                         acols[:, 2 + i % 2:3 + i % 2],
                                         None, OP.mult))

    if "attn" in debug:
        @blk.sync
        def _(sy):
            s_ave.wait(sy)
            s_dbg.inc(sy.dma_start(out=dbg["d_ofm"].ap(), in_=O_fm[:]), 16)

    if "attn2" in debug:
        @blk.sync
        def _(sy):
            s_ave.wait(sy)
            s_dbg.inc(sy.dma_start(out=dbg["d_qrot"].ap(), in_=qrot[:]), 16)
            s_dbg.inc(sy.dma_start(out=dbg["d_krot"].ap(), in_=krot[:]), 16)
            s_dbg.inc(sy.dma_start(out=dbg["d_qa"].ap(), in_=qa_sb[:]), 16)
            s_dbg.inc(sy.dma_start(out=dbg["d_vtm"].ap()[0], in_=vtmg[0][:]), 16)
            s_dbg.inc(sy.dma_start(out=dbg["d_vtm"].ap()[1], in_=vtmg[1][:]), 16)

    esB.close()
    esA.close()

    NTOT = 8 * NI
    esC = ExitStack()

    def sbC(name, shape, dt):
        return esC.enter_context(nc.sbuf_tensor(name, list(shape), dt))

    xr2 = [sbC(f"xr2_{j}", (128, EMB), F32) for j in range(3)]
    y_sb = [sbC(f"y_sb{j}", (128, EMB), F32) for j in range(2)]
    xlt2 = [sbC(f"xlt2_{j}", (128, EMB), BF16) for j in range(2)]
    u_tm = [sbC(f"u_tm{j}", (128, EMB), BF16) for j in range(2)]
    ones128b = sbC("ones128b", (128, 128), BF16)
    ones128f = sbC("ones128f", (128, 128), F32)
    affw_sb = sbC("affw_sb", (128, 8, 8), BF16)
    dev_scr = sbC("dev_scr", (128, 128), F32)
    sq2_scr = sbC("sq2_scr", (128, 128), F32)
    rm_bc = sbC("rm_bc", (128, 8), F32)
    rankm_bf = sbC("rankm_bf", (1, 8), BF16)
    bis_scr = sbC("bis_scr", (128, 8, 128), F32)
    bis_mid = sbC("bis_mid", (128, 2, 8), F32)
    bis_cf = sbC("bis_cf", (128, 2, 8), F32)
    bis_cb = sbC("bis_cb", (128, 2, 8), BF16)
    bis_io = sbC("bis_io", (128, 8), F32)

    s_x2 = s_xload
    X2B = [sm.n for sm in s_x2]
    s_ysp = [sem(f"s_ysp{j}") for j in range(2)]
    s_yv = sem("s_yv")
    s_sq2 = sem("s_sq2")
    s_ln2 = sem("s_ln2")
    s_impw = sem("s_impw")
    s_u = sem("s_u")
    s_bv = [sem(f"s_bv{j}") for j in range(2)]
    s_bt = [sem(f"s_bt{j}") for j in range(2)]
    s_bhi = sem("s_bhi")
    s_tempc = sem("s_tempc")
    s_rmb = sem("s_rmb")
    s_mskc = sem("s_mskc")
    s_affm = sem("s_affm")
    s_selw = sem("s_selw")
    s_cc = sem("s_cc")

    @blk.gpsimd
    def _(g):
        s_ave.wait(g, at=NTOT)
        s_gwl.inc(g.dma_start(
            out=affw_sb[:, :, :],
            in_=D["affw"].ap().rearrange("(c p) e -> p c e", p=128)), 16)

    @blk.sync
    def _(sy):
        s_ave.wait(sy, at=NTOT)
        for ti in range(NOT + 2):
            if ti < NOT:
                if ti >= 3:
                    s_u.wait(sy, at=ti - 2)
                s_x2[ti % 3].inc(sy.dma_start(
                    out=xr2[ti % 3][:],
                    in_=D["xa"].ap()[128 * (ti + 1):128 * (ti + 2), :]), 16)
            if ti >= 2:
                tj = ti - 2
                s_yv.wait(sy, at=tj + 1)
                s_ysp[tj % 2].inc(sy.dma_start(
                    out=D["yspill"].ap()[128 * tj:128 * (tj + 1), :],
                    in_=y_sb[tj % 2][:]), 16)

    @blk.vector
    def _(v):
        s_ave.wait(v, at=NTOT)
        v.memset(ones128b[:], 1.0)
        v.memset(ones128f[:], 1.0)
        v.tensor_copy(rankm_bf[:], rankm_sb[:])
        for ti in range(NOT):
            cols = gcols[:, 10 * ti:10 * ti + 10]
            s_x2[ti % 3].wait(v, at=X2B[ti % 3] + 16 * (ti // 3 + 1))
            s_op.wait(v, at=2 * (ti + 1))
            if ti >= 2:
                s_ysp[ti % 2].wait(v, at=16 * (ti // 2))
            bank0 = pb[2 * (ti % 2)]
            bank1 = pb[2 * (ti % 2) + 1]
            v.tensor_tensor(y_sb[ti % 2][:, 0:512], xr2[ti % 3][:, 0:512],
                            bank0[:], OP.add)
            s_yv.inc(v.tensor_tensor(y_sb[ti % 2][:, 512:1024],
                                     xr2[ti % 3][:, 512:1024], bank1[:],
                                     OP.add))
            v.drain()
            s_ln1.inc(_ln_stats(v, y_sb[ti % 2][:], cols))
            s_sq2.wait(v, at=ti + 1)
            mn = cols[:, 0:1]; rstd = cols[:, 1:2]; t1 = cols[:, 3:4]
            imp2 = cols[:, 5:6]; L1c = cols[:, 6:7]; comb = cols[:, 7:8]
            v.tensor_scalar(t1, mn, FP(1.0 / EMB), None, OP.mult)
            v.tensor_tensor(imp2, rstd, rstd, OP.mult)
            v.drain()
            v.tensor_scalar(xlt2[ti % 2][:], y_sb[ti % 2][:], t1, rstd,
                            OP.add, OP.mult)
            s_ln2.inc(v.tensor_tensor(imp2, imp2, cols[:, 2:3], OP.mult))
            v.drain()
            v.tensor_reduce(L1c, xlt2[ti % 2][:], AX.X, OP.add,
                            apply_absolute_value=True)
            v.drain()
            v.tensor_scalar(L1c, L1c, 1.0, FP(1e-8), OP.mult, OP.add)
            v.drain()
            v.reciprocal(L1c, L1c)
            v.drain()
            v.tensor_tensor(comb, rstd, L1c, OP.mult)
            if ti >= 2:
                s_utp.wait(v, at=ti - 1)
            v.drain()
            s_u.inc(v.tensor_scalar(u_tm[ti % 2][:], y_sb[ti % 2][:], t1,
                                    comb, OP.add, OP.mult))

    @blk.scalar
    def _(s):
        for ti in range(NOT + 2):
            if ti < NOT:
                cols = gcols[:, 10 * ti:10 * ti + 10]
                s_ln1.wait(s, at=NXT + ti + 1)
                s_sq2.inc(s.activation(cols[:, 1:2], cols[:, 1:2], AF.Sqrt))
                s_ln2.wait(s, at=ti + 1)
                s_impw.inc(s.activation(xchg[:, ti:ti + 1], cols[:, 5:6],
                                        AF.Sqrt))
            if ti >= 2:
                j = ti - 2
                s_utp.wait(s, at=j + 1)
                bank = (pt0 if j % 2 == 0 else pt1)
                s_ue.inc(s.copy(u_fm[:, :, 128 * j:128 * (j + 1)],
                                bank[:].rearrange("p (c j) -> p c j", c=8)))

    @blk.tensor
    def _(t):
        s_gload.wait(t)
        s_ave.wait(t, at=NTOT)
        for ti in range(NOT + 2):
            if ti < NOT:
                for ec in range(2):
                    bank = pb[2 * (ti % 2) + ec]
                    if ti >= 2:
                        s_yv.wait(t, at=ti - 1)
                    for k in range(8):
                        ins = t.matmul(bank[:],
                                       O_fm[:, k, 128 * ti:128 * (ti + 1)],
                                       outw_sb[:, k, 512 * ec:512 * (ec + 1)],
                                       start=(k == 0), stop=(k == 7))
                    s_op.inc(ins)
            if ti >= 2:
                tj = ti - 2
                t.wait_ge(s_u.h, tj + 1)
                bank = (pt0 if tj % 2 == 0 else pt1)
                if tj >= 2:
                    s_ue.wait(t, at=tj - 1)
                for j in range(8):
                    ins = t.transpose(bank[:, 128 * j:128 * (j + 1)],
                                      u_tm[tj % 2][:, 128 * j:128 * (j + 1)],
                                      identb[:])
                s_utp.inc(ins)
        s_gwl.wait(t)
        for ti in range(NOT):
            s_ue.wait(t, at=ti + 1)
            for k in range(8):
                ins = t.matmul(pb[4][:, 8 * ti:8 * ti + 8],
                               u_fm[:, k, 128 * ti:128 * (ti + 1)],
                               affw_sb[:, k, :],
                               start=(k == 0 and ti == 0),
                               stop=(k == 7 and ti == NOT - 1))
        s_qtm.inc(ins)

    @blk.scalar
    def _(s):
        s_qtm.wait(s)
        s_qtm.inc(s.copy(xchg[:, 16:144], pb[4][:, 0:128]))

    @blk.gpsimd
    def _(g):
        s_impw.wait(g, at=NOT)
        s_qtm.wait(g, at=2)
        g.dma_start(out=D["agin"].ap(), in_=xchg[:]).then_inc(s_ag.h, 16)
        s_ag.n += 16
        g.wait_ge(s_ag.h, 16)
        g.collective_compute(
            "AllGather", OP.bypass, replica_groups=[list(range(8))],
            ins=[D["agin"].ap()], outs=[D["agout"].ap()],
        ).then_inc(s_cc.h, 1)
        s_cc.n += 1
        g.wait_ge(s_cc.h, 1)
        s_agl.inc(g.dma_start(
            out=gath[:, :, :],
            in_=D["agout"].ap().rearrange("(r p) c -> p r c", p=128)), 16)

    BIS_R = 14
    B_LO = (SC_LO, 16)
    B_HI = (SC_HI, 17)
    B_W = (SC_MID, 18)
    B_NC = (SC_PRED, 19)
    B_TB = 20
    B_TB2 = 21

    @blk.tensor
    def _(t):
        s_bhi.wait(t, at=1)
        s_bhi.inc(t.transpose(pb[5][0:1, 0:128], scal[:, B_TB:B_TB + 1],
                              identf[:]))
        s_bhi.wait(t, at=3)
        s_bhi.inc(t.matmul(pb[5][:, 8:9], ones128f[0:1, :],
                           scal[0:1, B_TB2:B_TB2 + 1], start=True, stop=True))
        for r in range(BIS_R):
            s_bv[0].wait(t, at=r + 1)
            s_bt[0].inc(t.matmul(
                pb[5][:, 16 + 8 * (r % 2):16 + 8 * (r % 2) + 8],
                ones128b[:], bis_cb[:, 0, :], start=True, stop=True))
            s_bv[1].wait(t, at=2 * (r + 1))
            s_bt[1].inc(t.matmul(
                pb[5][:, 40 + 8 * (r % 2):40 + 8 * (r % 2) + 8],
                ones128b[:], bis_cb[:, 1, :], start=True, stop=True))
        s_tempc.wait(t, at=1)
        s_tempc.inc(t.matmul(pb[5][:, 4:5], ones128f[:],
                             scal[:, SC_CNT:SC_CNT + 1], start=True,
                             stop=True))
        s_tempc.wait(t, at=3)
        s_tempc.inc(t.matmul(pb[5][:, 5:6], ones128f[:],
                             scal[:, SC_CNT:SC_CNT + 1], start=True,
                             stop=True))
        s_bhi.wait(t, at=1)
        s_qtm.wait(t, at=2)
        s_rmb.inc(t.matmul(pb[4][:, 8:16], ones1[0:1, :], rankm_bf[0:1, :],
                           start=True, stop=True))
        s_mskc.wait(t, at=1)
        for r in range(8):
            for ti in range(NOT):
                ins = t.matmul(pb[5][0:8, 32:33],
                               gath[:, r, 16 + 8 * ti:16 + 8 * (ti + 1)],
                               dev_scr[:, 16 * r + ti:16 * r + ti + 1],
                               start=(r == 0 and ti == 0),
                               stop=(r == 7 and ti == NOT - 1))
        s_affm.inc(ins)

    @blk.vector
    def _(v):
        s_agl.wait(v)
        v.tensor_scalar(bis_io[:], idx8[:], 1.0, None, OP.add)
        v.drain()
        v.tensor_reduce(scal[:, B_TB:B_TB + 1], gath[:, :, 0:16], AX.XY,
                        OP.max)
        v.drain()
        s_bhi.inc(v.tensor_copy(scal[:, B_TB2:B_TB2 + 1],
                                scal[:, B_TB:B_TB + 1]))
        s_bhi.wait(v, at=2)
        v.tensor_reduce(scal[0:1, B_TB2:B_TB2 + 1], pb[5][0:1, 0:128], AX.X,
                        OP.max)
        v.drain()
        s_bhi.inc(v.tensor_copy(scal[0:1, B_TB:B_TB + 1],
                                scal[0:1, B_TB2:B_TB2 + 1]))
        s_bhi.wait(v, at=4)
        for srch in range(2):
            v.tensor_copy(scal[:, B_HI[srch]:B_HI[srch] + 1], pb[5][:, 8:9])
            v.memset(scal[:, B_LO[srch]:B_LO[srch] + 1], 0.0)
        v.drain()
        for srch in range(2):
            v.tensor_scalar(scal[:, B_W[srch]:B_W[srch] + 1],
                            scal[:, B_HI[srch]:B_HI[srch] + 1], 0.125, None,
                            OP.mult)
        for r in range(BIS_R):
            for srch in range(2):
                LO = scal[:, B_LO[srch]:B_LO[srch] + 1]
                HI = scal[:, B_HI[srch]:B_HI[srch] + 1]
                W = scal[:, B_W[srch]:B_W[srch] + 1]
                NC = scal[:, B_NC[srch]:B_NC[srch] + 1]
                v.drain()
                v.tensor_scalar(bis_mid[:, srch, :], bis_io[:], W, LO,
                                OP.mult, OP.add)
                v.drain()
                for m in range(8):
                    v.tensor_scalar(bis_scr[:, m, :], gath[:, :, 0:16],
                                    bis_mid[:, srch, m:m + 1], None,
                                    OP.is_le, OP.add,
                                    accum_out=bis_cf[:, srch, m:m + 1])
                v.drain()
                ins = v.tensor_copy(bis_cb[:, srch, :], bis_cf[:, srch, :])
                s_bv[srch].inc(ins)
            for srch in range(2):
                LO = scal[:, B_LO[srch]:B_LO[srch] + 1]
                HI = scal[:, B_HI[srch]:B_HI[srch] + 1]
                W = scal[:, B_W[srch]:B_W[srch] + 1]
                NC = scal[:, B_NC[srch]:B_NC[srch] + 1]
                KV = K1 if srch == 0 else K2
                base = 16 if srch == 0 else 40
                reg = pb[5][:, base + 8 * (r % 2):base + 8 * (r % 2) + 8]
                TB = scal[:, B_TB:B_TB + 1]
                TB2 = scal[:, B_TB2:B_TB2 + 1]
                TMP = scal[:, SC_TMP:SC_TMP + 1]
                s_bt[srch].wait(v, at=r + 1)
                v.tensor_scalar(bis_scr[:, 0, 0:8], reg, float(KV), None,
                                OP.is_lt, OP.add, accum_out=NC)
                v.drain()
                v.tensor_scalar(TB, NC, 1.0, None, OP.add)
                v.drain()
                v.tensor_scalar(TB, TB, W, LO, OP.mult, OP.add)
                v.tensor_scalar(TB2, NC, 8.0, None, OP.is_ge)
                v.drain()
                v.tensor_tensor(TMP, HI, TB, OP.subtract)
                v.drain()
                v.scalar_tensor_tensor(HI, TMP, TB2, TB, OP.mult, OP.add)
                v.tensor_scalar(LO, NC, W, LO, OP.mult, OP.add)
                v.drain()
                v.tensor_tensor(W, HI, LO, OP.subtract)
                v.drain()
                ins = v.tensor_scalar(W, W, 0.125, None, OP.mult)
                if srch == 0:
                    s_bv[1].inc(ins)
        v.drain()
        v.tensor_copy(scal[:, SC_S1:SC_S1 + 1], scal[:, B_HI[0]:B_HI[0] + 1])
        v.tensor_copy(scal[:, SC_S2:SC_S2 + 1], scal[:, B_HI[1]:B_HI[1] + 1])
        v.drain()
        v.tensor_scalar(scal[:, SC_T1:SC_T1 + 1], scal[:, SC_S1:SC_S1 + 1],
                        float(_LW), None, OP.mult)
        v.tensor_scalar(scal[:, SC_T2:SC_T2 + 1], scal[:, SC_S2:SC_S2 + 1],
                        float(_HW), None, OP.mult)
        v.drain()
        v.tensor_tensor(scal[:, SC_THR:SC_THR + 1], scal[:, SC_T1:SC_T1 + 1],
                        scal[:, SC_T2:SC_T2 + 1], OP.add)
        v.drain()
        v.tensor_scalar(dev_scr[:], gath[:, :, 0:16], 1.0, None,
                        OP.mult, OP.add, accum_out=scal[:, SC_T1:SC_T1 + 1])
        v.drain()
        s_tempc.inc(v.tensor_copy(scal[:, SC_CNT:SC_CNT + 1],
                                  scal[:, SC_T1:SC_T1 + 1]))
        s_tempc.wait(v, at=2)
        v.tensor_scalar(scal[:, SC_TMP:SC_TMP + 1], pb[5][:, 4:5],
                        FP(1.0 / 16384.0), None, OP.mult)
        v.drain()
        v.tensor_scalar(dev_scr[:], gath[:, :, 0:16],
                        scal[:, SC_TMP:SC_TMP + 1], None, OP.subtract)
        v.drain()
        v.tensor_tensor(dev_scr[:], dev_scr[:], dev_scr[:], OP.mult)
        v.drain()
        v.tensor_scalar(sq2_scr[:], dev_scr[:], 1.0, None, OP.mult,
                        OP.add, accum_out=scal[:, SC_T2:SC_T2 + 1])
        v.drain()
        s_tempc.inc(v.tensor_copy(scal[:, SC_CNT:SC_CNT + 1],
                                  scal[:, SC_T2:SC_T2 + 1]))
        s_tempc.wait(v, at=4)
        s_tempc.inc(v.tensor_scalar(scal[:, SC_TMP2:SC_TMP2 + 1],
                                    pb[5][:, 5:6], FP(1.0 / 16384.0), None,
                                    OP.mult))
        s_tempc.wait(v, at=6)
        v.tensor_scalar(scal[:, SC_RTEMP:SC_RTEMP + 1],
                        scal[:, SC_TMP2:SC_TMP2 + 1], 0.1, 1.0, OP.mult,
                        OP.add)
        v.drain()
        v.reciprocal(scal[:, SC_RTEMP:SC_RTEMP + 1],
                     scal[:, SC_RTEMP:SC_RTEMP + 1])
        v.drain()
        v.tensor_scalar(scal[:, 32:48], xchg[:, 0:16],
                        scal[:, SC_THR:SC_THR + 1], None, OP.is_gt)
        v.drain()
        ins = v.tensor_scalar(scal[:, 32:48], scal[:, 32:48],
                              scal[:, SC_RTEMP:SC_RTEMP + 1], None, OP.mult)
        s_selw.inc(ins)
        s_rmb.wait(v, at=2)
        for r in range(8):
            v.tensor_scalar(dev_scr[:, 16 * r:16 * r + 16],
                            gath[:, r, 0:16], scal[:, SC_THR:SC_THR + 1],
                            None, OP.is_gt)
        v.drain()
        for r in range(8):
            ins = v.tensor_scalar(dev_scr[:, 16 * r:16 * r + 16],
                                  dev_scr[:, 16 * r:16 * r + 16],
                                  rm_bc[:, r:r + 1], None, OP.mult)
        s_mskc.inc(ins)

    @blk.scalar
    def _(s):
        s_tempc.wait(s, at=5)
        s_tempc.inc(s.activation(scal[:, SC_TMP2:SC_TMP2 + 1],
                                 scal[:, SC_TMP2:SC_TMP2 + 1], AF.Sqrt))
        s_rmb.wait(s, at=1)
        s_rmb.inc(s.copy(rm_bc[:], pb[4][:, 8:16]))
        s_affm.wait(s, at=1)
        s_affm.inc(s.copy(scal[0:8, SC_AFF:SC_AFF + 1], pb[5][0:8, 32:33]))

    esC.close()
    esL1b.close()
    esD0 = ExitStack()

    def sbD0(name, shape, dt):
        return esD0.enter_context(nc.sbuf_tensor(name, list(shape), dt))

    logits_tm = sbD0("logits_tm", (128, 16, 8), F32)
    w_tm = sbD0("w_tm", (128, 16, 8), F32)
    tk8a = sbD0("tk8a", (128, 16, 8), F32)
    tk8b = sbD0("tk8b", (128, 16, 8), F32)
    tk8c = sbD0("tk8c", (128, 16, 8), F32)
    tkm_a = sbD0("tkm_a", (128, 16), F32)
    tkm_b = sbD0("tkm_b", (128, 16), F32)
    tkm_c = sbD0("tkm_c", (128, 16), F32)
    srow = sbD0("srow", (1, 16), F32)
    e8c = sbD0("e8c", (8, 4), F32)
    n128 = sbD0("n128", (128, 1), F32)
    ones1f = sbD0("ones1f", (1, 128), F32)
    selflat = sbD0("selflat", (1, 2048), F32)

    esD = ExitStack()

    def sbD(name, shape, dt):
        return esD.enter_context(nc.sbuf_tensor(name, list(shape), dt))

    gw1_sb = sbD("gw1_sb", (128, 8, 1024), BF16)
    gw2_sb = sbD("gw2_sb", (128, 8, 512), BF16)
    gw3_sb = sbD("gw3_sb", (128, 4, 8), BF16)
    g1_sb = sbD("g1_sb", (128, 8, TOK), BF16)
    g2_sb = sbD("g2_sb", (128, 4, TOK), BF16)
    sqg_sb = sbD("sqg_sb", (128, 8, 512), BF16)
    gscr = sbD("gscr", (128, 512), F32)
    gscr2 = sbD("gscr2", (128, 512), F32)
    grows = sbD("grows", (1, 2048), F32)
    lrow = sbD("lrow", (8, 512), F32)

    s_sst = sem("s_sst")
    s_sse = sem("s_sse")
    s_ssb = sem("s_ssb")
    s_xn = sem("s_xn")
    s_gwx = sem("s_gwx")
    s_g1 = sem("s_g1")
    s_g1e = sem("s_g1e")
    s_gsq = sem("s_gsq")
    s_gl = sem("s_gl")
    s_g2 = sem("s_g2")
    s_g2e = sem("s_g2e")
    s_g3 = sem("s_g3")
    s_lgr = sem("s_lgr")
    s_ltt = sem("s_ltt")
    s_ltm = sem("s_ltm")
    s_ns = sem("s_ns")
    s_tk = sem("s_tk")
    s_tke = sem("s_tke")

    @blk.gpsimd
    def _(g):
        s_affm.wait(g, at=2)
        s_mskc.wait(g, at=1)
        s_tempc.wait(g, at=6)
        s_selw.wait(g, at=1)
        for j in range(2):
            s_ysp[j].wait(g)
        for j in range(3):
            s_x2[j].wait(g)
        s_agl.wait(g)
        s_utp.wait(g, at=NOT)
        s_ue.wait(g, at=NOT)
        s_gwx.inc(g.dma_start(
            out=gw1_sb[:, :, :],
            in_=D["gw1"].ap().rearrange("(c p) e -> p c e", p=128)), 16)
        s_gwx.inc(g.dma_start(
            out=gw2_sb[:, :, :],
            in_=D["gw2"].ap().rearrange("(c p) e -> p c e", p=128)), 16)
        s_gwx.inc(g.dma_start(
            out=gw3_sb[:, :, :],
            in_=D["gw3"].ap().rearrange("(c p) e -> p c e", p=128)), 16)
        for snm, ei in (("e0w2", 0), ("e1w3", 0), ("e2w2", 0), ("e0w2", 1),
                        ("e1w3", 1), ("e2w2", 1), ("e0w2", 2), ("e1w3", 2)):
            R = {"e0w2": 4096, "e1w3": 2048, "e2w2": 3072}[snm]
            for ch in range(R // 512):
                s_wcast.inc(g.dma_start(
                    out=D["c_" + snm].ap()[ei][512 * ch:512 * ch + 512, :],
                    in_=D[snm].ap()[ei][512 * ch:512 * ch + 512, :]), 16)

    @blk.vector
    def _(v):
        s_affm.wait(v, at=2)
        v.memset(ones1f[:], 1.0)
        v.tensor_scalar(srow[:, 5:13], usage_sb[:, :], 1.0, FP(1e-8),
                        OP.mult, OP.add)
        v.drain()
        v.tensor_reduce(srow[:, 4:5], srow[:, 5:13], AX.X, OP.min)
        v.tensor_scalar(srow[:, 5:13], srow[:, 5:13], FP(1e-18), None,
                        OP.max)
        v.drain()
        s_ns.inc(v.tensor_scalar(srow[:, 0:1], srow[:, 4:5], 0.0, None,
                                 OP.is_ge))

    @blk.scalar
    def _(s):
        s_ns.wait(s, at=1)
        s_ns.inc(s.activation(srow[:, 5:13], srow[:, 5:13], AF.Ln))

    @blk.vector
    def _(v):
        s_ns.wait(v, at=2)
        v.tensor_tensor(srow[:, 5:13], usage_sb[:, :], srow[:, 5:13], OP.mult)
        v.drain()
        v.tensor_reduce(srow[:, 3:4], srow[:, 5:13], AX.X, OP.add,
                        negate=True)
        v.drain()
        v.tensor_scalar(srow[:, 2:3], srow[:, 3:4], -0.01, 0.02,
                        OP.mult, OP.add)
        v.drain()
        v.tensor_scalar(srow[:, 2:3], srow[:, 2:3], FP(1e-4), None, OP.max)
        v.drain()
        v.tensor_tensor(srow[:, 1:2], srow[:, 2:3], srow[:, 0:1], OP.mult)
        v.drain()
        s_ns.inc(v.tensor_scalar(srow[:, 1:2], srow[:, 1:2], FP(1.0 / 1.5),
                                 None, OP.mult))

    @blk.tensor
    def _(t):
        s_ns.wait(t, at=3)
        t.matmul(pb[4][0:8, 0:1], ones1f[0:1, 0:8], srow[:, 0:1],
                 start=True, stop=True)
        t.matmul(pb[4][0:8, 1:2], ones1f[0:1, 0:8], srow[:, 1:2],
                 start=True, stop=True)
        s_ns.inc(t.matmul(pb[4][0:128, 2:3], ones1f[0:1, :], srow[:, 1:2],
                          start=True, stop=True))

    @blk.scalar
    def _(s):
        s_ns.wait(s, at=4)
        s.copy(e8c[0:8, 0:2], pb[4][0:8, 0:2])
        s_ns.inc(s.copy(n128[:, 0:1], pb[4][0:128, 2:3]))

    @blk.vector
    def _(v):
        s_ns.wait(v, at=5)
        v.tensor_scalar(e8c[0:8, 2:3], gb3_sb[0:8, 0:1], FP(1.0 / 1.5), None,
                        OP.mult)
        v.drain()
        v.scalar_tensor_tensor(e8c[0:8, 2:3], scal[0:8, SC_AFF:SC_AFF + 1],
                               FP(0.3 / 1.5), e8c[0:8, 2:3], OP.mult, OP.add)
        v.drain()
        v.tensor_tensor(e8c[0:8, 2:3], e8c[0:8, 2:3], e8c[0:8, 0:1], OP.mult)
        s_ns.inc(v.tensor_scalar(e8c[0:8, 3:4], e8c[0:8, 0:1], FP(1.0 / 1.5),
                                 None, OP.mult))

    @blk.tensor
    def _(t):
        s_selw.wait(t, at=1)
        s_qtm.wait(t, at=2)
        s_ns.wait(t, at=5)
        s_affm.wait(t, at=2)
        s_tempc.wait(t, at=5)
        s_rmb.wait(t, at=2)
        for ti in range(NOT):
            bank = pb[4 + ti % 2]
            if ti >= 2:
                s_sse.wait(t, at=ti - 1)
                s_xn.wait(t, at=ti - 1)
            s_sst.inc(t.transpose(bank[0:1, 0:128],
                                  scal[:, 32 + ti:33 + ti], identf[:]))
            s_sse.wait(t, at=ti + 1)
            s_ssb.inc(t.matmul(bank[:, 0:128], ones1f[0:1, :],
                               selflat[0:1, 128 * ti:128 * ti + 128],
                               start=True, stop=True))

    @blk.scalar
    def _(s):
        for ti in range(NOT):
            s_sst.wait(s, at=ti + 1)
            s_sse.inc(s.copy(selflat[0:1, 128 * ti:128 * ti + 128],
                             pb[4 + ti % 2][0:1, 0:128]))

    @blk.vector
    def _(v):
        for ti in range(NOT):
            s_ssb.wait(v, at=ti + 1)
            for j in range(8):
                ins = v.tensor_tensor(
                    u_fm[:, j, 128 * ti:128 * (ti + 1)],
                    u_fm[:, j, 128 * ti:128 * (ti + 1)],
                    pb[4 + ti % 2][:, 0:128], OP.mult)
            s_xn.inc(ins)

    @blk.tensor
    def _(t):
        s_gwx.wait(t, at=48)
        s_xn.wait(t, at=NOT)
        for tc in range(4):
            for oc in range(8):
                n = 8 * tc + oc
                bank = pb[n % 4]
                if n >= 4:
                    s_g1e.wait(t, at=n - 3)
                for k in range(8):
                    ins = t.matmul(bank[:], gw1_sb[:, k, 128 * oc:128 * oc + 128],
                                   u_fm[:, k, 512 * tc:512 * tc + 512],
                                   start=(k == 0), stop=(k == 7))
                s_g1.inc(ins)

    @blk.scalar
    def _(s):
        s_gload.wait(s)
        for tc in range(4):
            for oc in range(8):
                n = 8 * tc + oc
                s_g1.wait(s, at=n + 1)
                s_g1e.inc(s.activation(
                    g1_sb[:, oc, 512 * tc:512 * tc + 512], pb[n % 4][:],
                    AF.Silu, bias=biases[:, BOFF["gb1"] + oc:BOFF["gb1"] + oc + 1]))

    @blk.vector
    def _(v):
        s_gload.wait(v)
        for tc in range(4):
            s_gl.wait(v, at=5 * tc + 1)
            v.tensor_scalar(grows[0:1, 0:512], pb[4][0:1, 0:512],
                            FP(1.0 / 1024.0), None, OP.mult)
            v.tensor_scalar(grows[0:1, 512:1024], pb[5][0:1, 0:512],
                            FP(1.0 / 1024.0), None, OP.mult)
            v.drain()
            v.tensor_tensor(grows[0:1, 1024:1536], grows[0:1, 0:512],
                            grows[0:1, 0:512], OP.mult)
            v.drain()
            v.tensor_tensor(grows[0:1, 512:1024], grows[0:1, 512:1024],
                            grows[0:1, 1024:1536], OP.subtract)
            v.drain()
            v.tensor_scalar(grows[0:1, 512:1024], grows[0:1, 512:1024], 1.0,
                            LN_EPS, OP.mult, OP.add)
            v.drain()
            s_gl.inc(v.reciprocal(grows[0:1, 512:1024],
                                  grows[0:1, 512:1024]))
            s_gl.wait(v, at=5 * tc + 4)
            for j in range(8):
                v.tensor_tensor(gscr[:], g1_sb[:, j, 512 * tc:512 * tc + 512],
                                pb[4][:, 0:512], OP.subtract)
                v.drain()
                v.tensor_tensor(gscr2[:], gscr[:], pb[5][:, 0:512], OP.mult)
                v.drain()
                ins = v.tensor_scalar(
                    g1_sb[:, j, 512 * tc:512 * tc + 512], gscr2[:],
                    biases[:, BOFF["lnsc"] + j:BOFF["lnsc"] + j + 1],
                    biases[:, BOFF["lnof"] + j:BOFF["lnof"] + j + 1],
                    OP.mult, OP.add)
                v.drain()
            s_gl.inc(ins)

    @blk.scalar
    def _(s):
        for tc in range(4):
            s_g1e.wait(s, at=8 * (tc + 1))
            if tc >= 1:
                s_gl.wait(s, at=5 * (tc - 1) + 1)
            for j in range(8):
                ins = s.activation(sqg_sb[:, j, :],
                                   g1_sb[:, j, 512 * tc:512 * tc + 512],
                                   AF.Square)
            s_gsq.inc(ins)
            s_gl.wait(s, at=5 * tc + 2)
            s_gl.inc(s.activation(grows[0:1, 512:1024],
                                  grows[0:1, 512:1024], AF.Sqrt))

    @blk.tensor
    def _(t):
        for tc in range(4):
            s_g1e.wait(t, at=8 * (tc + 1))
            s_gsq.wait(t, at=tc + 1)
            if tc >= 1:
                s_gl.wait(t, at=5 * tc)
            for j in range(8):
                t.matmul(pb[4][0:1, 0:512], onesc[:, 0:1],
                         g1_sb[:, j, 512 * tc:512 * tc + 512],
                         start=(j == 0), stop=(j == 7))
            for j in range(8):
                ins = t.matmul(pb[5][0:1, 0:512], onesc[:, 0:1],
                               sqg_sb[:, j, :], start=(j == 0), stop=(j == 7))
            s_gl.inc(ins)
            s_gl.wait(t, at=5 * tc + 3)
            t.matmul(pb[4][:, 0:512], ones1f[0:1, :], grows[0:1, 0:512],
                     start=True, stop=True)
            ins = t.matmul(pb[5][:, 0:512], ones1f[0:1, :],
                           grows[0:1, 512:1024], start=True, stop=True)
            s_gl.inc(ins)

    @blk.tensor
    def _(t):
        for tc in range(4):
            s_gl.wait(t, at=5 * tc + 5)
            for oc in range(4):
                n = 4 * tc + oc
                bank = pb[n % 4]
                if n >= 4:
                    s_g2e.wait(t, at=n - 3)
                for k in range(8):
                    ins = t.matmul(bank[:],
                                   gw2_sb[:, k, 128 * oc:128 * oc + 128],
                                   g1_sb[:, k, 512 * tc:512 * tc + 512],
                                   start=(k == 0), stop=(k == 7))
                s_g2.inc(ins)

    @blk.scalar
    def _(s):
        for tc in range(4):
            for oc in range(4):
                n = 4 * tc + oc
                s_g2.wait(s, at=n + 1)
                s_g2e.inc(s.activation(
                    g2_sb[:, oc, 512 * tc:512 * tc + 512], pb[n % 4][:],
                    AF.Silu,
                    bias=biases[:, BOFF["gb2"] + oc:BOFF["gb2"] + oc + 1]))

    @blk.tensor
    def _(t):
        s_ns.wait(t, at=6)
        s_g2e.wait(t, at=16)
        for tc in range(4):
            bank = pb[tc % 2]
            if tc >= 2:
                s_lgr.wait(t, at=tc - 1)
            for k in range(4):
                ins = t.matmul(bank[0:8, 0:512], gw3_sb[:, k, :],
                               g2_sb[:, k, 512 * tc:512 * tc + 512],
                               start=(k == 0), stop=(k == 3))
            s_g3.inc(ins)
            s_lgr.wait(t, at=tc + 1)
            for q in range(4):
                ti = 4 * tc + q
                if ti >= 2:
                    s_ltm.wait(t, at=ti - 1)
                s_ltt.inc(t.transpose(
                    pb[2 + ti % 2][0:128, 0:8],
                    lrow[0:8, 128 * q:128 * q + 128], identf[0:8, 0:8]))

    @blk.vector
    def _(v):
        for tc in range(4):
            s_g3.wait(v, at=tc + 1)
            s_lgr.inc(v.tensor_scalar(lrow[0:8, :], pb[tc % 2][0:8, 0:512],
                                      e8c[0:8, 3:4], e8c[0:8, 2:3],
                                      OP.mult, OP.add))
            for q in range(4):
                ti = 4 * tc + q
                s_ltt.wait(v, at=ti + 1)
                s_ltm.inc(v.scalar_tensor_tensor(
                    logits_tm[:, ti, :], noise_sb[:, ti, :], n128[:, 0:1],
                    pb[2 + ti % 2][0:128, 0:8], OP.mult, OP.add))

    @blk.vector
    def _(v):
        s_ltm.wait(v, at=NOT)
        L3 = logits_tm
        v.drain()
        v.tensor_reduce(tkm_a[:], L3[:, :, :], AX.X, OP.max)
        v.drain()
        for e in range(8):
            v.tensor_tensor(tk8a[:, :, e:e + 1], L3[:, :, e:e + 1],
                            tkm_a[:], OP.is_equal)
        v.drain()
        for e in range(8):
            v.tensor_scalar(tk8b[:, :, e:e + 1], tk8a[:, :, e:e + 1],
                            float(100 - e), None, OP.mult)
        v.drain()
        for e in range(8):
            v.tensor_scalar(tk8b[:, :, e:e + 1], tk8b[:, :, e:e + 1],
                            -1.0, 100.0, OP.mult, OP.add)
        v.drain()
        v.tensor_reduce(tkm_b[:], tk8b[:, :, :], AX.X, OP.min)
        v.drain()
        for e in range(8):
            v.tensor_scalar(tk8a[:, :, e:e + 1], tkm_b[:], float(e), None,
                            OP.is_equal)
        v.drain()
        for e in range(8):
            v.scalar_tensor_tensor(tk8b[:, :, e:e + 1], tk8a[:, :, e:e + 1],
                                   -3e38, L3[:, :, e:e + 1],
                                   OP.mult, OP.add)
        v.drain()
        v.tensor_reduce(tkm_c[:], tk8b[:, :, :], AX.X, OP.max)
        v.drain()
        for e in range(8):
            v.tensor_tensor(tk8b[:, :, e:e + 1], tk8b[:, :, e:e + 1],
                            tkm_c[:], OP.is_equal)
        v.drain()
        for e in range(8):
            v.tensor_scalar(tk8b[:, :, e:e + 1], tk8b[:, :, e:e + 1],
                            float(100 - e), None, OP.mult)
        v.drain()
        for e in range(8):
            v.tensor_scalar(tk8b[:, :, e:e + 1], tk8b[:, :, e:e + 1],
                            -1.0, 100.0, OP.mult, OP.add)
        v.drain()
        v.tensor_reduce(tkm_c[:], tk8b[:, :, :], AX.X, OP.min)
        v.drain()
        for e in range(8):
            v.tensor_scalar(tk8b[:, :, e:e + 1], tkm_c[:], float(e), None,
                            OP.is_equal)
        v.drain()
        for e in range(8):
            v.tensor_tensor(tk8c[:, :, e:e + 1], tk8a[:, :, e:e + 1],
                            tk8b[:, :, e:e + 1], OP.add)
        for e in range(8):
            v.tensor_tensor(tk8b[:, :, e:e + 1], tk8b[:, :, e:e + 1],
                            L3[:, :, e:e + 1], OP.mult)
        v.drain()
        v.tensor_reduce(tkm_c[:], tk8b[:, :, :], AX.X, OP.add)
        v.drain()
        for e in range(8):
            v.tensor_tensor(tk8a[:, :, e:e + 1], L3[:, :, e:e + 1],
                            tkm_c[:], OP.is_ge)
        v.drain()
        for e in range(8):
            v.tensor_tensor(tk8b[:, :, e:e + 1], tk8a[:, :, e:e + 1],
                            L3[:, :, e:e + 1], OP.mult)
        for e in range(8):
            v.tensor_scalar(tk8a[:, :, e:e + 1], tk8a[:, :, e:e + 1],
                            FP(1e9), FP(-1e9), OP.mult, OP.add)
        v.drain()
        for e in range(8):
            v.tensor_tensor(tk8b[:, :, e:e + 1], tk8b[:, :, e:e + 1],
                            tk8a[:, :, e:e + 1], OP.add)
        v.drain()
        v.tensor_reduce(tkm_a[:], tk8b[:, :, :], AX.X, OP.max)
        v.drain()
        for e in range(8):
            ins = v.tensor_tensor(tk8b[:, :, e:e + 1], tk8b[:, :, e:e + 1],
                                  tkm_a[:], OP.subtract)
        s_tk.inc(ins)
        s_tke.wait(v, at=1)
        v.tensor_reduce(tkm_b[:], tk8a[:, :, :], AX.X, OP.add)
        v.drain()
        v.reciprocal(tkm_b[:], tkm_b[:])
        v.drain()
        for e in range(8):
            v.tensor_tensor(tk8a[:, :, e:e + 1], tk8a[:, :, e:e + 1],
                            tkm_b[:], OP.mult)
        v.drain()
        for e in range(8):
            ins = v.tensor_tensor(w_tm[:, :, e:e + 1], tk8a[:, :, e:e + 1],
                                  tk8c[:, :, e:e + 1], OP.mult)
        s_tk.inc(ins)

    @blk.scalar
    def _(s):
        s_tk.wait(s, at=1)
        s_tke.inc(s.activation(tk8a[:, :, :], tk8b[:, :, :], AF.Exp))

    if "gate" in debug:
        @blk.sync
        def _(sy):
            s_ltm.wait(sy)
            s_dbg.inc(sy.dma_start(out=dbg["d_logits"].ap(),
                                   in_=logits_tm[:]), 16)
            s_tk.wait(sy)
            s_dbg.inc(sy.dma_start(out=dbg["d_w"].ap(), in_=w_tm[:]), 16)
            s_dbg.inc(sy.dma_start(out=dbg["d_sel2"].ap(), in_=tk8c[:]), 16)
            s_dbg.inc(sy.dma_start(out=dbg["d_i1"].ap(), in_=tkm_b[:]), 16)
            s_dbg.inc(sy.dma_start(out=dbg["d_i2"].ap(), in_=tkm_c[:]), 16)
            s_xn.wait(sy)
            s_dbg.inc(sy.dma_start(out=dbg["d_xnorm"].ap(), in_=u_fm[:]), 16)

    esD.close()
    NSLOT = 256
    esE0 = ExitStack()

    def sbE0(name, shape, dt):
        return esE0.enter_context(nc.sbuf_tensor(name, list(shape), dt))

    grankm = sbE0("grankm", (128, 16), F32)
    ecst = sbE0("ecst", (128, 8, 8), F32)
    b2mx = sbE0("b2mx", (128, 8, 8), F32)
    econT = sbE0("econT", (8, 1024), F32)
    b2T = sbE0("b2T", (8, 1024), F32)
    wsl_tm = sbE0("wsl_tm", (128, 2, 8), F32)

    esEa = ExitStack()

    def sbEa(name, shape, dt):
        return esEa.enter_context(nc.sbuf_tensor(name, list(shape), dt))

    x_tm = sbEa("x_tm", (128, 16, 1024), BF16)
    m_bf = sbEa("m_bf", (128, 16), BF16)
    m_f = sbEa("m_f", (128, 16), F32)
    cnt8p = sbEa("cnt8p", (128, 8), F32)
    cnt8b = sbEa("cnt8b", (128, 8), BF16)
    cntc_f = sbEa("cntc_f", (8, 1), F32)
    cntrow = sbEa("cntrow", (1, 8), F32)
    totc = sbEa("totc", (16, 1), BF16)
    prefc = sbEa("prefc", (16, 1), F32)
    tioff = sbEa("tioff", (1, 16), F32)
    pre_sb = sbEa("pre_sb", (128, 16), F32)
    coff = sbEa("coff", (128, 1), F32)
    gsh8 = sbEa("gsh8", (128, 16), F32)
    gscr8 = sbEa("gscr8", (128, 16), F32)
    pbuf_b = [sbEa(f"pbuf_b{j}", (128, NSLOT), BF16) for j in range(2)]
    pbuf_f = [sbEa(f"pbuf_f{j}", (128, NSLOT), F32) for j in range(2)]
    partial = sbEa("partial", (128, 16, 1024), BF16)
    wpart = sbEa("wpart", (8, 8, NSLOT), F32)

    s_sp = SemAlias(s_bhi)
    s_xtt = SemAlias(s_sst)
    s_xte = SemAlias(s_sse)
    s_pb = SemAlias(s_bv[0])
    s_pc = SemAlias(s_bv[1])
    s_gev = SemAlias(s_bt[1])
    s_rsx = SemAlias(s_cload)
    s_rsd = sem("s_rsd")
    s_cc2 = sem("s_cc2")
    s_gxt = SemAlias(s_g3)
    s_ew1 = s_wq + [sem("s_ew1c"), sem("s_ew1d")]
    s_ew2 = s_wk + [sem("s_ew2c")]
    EW1B = [sm.n for sm in s_ew1]
    EW2B = [sm.n for sm in s_ew2]
    s_hx = sem("s_hx")
    s_he = SemAlias(s_lgr)
    s_h2v = SemAlias(s_ltt)
    s_whk = sem("s_whk")
    s_oc = SemAlias(s_tke)
    s_ecn = SemAlias(s_sq2)
    s_agd = sem("s_agd")
    s_rw = SemAlias(s_yv)
    s_scm = SemAlias(s_utp)
    s_y0m = SemAlias(s_ue)
    s_y0e = SemAlias(s_ave)
    s_qrw = SemAlias(s_op)
    s_acv = SemAlias(s_ev1)
    s_ftr = SemAlias(s_mskc)
    s_fot = SemAlias(s_impw)
    s_yr2 = s_ysp
    YR2B = [sm.n for sm in s_yr2]
    s_out2 = [sem(f"s_out2_{j}") for j in range(2)]

    ESP = {}
    for k in range(8):
        fam, ei = FAM[k]
        if fam == "e0":
            ESP[k] = dict(H=4096, w1=("c_e0w1", ei), b1=BOFF["e0b1"] + 32 * ei,
                          act=AF.Gelu_apprx_tanh, w2=("c_e0w2", ei),
                          b2=BOFF["e0b2"] + 8 * ei, mid=None, h2base=0,
                          nhk=32, e2=False)
        elif fam == "e1":
            ESP[k] = dict(H=2048, w1=("c_e1w1", ei), b1=BOFF["e1b1"] + 16 * ei,
                          act=AF.Silu, mid=("c_e1w2", ei),
                          bmid=BOFF["e1b2"] + 16 * ei, w2=("c_e1w3", ei),
                          b2=BOFF["e1b3"] + 8 * ei, h2base=16, nhk=16,
                          e2=False)
        else:
            ESP[k] = dict(H=3072, w1=("c_e2w1", ei), b1=BOFF["e2b1"] + 24 * ei,
                          act=None, w2=("c_e2w2", ei),
                          b2=BOFF["e2b2"] + 8 * ei, mid=None, h2base=0,
                          nhk=24, e2=True)

    A_SETS = []
    for k in range(8):
        sp = ESP[k]
        for q in range(sp["H"] // 1024):
            A_SETS.append((k, "q", q))
        if sp["mid"] is not None:
            for mp in range(4):
                A_SETS.append((k, "m", mp))
    NA = len(A_SETS)
    ECN_BASE = {}
    for k in range(8):
        ECN_BASE[k] = 3 * k
    ECN_TOT = 24
    NB_END = {}
    _nbe = 0
    for k in range(8):
        _nbe += ESP[k]["nhk"]
        NB_END[k] = _nbe
    HX_AT = []
    _hxl = 0
    for k in range(8):
        sp = ESP[k]
        for q in range(sp["H"] // 1024):
            _hxl += 8
            HX_AT.append(_hxl)
        if sp["mid"] is not None:
            for hh in range(2):
                _hxl += 8
                HX_AT.append(_hxl)
                HX_AT.append(_hxl)

    @blk.vector
    def _(v):
        s_tk.wait(v, at=2)
        s_xn.wait(v, at=NOT)
        v.tensor_scalar(m_bf[:], xchg[:, 0:16], scal[:, SC_THR:SC_THR + 1],
                        None, OP.is_gt)
        v.tensor_scalar(m_f[:], xchg[:, 0:16], scal[:, SC_THR:SC_THR + 1],
                        None, OP.is_gt)
        for r in range(8):
            v.tensor_scalar(gscr8[:], gath[:, r, 0:16],
                            scal[:, SC_THR:SC_THR + 1], None, OP.is_gt,
                            OP.add, accum_out=cnt8p[:, r:r + 1])
            v.drain()
        s_sp.inc(v.tensor_copy(cnt8b[:], cnt8p[:]))
        s_sp.wait(v, at=8)
        v.tensor_tensor(gscr8[:, 0:8], pb[2][:, 0:8], rltm_sb[:], OP.mult)
        v.drain()
        v.tensor_reduce(coff[:], gscr8[:, 0:8], AX.X, OP.add)
        v.drain()
        v.tensor_tensor(grankm[:], pre_sb[:], pb[4][:, 0:16], OP.add)
        v.drain()
        v.tensor_scalar(grankm[:], grankm[:], coff[:], None, OP.add)
        v.drain()
        v.tensor_tensor(grankm[:], grankm[:], m_f[:], OP.mult)
        v.tensor_scalar(gscr8[:], m_f[:], 1.0, -1.0, OP.mult, OP.add)
        v.drain()
        ins = v.tensor_tensor(grankm[:], grankm[:], gscr8[:], OP.add)
        s_sp.inc(ins)

    @blk.tensor
    def _(t):
        s_tk.wait(t, at=2)
        s_gl.wait(t, at=20)
        s_sp.wait(t, at=1)
        t.matmul(pb[0][:, 0:16], ltri_b[:], m_bf[:], start=True, stop=True)
        t.matmul(pb[1][0:16, 0:1], m_bf[:], onesc[:, 0:1], start=True,
                 stop=True)
        ins = t.matmul(pb[1][0:8, 2:3], cnt8b[:], onesc[:, 0:1], start=True,
                       stop=True)
        s_sp.inc(ins)
        s_sp.wait(t, at=3)
        t.matmul(pb[3][0:16, 0:1], ltri_b[0:16, 0:16], totc[:, 0:1],
                 start=True, stop=True)
        ins = t.transpose(pb[3][0:1, 4:12], cntc_f[:, 0:1],
                          identf[0:8, 0:8])
        s_sp.inc(ins)
        s_sp.wait(t, at=5)
        t.transpose(pb[3][0:1, 16:32], prefc[:, 0:1], identf[0:16, 0:16])
        ins = t.matmul(pb[2][:, 0:8], ones1f[0:1, :], cntrow[0:1, :],
                       start=True, stop=True)
        s_sp.inc(ins)
        s_sp.wait(t, at=7)
        ins = t.matmul(pb[4][:, 0:16], ones1f[0:1, :], tioff[0:1, :],
                       start=True, stop=True)
        s_sp.inc(ins)

    @blk.scalar
    def _(s):
        s_sp.wait(s, at=2)
        s.copy(pre_sb[:], pb[0][:, 0:16])
        s.copy(totc[:, 0:1], pb[1][0:16, 0:1])
        ins = s.copy(cntc_f[:, 0:1], pb[1][0:8, 2:3])
        s_sp.inc(ins)
        s_sp.wait(s, at=4)
        s.copy(prefc[:, 0:1], pb[3][0:16, 0:1])
        ins = s.copy(cntrow[0:1, :], pb[3][0:1, 4:12])
        s_sp.inc(ins)
        s_sp.wait(s, at=6)
        ins = s.copy(tioff[0:1, :], pb[3][0:1, 16:32])
        s_sp.inc(ins)

    @blk.tensor
    def _(t):
        s_xn.wait(t, at=NOT)
        s_tk.wait(t, at=2)
        for ti in range(NOT):
            ptbank = pt0 if ti % 2 == 0 else pt1
            if ti >= 2:
                s_xte.wait(t, at=ti - 1)
            for k in range(8):
                ins = t.transpose(ptbank[:, 128 * k:128 * k + 128],
                                  u_fm[:, k, 128 * ti:128 * ti + 128],
                                  identb[:])
            s_xtt.inc(ins)

    @blk.scalar
    def _(s):
        for ti in range(NOT):
            ptbank = pt0 if ti % 2 == 0 else pt1
            s_xtt.wait(s, at=ti + 1)
            s_xte.inc(s.copy(x_tm[:, ti, :], ptbank[:, 0:1024]))

    @blk.vector
    def _(v):
        s_sp.wait(v, at=9)
        n = 0
        for g8 in range(8):
            v.drain()
            v.tensor_scalar(gsh8[:], grankm[:], float(-g8), None, OP.add)
            v.drain()
            for ti in range(NOT):
                j = n % 2
                if n >= 2:
                    s_pc.wait(v, at=n - 1)
                v.tensor_scalar(pbuf_b[j][:], r2560[:], gsh8[:, ti:ti + 1],
                                None, OP.is_equal)
                v.drain()
                ins = v.tensor_copy(pbuf_f[j][:], pbuf_b[j][:])
                s_pb.inc(ins)
                n += 1

    @blk.tensor
    def _(t):
        s_xte.wait(t, at=NOT)
        n = 0
        for g8 in range(8):
            if g8 >= 1:
                s_gev.wait(t, at=5 * g8)
            for ti in range(NOT):
                j = n % 2
                s_pb.wait(t, at=n + 1)
                t.matmul(pb[4][0:8, 0:256], w_tm[:, ti, :], pbuf_f[j][:],
                         start=(ti == 0), stop=(ti == NOT - 1))
                for stl in range(2):
                    for fb in range(2):
                        ins = t.matmul(
                            pb[2 * stl + fb][:],
                            pbuf_b[j][:, 128 * stl:128 * stl + 128],
                            x_tm[:, ti, 512 * fb:512 * fb + 512],
                            start=(ti == 0), stop=(ti == NOT - 1))
                s_pc.inc(ins)
                n += 1

    @blk.scalar
    def _(s):
        for g8 in range(8):
            s_pc.wait(s, at=16 * (g8 + 1))
            for stl in range(2):
                for fb in range(2):
                    s_gev.inc(s.copy(
                        partial[:, 2 * g8 + stl, 512 * fb:512 * fb + 512],
                        pb[2 * stl + fb][:]))
            s_gev.inc(s.copy(wpart[:, g8, :], pb[4][0:8, 0:256]))

    @blk.sync
    def _(sy):
        for g8 in range(8):
            s_gev.wait(sy, at=5 * (g8 + 1))
            s_rsx.inc(sy.dma_start(
                out=D["rsin"].ap()[g8],
                in_=partial[:, 2 * g8:2 * g8 + 2, :]), 16)
        s_rsx.inc(sy.dma_start(
            out=D["wrsin"].ap().rearrange("c e j -> e c j"),
            in_=wpart[:, :, :]), 16)

    esEa.close()
    esEb = ExitStack()

    def sbEb(name, shape, dt):
        return esEb.enter_context(nc.sbuf_tensor(name, list(shape), dt))

    gx_tm = sbEb("gx_tm", (128, 2, 1024), BF16)
    gxf = sbEb("gxf", (128, 8, NSLOT), BF16)
    wsl_raw = sbEb("wsl_raw", (8, NSLOT), F32)
    h_sb = sbEb("h_sb", (128, 32, NSLOT), BF16)
    bscr = [sbEb(f"bscr{j}", (128, NSLOT), BF16) for j in range(4)]
    wq1 = [sbEb(f"wq1_{j}", (128, 8, 1024), BF16) for j in range(4)]
    wq2 = [sbEb(f"wq2_{j}", (128, 1024), BF16) for j in range(3)]
    comb = sbEb("comb", (128, 2, 1024), F32)
    comb_b = sbEb("comb_b", (128, 2, 1024), BF16)
    erow = sbEb("erow", (1, 1024), F32)

    @blk.gpsimd
    def _(g):
        s_rsx.wait(g, at=16 * 9)
        g.collective_compute(
            "ReduceScatter", OP.add, replica_groups=[list(range(8))],
            ins=[D["rsin"].ap()], outs=[D["rsout"].ap()],
        ).then_inc(s_cc2.h, 1)
        s_cc2.n += 1
        g.collective_compute(
            "ReduceScatter", OP.add, replica_groups=[list(range(8))],
            ins=[D["wrsin"].ap()], outs=[D["wrsout"].ap()],
        ).then_inc(s_cc2.h, 1)
        s_cc2.n += 1
        g.wait_ge(s_cc2.h, 2)
        s_rsd.inc(g.dma_start(out=gx_tm[:, :, :], in_=D["rsout"].ap()), 16)
        s_rsd.inc(g.dma_start(out=wsl_raw[:, :], in_=D["wrsout"].ap()), 16)

    @blk.tensor
    def _(t):
        s_rsd.wait(t, at=32)
        for sl in range(2):
            ptbank = pt0 if sl == 0 else pt1
            for k in range(8):
                ins = t.transpose(ptbank[:, 128 * k:128 * k + 128],
                                  gx_tm[:, sl, 128 * k:128 * k + 128],
                                  identb[:])
            s_gxt.inc(ins)
        s_gxt.wait(t, at=4)
        t.transpose(pb[5][:, 0:8], wsl_raw[0:8, 0:128], identf[0:8, 0:8])
        ins = t.transpose(pb[5][:, 8:16], wsl_raw[0:8, 128:256],
                          identf[0:8, 0:8])
        s_gxt.inc(ins)

    @blk.scalar
    def _(s):
        for sl in range(2):
            s_gxt.wait(s, at=2 * sl + 1)
            ptbank = pt0 if sl == 0 else pt1
            for k in range(8):
                ins = s.copy(gxf[:, k, 128 * sl:128 * sl + 128],
                             ptbank[:, 128 * k:128 * k + 128])
            s_gxt.inc(ins)
        s_gxt.wait(s, at=5)
        s.copy(wsl_tm[:, 0, :], pb[5][:, 0:8])
        ins = s.copy(wsl_tm[:, 1, :], pb[5][:, 8:16])
        s_gxt.inc(ins)

    @blk.gpsimd
    def _(g):
        s_wcast.wait(g)
        s_ltm.wait(g, at=NOT)
        s_tk.wait(g, at=2)
        s_xn.wait(g, at=NOT)
        s_gl.wait(g, at=20)
        s_g2e.wait(g, at=16)
        s_g1e.wait(g, at=32)
        n = 0
        nb = 0
        for k in range(8):
            sp = ESP[k]
            nsets = sp["H"] // 1024 + (4 if sp["mid"] is not None else 0)
            for si in range(nsets):
                kind = "q" if si < sp["H"] // 1024 else "m"
                idx = si if kind == "q" else si - sp["H"] // 1024
                j = n % 4
                if n >= 4:
                    g.wait_ge(s_hx.h, HX_AT[n - 4])
                if kind == "q":
                    nm, ei = sp["w1"]
                    srcw = D[nm].ap()[ei][:, 1024 * idx:1024 * idx + 1024]
                else:
                    nm, ei = sp["mid"]
                    hh, prt = idx // 2, idx % 2
                    srcw = D[nm].ap()[ei][1024 * prt:1024 * prt + 1024,
                                          1024 * hh:1024 * hh + 1024]
                s_ew1[j].inc(g.dma_start(
                    out=wq1[j][:, 0:8, :],
                    in_=srcw.rearrange("(c p) f -> p c f", p=128)), 16)
                n += 1
            nm, ei = sp["w2"]
            for hk in range(sp["nhk"]):
                j = nb % 3
                if nb >= 3:
                    g.wait_ge(s_whk.h, nb - 2)
                srcw = D[nm].ap()[ei][128 * hk:128 * hk + 128, :]
                s_ew2[j].inc(g.dma_start(out=wq2[j][:, :], in_=srcw), 16)
                nb += 1

    @blk.tensor
    def _(t):
        s_gxt.wait(t, at=6)
        n = 0
        nb = 0
        hx = 0
        h2v = 0
        for k in range(8):
            sp = ESP[k]
            eb = ECN_BASE[k]
            nq = sp["H"] // 1024
            for q in range(nq):
                j = n % 4
                s_ew1[j].wait(t, at=EW1B[j] + 16 * (n // 4 + 1))
                if k >= 1 and q == 0:
                    s_oc.wait(t, at=4 * k)
                    s_ecn.wait(t, at=eb)
                for hc in range(8):
                    reg = pb[hc % 4][:, 256 * (hc // 4):256 * (hc // 4)
                                     + 256]
                    if hx >= 4:
                        s_he.wait(t, at=hx - 3)
                    for kk in range(8):
                        ins = t.matmul(
                            reg, wq1[j][:, kk, 128 * hc:128 * hc + 128],
                            gxf[:, kk, :], start=(kk == 0), stop=(kk == 7))
                    s_hx.inc(ins)
                    hx += 1
                    if sp["e2"]:
                        h2v += 1
                n += 1
            if sp["mid"] is not None:
                s_he.wait(t, at=hx)
                for hh in range(2):
                    n0 = n
                    for jp in range(2):
                        jj = (n0 + jp) % 4
                        s_ew1[jj].wait(
                            t, at=EW1B[jj] + 16 * ((n0 + jp) // 4 + 1))
                    for hc in range(8):
                        reg = pb[hc % 4][:, 256 * (hc // 4):
                                         256 * (hc // 4) + 256]
                        if hx >= 4:
                            s_he.wait(t, at=hx - 3)
                        for h1c in range(16):
                            jj = (n0 + h1c // 8) % 4
                            ins = t.matmul(
                                reg,
                                wq1[jj][:, h1c % 8, 128 * hc:128 * hc + 128],
                                h_sb[:, h1c, :], start=(h1c == 0),
                                stop=(h1c == 15))
                        s_hx.inc(ins)
                        hx += 1
                    n += 2
            s_he.wait(t, at=hx)
            if sp["e2"]:
                s_h2v.wait(t, at=h2v)
            for hk in range(sp["nhk"]):
                j = nb % 3
                s_ew2[j].wait(t, at=EW2B[j] + 16 * (nb // 3 + 1))
                if k >= 1 and hk == 0:
                    s_oc.wait(t, at=4 * k)
                for stl in range(2):
                    for fb in range(2):
                        ins = t.matmul(
                            pb[2 * stl + fb][:],
                            h_sb[:, sp["h2base"] + hk,
                                 128 * stl:128 * stl + 128],
                            wq2[j][:, 512 * fb:512 * fb + 512],
                            start=(hk == 0), stop=(hk == sp["nhk"] - 1))
                s_whk.inc(ins)
                nb += 1
            s_ecn.wait(t, at=eb + 1)
            for fc in range(8):
                ins = t.transpose(pb[4][:, fc:fc + 1],
                                  erow[0:1, 128 * fc:128 * fc + 128],
                                  identf[0:1, 0:1])
            s_ecn.inc(ins)

    @blk.scalar
    def _(s):
        s_gxt.wait(s, at=6)
        hx = 0
        h2v = 0
        for k in range(8):
            sp = ESP[k]
            eb = ECN_BASE[k]
            nq = sp["H"] // 1024
            for q in range(nq):
                for hc in range(8):
                    reg = pb[hc % 4][:, 256 * (hc // 4):256 * (hc // 4)
                                     + 256]
                    s_hx.wait(s, at=hx + 1)
                    hr = 8 * q + hc
                    bias = biases[:, sp["b1"] + hr:sp["b1"] + hr + 1]
                    if sp["e2"]:
                        if h2v >= 2:
                            s_h2v.wait(s, at=h2v - 1)
                        s.activation(bscr[2 * (h2v % 2)][:], reg, AF.Silu,
                                     bias=bias)
                        s_he.inc(s.activation(bscr[2 * (h2v % 2) + 1][:],
                                              reg, AF.Sigmoid, bias=bias))
                        h2v += 1
                    else:
                        s_he.inc(s.activation(h_sb[:, hr, :], reg,
                                              sp["act"], bias=bias))
                    hx += 1
            if sp["mid"] is not None:
                for hh in range(2):
                    for hc in range(8):
                        reg = pb[hc % 4][:, 256 * (hc // 4):
                                         256 * (hc // 4) + 256]
                        s_hx.wait(s, at=hx + 1)
                        hr = 8 * hh + hc
                        bias = biases[:, sp["bmid"] + hr:sp["bmid"] + hr + 1]
                        s_he.inc(s.activation(h_sb[:, 16 + hr, :], reg,
                                              AF.Silu, bias=bias))
                        hx += 1
            s_whk.wait(s, at=NB_END[k])
            s.copy(erow[0:1, 0:512], pb[2][64:65, 0:512])
            ins = s.copy(erow[0:1, 512:1024], pb[3][64:65, 0:512])
            s_ecn.inc(ins)
            s_ecn.wait(s, at=eb + 2)
            for fc in range(8):
                s.copy(ecst[:, fc, k:k + 1], pb[4][:, fc:fc + 1])
            for fc in range(8):
                ins = s.copy(b2mx[:, fc, k:k + 1],
                             biases[:, sp["b2"] + fc:sp["b2"] + fc + 1])
            s_ecn.inc(ins)

    @blk.vector
    def _(v):
        s_gxt.wait(v, at=6)
        v.memset(comb[:], 0.0)
        v.drain()
        hx = 0
        h2v = 0
        ox = 0
        for k in range(8):
            sp = ESP[k]
            nq = sp["H"] // 1024
            for q in range(nq):
                for hc in range(8):
                    if sp["e2"]:
                        s_he.wait(v, at=hx + 1)
                        hr = 8 * q + hc
                        s_h2v.inc(v.tensor_tensor(
                            h_sb[:, hr, :], bscr[2 * (h2v % 2)][:],
                            bscr[2 * (h2v % 2) + 1][:], OP.mult))
                        h2v += 1
                    hx += 1
            if sp["mid"] is not None:
                hx += 16
            s_whk.wait(v, at=NB_END[k])
            for stl in range(2):
                for fb in range(2):
                    ins = v.scalar_tensor_tensor(
                        comb[:, stl, 512 * fb:512 * fb + 512],
                        pb[2 * stl + fb][:], wsl_tm[:, stl, k:k + 1],
                        comb[:, stl, 512 * fb:512 * fb + 512],
                        OP.mult, OP.add)
                    s_oc.inc(ins)
                    ox += 1
        v.drain()
        ins = v.tensor_copy(comb_b[:], comb[:])
        s_oc.inc(ins)

    @blk.sync
    def _(sy):
        s_oc.wait(sy, at=33)
        s_rsx.inc(sy.dma_start(
            out=D["agin2"].ap().rearrange("s p f -> p s f"),
            in_=comb_b[:, :, :]), 16)

    esEb.close()
    esEc = ExitStack()

    def sbEc(name, shape, dt):
        return esEc.enter_context(nc.sbuf_tensor(name, list(shape), dt))

    agx = sbEc("agx", (128, 16, 1024), BF16)
    grow = sbEc("grow", (1, 2048), F32)
    grbcq = sbEc("grbcq", (128, 512), F32)
    w_em = sbEc("w_em", (8, 2048), F32)
    wns_scr = sbEc("wns_scr", (8, 512), F32)
    nmem_scr = sbEc("nmem_scr", (8, 512), F32)
    pth = sbEc("pth", (128, 16, 512), BF16)
    yscrE = sbEc("yscrE", (128, 512), F32)
    acc = sbEc("acc", (128, 8, 512), F32)
    yrld2 = [sbEc(f"yrld2_{j}", (128, EMB), F32) for j in range(2)]
    out_sb = [sbEc(f"out_sb{j}", (128, EMB), F32) for j in range(2)]

    @blk.gpsimd
    def _(g):
        s_rsx.wait(g, at=16 * 10)
        g.collective_compute(
            "AllGather", OP.bypass, replica_groups=[list(range(8))],
            ins=[D["agin2"].ap()], outs=[D["agout2"].ap()],
        ).then_inc(s_cc2.h, 1)
        s_cc2.n += 1
        g.wait_ge(s_cc2.h, 3)
        s_agd.inc(g.dma_start(
            out=agx[:, :, :],
            in_=D["agout2"].ap().rearrange("c s p f -> p (c s) f")), 16)

    @blk.tensor
    def _(t):
        s_oc.wait(t, at=33)
        for ti in range(NOT):
            if ti >= 1:
                s_rw.wait(t, at=2 * ti)
            t.transpose(pb[0][0:1, 0:128], grankm[:, ti:ti + 1], identf[:])
            ins = t.transpose(pb[0][0:8, 128:256], w_tm[:, ti, :],
                              identf[:])
            s_rw.inc(ins)
        s_ecn.wait(t, at=ECN_TOT)
        for fc in range(8):
            s_rw.wait(t, at=32 + 2 * fc)
            t.transpose(pb[2][0:8, 0:128], ecst[:, fc, :], identf[:])
            ins = t.transpose(pb[3][0:8, 0:128], b2mx[:, fc, :], identf[:])
            s_rw.inc(ins)

    @blk.scalar
    def _(s):
        for ti in range(NOT):
            s_rw.wait(s, at=2 * ti + 1)
            s.copy(grow[0:1, 128 * ti:128 * ti + 128], pb[0][0:1, 0:128])
            ins = s.copy(w_em[0:8, 128 * ti:128 * ti + 128],
                         pb[0][0:8, 128:256])
            s_rw.inc(ins)
        for fc in range(8):
            s_rw.wait(s, at=33 + 2 * fc)
            s.copy(econT[0:8, 128 * fc:128 * fc + 128], pb[2][0:8, 0:128])
            ins = s.copy(b2T[0:8, 128 * fc:128 * fc + 128],
                         pb[3][0:8, 0:128])
            s_rw.inc(ins)


    @blk.vector
    def _(v):
        for qr in range(4):
            s_qrw.wait(v, at=3 * qr + 2)
            if qr >= 1:
                s_scm.wait(v, at=8 * qr)
            for st in range(16):
                v.tensor_scalar(pth[:, st, :], grbcq[:, :],
                                gc16_sb[:, st:st + 1], None, OP.is_equal)
            v.tensor_scalar(nmem_scr[0:8, :], grbcq[0:8, :], 0.0, None,
                            OP.is_lt)
            v.drain()
            ins = v.tensor_tensor(wns_scr[0:8, :],
                                  w_em[0:8, 512 * qr:512 * qr + 512],
                                  nmem_scr[0:8, :], OP.mult)
            s_qrw.inc(ins)
            for fc in range(8):
                u = 8 * qr + fc
                s_scm.wait(v, at=u + 1)
                s_y0e.wait(v, at=u + 1)
                v.tensor_tensor(acc[:, fc, :], pb[fc % 2][:], yscrE[:],
                                OP.add)
                v.drain()
                ins = v.scalar_tensor_tensor(
                    acc[:, fc, :], u_fm[:, fc, 512 * qr:512 * qr + 512],
                    0.1, acc[:, fc, :], OP.mult, OP.add)
                s_acv.inc(ins)
            for tj in range(4):
                ti = 4 * qr + tj
                s_ftr.wait(v, at=ti + 1)
                s_yr2[ti % 2].wait(v, at=YR2B[ti % 2] + 16 * (ti // 2 + 1))
                if ti >= 2:
                    s_out2[ti % 2].wait(v, at=16 * ((ti - 2) // 2 + 1))
                v.tensor_tensor(out_sb[ti % 2][:, 0:512],
                                yrld2[ti % 2][:, 0:512], pb[4][:], OP.add)
                ins = v.tensor_tensor(out_sb[ti % 2][:, 512:1024],
                                      yrld2[ti % 2][:, 512:1024], pb[5][:],
                                      OP.add)
                s_fot.inc(ins)

    @blk.tensor
    def _(t):
        s_agd.wait(t, at=16)
        s_rw.wait(t, at=48)
        for qr in range(4):
            if qr >= 1:
                s_acv.wait(t, at=8 * qr)
            ins = t.matmul(pb[1][:, 0:512], ones1f[0:1, :],
                           grow[0:1, 512 * qr:512 * qr + 512], start=True,
                           stop=True)
            s_qrw.inc(ins)
            s_qrw.wait(t, at=3 * qr + 3)
            for fc in range(8):
                u = 8 * qr + fc
                if u >= 2:
                    s_acv.wait(t, at=u - 1)
                ba = pb[fc % 2]
                for st in range(16):
                    ins = t.matmul(
                        ba[:], agx[:, st, 128 * fc:128 * fc + 128],
                        pth[:, st, :], start=(st == 0), stop=(st == 15))
                s_scm.inc(ins)
                by = pb[2 + fc % 2]
                if u >= 2:
                    s_y0e.wait(t, at=u - 1)
                t.matmul(by[:], econT[0:8, 128 * fc:128 * fc + 128],
                         wns_scr[0:8, :], start=True, stop=False)
                ins = t.matmul(by[:], b2T[0:8, 128 * fc:128 * fc + 128],
                               w_em[0:8, 512 * qr:512 * qr + 512],
                               start=False, stop=True)
                s_y0m.inc(ins)
            s_acv.wait(t, at=8 * (qr + 1))
            for tj in range(4):
                ti = 4 * qr + tj
                if ti >= 1:
                    s_fot.wait(t, at=ti)
                for jc in range(8):
                    bank = pb[4 + jc // 4]
                    ins = t.transpose(
                        bank[:, 128 * (jc % 4):128 * (jc % 4) + 128],
                        acc[:, jc, 128 * tj:128 * tj + 128], identf[:])
                s_ftr.inc(ins)

    @blk.scalar
    def _(s):
        for qr in range(4):
            s_qrw.wait(s, at=3 * qr + 1)
            ins = s.copy(grbcq[:, :], pb[1][:, 0:512])
            s_qrw.inc(ins)
            for fc in range(8):
                u = 8 * qr + fc
                s_y0m.wait(s, at=u + 1)
                if u >= 1:
                    s_acv.wait(s, at=u)
                ins = s.copy(yscrE[:], pb[2 + fc % 2][:])
                s_y0e.inc(ins)

    @blk.sync
    def _(sy):
        for j in range(2):
            s_ysp[j].wait(sy)
        for ti in range(NOT + 1):
            if ti < NOT:
                if ti >= 2:
                    s_fot.wait(sy, at=ti - 1)
                s_yr2[ti % 2].inc(sy.dma_start(
                    out=yrld2[ti % 2][:],
                    in_=D["yspill"].ap()[128 * ti:128 * (ti + 1), :]), 16)
            if ti >= 1:
                tj = ti - 1
                s_fot.wait(sy, at=tj + 1)
                s_out2[tj % 2].inc(sy.dma_start(
                    out=D["out"].ap()[128 * tj:128 * (tj + 1), :],
                    in_=out_sb[tj % 2][:]), 16)

    if "sp" in debug:
        @blk.sync
        def _(sy):
            s_sp.wait(sy)
            s_dbg.inc(sy.dma_start(out=dbg["d_grank"].ap(), in_=grankm[:]),
                      16)
            s_rsd.wait(sy)
            s_dbg.inc(sy.dma_start(out=dbg["d_gx"].ap(), in_=gx_tm[:]), 16)
            s_dbg.inc(sy.dma_start(out=dbg["d_wsl"].ap(), in_=wsl_raw[:]),
                      16)
            s_oc.wait(sy)
            s_dbg.inc(sy.dma_start(out=dbg["d_comb"].ap(), in_=comb[:]), 16)

    @blk.gpsimd
    def _(g):
        s_sp.wait(g)
        s_agd.wait(g)
        s_hx.wait(g)
        s_y0m.wait(g)
        s_xtt.wait(g)
        s_xte.wait(g)
        s_pb.wait(g)
        s_pc.wait(g)
        s_gev.wait(g)
        s_he.wait(g)
        s_h2v.wait(g)
        s_whk.wait(g)
        s_ecn.wait(g)
        s_rw.wait(g)
        s_qrw.wait(g)
        s_scm.wait(g)
        s_y0e.wait(g)
        s_acv.wait(g)
        s_fot.wait(g)
        for j in range(2):
            s_ysp[j].wait(g)
        s_dbg.wait(g)
        for j in range(2):
            s_out2[j].wait(g)

    esEc.close()
    esE0.close()

    esD0.close()
    es.close()
    return nc, D


def kernel(**inputs):
    noise = _noise_table()
    nc, D = build_nc()
    in_maps = [_core_inputs(c, inputs, noise) for c in range(8)]
    res = run_bass_kernel_spmd(nc, in_maps, core_ids=list(range(8)))
    out = np.zeros((4, S, EMB), np.float32)
    for c in range(8):
        b, h = c // 2, c % 2
        out[b, 2048 * h:2048 * (h + 1)] = np.asarray(res.results[c]["out"],
                                                     np.float32)
    return out


# version 13 (final)
# speedup vs baseline: 1.8623x; 1.0054x over previous
import numpy as np
import ml_dtypes
import concourse.bass as bass
import concourse.mybir as mybir
from concourse.alu_op_type import AluOpType
from concourse.bass_utils import run_bass_kernel_spmd
from contextlib import ExitStack

F32 = mybir.dt.float32
BF16 = mybir.dt.bfloat16
I32 = mybir.dt.int32
AF = mybir.ActivationFunctionType
AX = mybir.AxisListType
OP = AluOpType

EMB = 1024
HEADS = 16
HD = 64
S = 4096
TOK = 2048
XROWS = 2304
NXT = 18
NOT = 16
MASKRAW = -2.0 ** 33
LN_EPS = float(np.float32(1e-5))
FP = lambda x: float(np.float32(x))
BF = ml_dtypes.bfloat16

_QPOS = np.float32(np.float32(0.9) * np.float32(16383.0))
_LOI = int(np.floor(_QPOS))
_HW = np.float32(_QPOS - np.float32(_LOI))
_LW = np.float32(np.float32(1.0) - _HW)
K1 = float(_LOI + 1)
K2 = float(_LOI + 2)


class SemAlias:

    def __init__(self, base_sem):
        self.h = base_sem.h
        self._s = base_sem
        self.b = base_sem.n

    @property
    def n(self):
        return self._s.n - self.b

    @n.setter
    def n(self, v):
        self._s.n = self.b + v

    def inc(self, instr, k=1):
        instr.then_inc(self.h, k)
        self._s.n += k
        return instr

    def wait(self, eng, at=None):
        eng.wait_ge(self.h, self._s.n if at is None else self.b + at)


class Sem:

    def __init__(self, nc, es, name):
        self.h = es.enter_context(nc.semaphore(name))
        self.n = 0

    def inc(self, instr, k=1):
        instr.then_inc(self.h, k)
        self.n += k
        return instr

    def wait(self, eng, at=None):
        eng.wait_ge(self.h, self.n if at is None else at)


def _host_consts():
    c = {}
    c["identf"] = np.eye(128, dtype=np.float32)
    ii = np.arange(128)
    band = np.where(ii[None, :] >= ii[:, None], 0.0, MASKRAW).astype(np.float32)
    diag = np.where(ii[None, :] <= ii[:, None], 0.0, MASKRAW).astype(np.float32)
    c["mband"] = band
    c["mdiag"] = diag
    R = np.zeros((HD, HD), np.float32)
    for i in range(0, HD, 2):
        R[i, i + 1] = -1.0
        R[i + 1, i] = 1.0
    R2 = np.zeros((128, 128), np.float32)
    R2[0:64, 0:64] = R
    R2[64:128, 64:128] = R
    c["rperm"] = np.ascontiguousarray(R2.T)
    c["idx8"] = np.broadcast_to(np.arange(8, dtype=np.float32), (128, 8)).copy()
    c["negbig"] = np.full((128, 8), -3e38, np.float32)
    c["neg1e9"] = np.full((128, 8), -1e9, np.float32)
    inv_freq = 1.0 / (10000.0 ** (np.arange(0, HD, 2, dtype=np.float32) / HD))
    pos = np.arange(S, dtype=np.float32)
    fr = np.outer(pos, inv_freq)
    emb = np.concatenate([fr, fr], -1)
    c["cosS"] = np.cos(emb).astype(np.float32)
    c["sinS"] = np.sin(emb).astype(np.float32)
    return c


_CONSTS = _host_consts()


def _noise_table():
    import jax
    with jax.default_device(jax.devices("cpu")[0]):
        n = np.asarray(jax.random.normal(jax.random.key(1), (4, S, 8)),
                       np.float32)
    return n


def _core_inputs(c, inp, noise):
    b, h = c // 2, c % 2
    p0 = 2048 * h
    x = inp["L0_attn_x"][b]
    xa = np.zeros((XROWS, EMB), np.float32)
    if h == 1:
        xa[0:128] = x[p0 - 128:p0]
    xa[128:128 + TOK] = x[p0:p0 + TOK]
    xa[128 + TOK:128 + TOK + 4] = x[0:4]
    m = {"xa": xa}
    posn = np.zeros(XROWS, np.int64)
    posn[0:128] = np.arange(p0 - 128, p0) if h == 1 else 0
    posn[128:128 + TOK] = np.arange(p0, p0 + TOK)
    posn[128 + TOK:128 + TOK + 4] = np.arange(4)
    cos1 = _CONSTS["cosS"][posn]
    sin1 = _CONSTS["sinS"][posn]
    m["cosb"] = np.concatenate([cos1, cos1], 1).T.copy()
    m["sinb"] = np.concatenate([sin1, sin1], 1).T.copy()
    m["mA0"] = (np.full((128, 128), MASKRAW, np.float32) if h == 0
                else _CONSTS["mband"])
    m["gmask0"] = np.full((128, 4), MASKRAW if h == 0 else 0.0, np.float32)
    m["qkvwT"] = np.ascontiguousarray(inp["L0_attn_qkv_w"].T)
    m["outwT"] = np.ascontiguousarray(inp["L0_attn_out_w"].T)
    m["noise"] = np.ascontiguousarray(
        noise[b, p0:p0 + TOK, :].reshape(16, 128, 8).transpose(1, 0, 2))
    for k in ("identf", "mband", "mdiag", "rperm", "idx8", "negbig",
              "neg1e9"):
        m[k] = _CONSTS[k]
    m["gw1"] = inp["L0_ffn_gate_w1"]
    m["gw2"] = inp["L0_ffn_gate_w2"]
    m["gw3"] = inp["L0_ffn_gate_w3"]
    m["affw"] = inp["L0_ffn_aff_w"]
    m["gb1"] = inp["L0_ffn_gate_b1"].reshape(8, 128).T.copy()
    m["gb2"] = inp["L0_ffn_gate_b2"].reshape(4, 128).T.copy()
    m["lnsc"] = inp["L0_ffn_ln_scale"].reshape(8, 128).T.copy()
    m["lnof"] = inp["L0_ffn_ln_offset"].reshape(8, 128).T.copy()
    m["gb3"] = inp["L0_ffn_gate_b3"].reshape(8, 1).copy()
    m["affb"] = inp["L0_ffn_aff_b"].reshape(8, 1).copy()
    m["usage"] = inp["L0_ffn_expert_usage"].reshape(1, 8).copy()
    sel = np.zeros((1, 8), np.float32)
    sel[0, 2 * b] = 1.0
    sel[0, 2 * b + 1] = 1.0
    m["rankmask"] = sel
    m["e0w1"] = inp["L0_ffn_e0_w1"]
    m["e0w2"] = inp["L0_ffn_e0_w2"]
    m["e1w1"] = inp["L0_ffn_e1_w1"]
    m["e1w2"] = inp["L0_ffn_e1_w2"]
    m["e1w3"] = inp["L0_ffn_e1_w3"]
    m["e2w1"] = inp["L0_ffn_e2_w1"]
    m["e2w2"] = inp["L0_ffn_e2_w2"]
    m["e0b1"] = inp["L0_ffn_e0_b1"].reshape(3, 32, 128).transpose(0, 2, 1).copy()
    m["e0b2"] = inp["L0_ffn_e0_b2"].reshape(3, 8, 128).transpose(0, 2, 1).copy()
    m["e1b1"] = inp["L0_ffn_e1_b1"].reshape(3, 16, 128).transpose(0, 2, 1).copy()
    m["e1b2"] = inp["L0_ffn_e1_b2"].reshape(3, 16, 128).transpose(0, 2, 1).copy()
    m["e1b3"] = inp["L0_ffn_e1_b3"].reshape(3, 8, 128).transpose(0, 2, 1).copy()
    m["e2b1"] = inp["L0_ffn_e2_b1"].reshape(2, 24, 128).transpose(0, 2, 1).copy()
    m["e2b2"] = inp["L0_ffn_e2_b2"].reshape(2, 8, 128).transpose(0, 2, 1).copy()
    q = np.arange(128)
    m["ltri"] = (q[:, None] < q[None, :]).astype(np.float32)
    m["r256"] = np.broadcast_to(8.0 * np.arange(256, dtype=np.float32),
                                (128, 256)).copy()
    st = np.arange(16)
    m["gc16"] = (8.0 * (128.0 * (st[None, :] % 2) + q[:, None])
                 + st[None, :] // 2).astype(np.float32)
    m["rltm"] = np.broadcast_to(
        (np.arange(8)[None, :] < c).astype(np.float32), (128, 8)).copy()
    return m


FAM = {0: ("e0", 0), 1: ("e1", 0), 2: ("e2", 0), 3: ("e0", 1),
       4: ("e1", 1), 5: ("e2", 1), 6: ("e0", 2), 7: ("e1", 2)}


def build_nc(debug=()):
    nc = bass.Bass("TRN2", target_bir_lowering=False)
    D = {}

    def din(name, shape, dt=F32):
        D[name] = nc.dram_tensor(name, list(shape), dt, kind="ExternalInput")
        return D[name]

    def dout(name, shape, dt=F32):
        D[name] = nc.dram_tensor(name, list(shape), dt, kind="ExternalOutput")
        return D[name]

    din("xa", (XROWS, EMB))
    din("cosb", (128, XROWS)); din("sinb", (128, XROWS))
    din("mA0", (128, 128)); din("gmask0", (128, 4))
    din("mband", (128, 128)); din("mdiag", (128, 128))
    din("identf", (128, 128)); din("rperm", (128, 128))
    din("idx8", (128, 8)); din("negbig", (128, 8)); din("neg1e9", (128, 8))
    din("qkvwT", (EMB, 3 * EMB)); din("outwT", (EMB, EMB))
    din("noise", (128, 16, 8))
    din("gw1", (1024, 1024)); din("gw2", (1024, 512)); din("gw3", (512, 8))
    din("affw", (1024, 8))
    din("gb1", (128, 8)); din("gb2", (128, 4)); din("lnsc", (128, 8))
    din("lnof", (128, 8)); din("gb3", (8, 1)); din("affb", (8, 1))
    din("usage", (1, 8)); din("rankmask", (1, 8))
    din("e0w1", (3, 1024, 4096)); din("e0w2", (3, 4096, 1024))
    din("e1w1", (3, 1024, 2048)); din("e1w2", (3, 2048, 2048))
    din("e1w3", (3, 2048, 1024))
    din("e2w1", (2, 1024, 3072)); din("e2w2", (2, 3072, 1024))
    din("e0b1", (3, 128, 32)); din("e0b2", (3, 128, 8))
    din("e1b1", (3, 128, 16)); din("e1b2", (3, 128, 16))
    din("e1b3", (3, 128, 8))
    din("e2b1", (2, 128, 24)); din("e2b2", (2, 128, 8))
    din("ltri", (128, 128)); din("r256", (128, 256))
    din("gc16", (128, 16)); din("rltm", (128, 8))
    for nm, sh in (("c_e0w1", (3, 1024, 4096)), ("c_e0w2", (3, 4096, 1024)),
                   ("c_e1w1", (3, 1024, 2048)), ("c_e1w2", (3, 2048, 2048)),
                   ("c_e1w3", (3, 2048, 1024)), ("c_e2w1", (2, 1024, 3072)),
                   ("c_e2w2", (2, 3072, 1024))):
        D[nm] = nc.dram_tensor(nm, list(sh), BF16, kind="Internal")
    D["rsin"] = nc.dram_tensor("rsin", [8, 128, 2, 1024], BF16,
                               kind="Internal")
    D["rsout"] = nc.dram_tensor("rsout", [128, 2, 1024], BF16,
                                kind="Internal")
    D["wrsin"] = nc.dram_tensor("wrsin", [8, 8, 256], F32, kind="Internal")
    D["wrsout"] = nc.dram_tensor("wrsout", [8, 256], F32, kind="Internal")
    D["agin2"] = nc.dram_tensor("agin2", [2, 128, 1024], BF16,
                                kind="Internal")
    D["agout2"] = nc.dram_tensor("agout2", [8, 2, 128, 1024], BF16,
                                 kind="Internal")

    dout("out", (TOK, EMB))

    D["yspill"] = nc.dram_tensor("yspill", [TOK, EMB], F32)
    D["agin"] = nc.dram_tensor("agin", [128, 144], F32)
    D["agout"] = nc.dram_tensor("agout", [1024, 144], F32)

    dbg = {}
    if "ln1" in debug:
        dbg["d_xlnfm"] = dout("d_xlnfm", (128, 8, XROWS), BF16)
    if "attn" in debug:
        dbg["d_ofm"] = dout("d_ofm", (128, 8, TOK), BF16)
    if "attn2" in debug:
        dbg["d_qrot"] = dout("d_qrot", (128, TOK), BF16)
        dbg["d_krot"] = dout("d_krot", (128, XROWS), BF16)
        dbg["d_qa"] = dout("d_qa", (128, TOK), BF16)
        dbg["d_vtm"] = dout("d_vtm", (2, 128, 18, 128), BF16)
    if "y" in debug:
        dbg["d_y"] = dout("d_y", (TOK, EMB))
        dbg["d_u"] = dout("d_u", (128, 8, TOK), BF16)
        dbg["d_xchg"] = dout("d_xchg", (128, 144))
    if "thr" in debug:
        dbg["d_thr"] = dout("d_thr", (128, 16))
    if "sp" in debug:
        dbg["d_grank"] = dout("d_grank", (128, 16))
        dbg["d_gx"] = dout("d_gx", (128, 2, 1024), BF16)
        dbg["d_wsl"] = dout("d_wsl", (8, 256))
        dbg["d_comb"] = dout("d_comb", (128, 2, 1024))
    if "gate" in debug:
        dbg["d_logits"] = dout("d_logits", (128, 16, 8))
        dbg["d_w"] = dout("d_w", (128, 16, 8))
        dbg["d_xnorm"] = dout("d_xnorm", (128, 8, TOK), BF16)
        dbg["d_sel2"] = dout("d_sel2", (128, 16, 8))
        dbg["d_i1"] = dout("d_i1", (128, 16))
        dbg["d_i2"] = dout("d_i2", (128, 16))

    es = ExitStack()

    def sb(name, shape, dt):
        return es.enter_context(nc.sbuf_tensor(name, list(shape), dt))

    def ps(name, shape, dt=F32):
        return es.enter_context(nc.psum_tensor(name, list(shape), dt))

    def sem(name):
        return Sem(nc, es, name)

    identf = sb("identf_s", (128, 128), F32)
    identb = sb("identb", (128, 128), BF16)
    ones1 = sb("ones1", (1, 128), BF16)
    onesc = sb("onesc", (128, 1), BF16)
    masks = sb("masks", (128, 4 * 128 + 4), BF16)
    idx8 = sb("idx8_s", (128, 8), F32)
    ltri_b = sb("ltri_b", (128, 128), BF16)
    r2560 = sb("r2560", (128, 256), F32)
    gc16_sb = sb("gc16_sb", (128, 16), F32)
    rltm_sb = sb("rltm_sb", (128, 8), F32)
    negbig = sb("negbig_s", (128, 8), F32)
    neg1e9 = sb("neg1e9_s", (128, 8), F32)
    xchg = sb("xchg", (128, 144), F32)
    gath = sb("gath", (128, 8, 144), F32)
    u_fm = sb("u_fm", (128, 8, TOK), BF16)
    scal = sb("scal", (128, 64), F32)
    SC_THR = 0
    SC_RTEMP = 1
    SC_NSC = 2
    SC_B3P = 3
    SC_LO = 4; SC_HI = 5; SC_MID = 6; SC_PRED = 7; SC_CNT = 8
    SC_T1 = 9; SC_T2 = 10; SC_S1 = 11; SC_S2 = 12
    SC_AFF = 13
    SC_TMP = 14; SC_TMP2 = 15
    gcols = sb("gcols", (128, 16 * 10), F32)
    biases = sb("biases", (128, 332), F32)
    BOFF = {}
    _bo = 0
    for nm, w in (("gb1", 8), ("gb2", 4), ("lnsc", 8), ("lnof", 8),
                  ("e0b1", 32 * 3), ("e0b2", 8 * 3), ("e1b1", 16 * 3),
                  ("e1b2", 16 * 3), ("e1b3", 8 * 3), ("e2b1", 24 * 2),
                  ("e2b2", 8 * 2)):
        BOFF[nm] = _bo
        _bo += w
    assert _bo <= 332, _bo
    noise_sb = sb("noise_sb", (128, 16, 8), F32)
    usage_sb = sb("usage_sb", (1, 8), F32)
    rankm_sb = sb("rankm_sb", (1, 8), F32)
    gb3_sb = sb("gb3_sb", (8, 1), F32)
    affb_sb = sb("affb_sb", (8, 1), F32)

    pb = [ps(f"pb{i}", (128, 512), F32) for i in range(6)]
    pt0 = ps("pt0", (128, 1024), BF16)
    pt1 = ps("pt1", (128, 1024), BF16)

    s_xload = [sem(f"s_xload{j}") for j in range(3)]
    s_ln1 = sem("s_ln1")
    s_sq1 = sem("s_sq1")
    s_ap1 = sem("s_ap1")
    s_tp1 = sem("s_tp1")
    s_ev1 = sem("s_ev1")
    s_cload = sem("s_cload")
    s_wcast = sem("s_wcast")
    s_gload = sem("s_gload")
    s_sc = sem("s_sc")
    s_exp = sem("s_exp")
    s_pn = sem("s_pn")
    s_pt = sem("s_pt")
    s_pte = sem("s_pte")
    s_av = sem("s_av")
    s_ave = sem("s_ave")
    s_op = sem("s_op")
    s_utp = sem("s_utp")
    s_ue = sem("s_ue")
    s_qtm = sem("s_qtm")
    s_ag = sem("s_ag")
    s_agl = sem("s_agl")
    s_gwl = sem("s_gwl")
    s_out = sem("s_out")
    s_dbg = sem("s_dbg")

    blk = es.enter_context(nc.Block())

    esL1b = ExitStack()
    O_fm = esL1b.enter_context(nc.sbuf_tensor("O_fm", [128, 8, TOK], BF16))
    outw_sb = esL1b.enter_context(nc.sbuf_tensor("outw_sb", [128, 8, EMB], BF16))

    esA = ExitStack()

    def sbA(name, shape, dt):
        return esA.enter_context(nc.sbuf_tensor(name, list(shape), dt))

    xln_fm = sbA("xln_fm", (128, 8, XROWS), BF16)
    cos_sb = sbA("cos_sb", (128, XROWS), BF16)
    sin_sb = sbA("sin_sb", (128, XROWS), BF16)

    @blk.sync
    def _(sy):
        s_cload.inc(sy.dma_start(out=identf[:], in_=D["identf"].ap()), 16)
        s_cload.inc(sy.dma_start(out=idx8[:], in_=D["idx8"].ap()), 16)
        s_cload.inc(sy.dma_start(out=negbig[:], in_=D["negbig"].ap()), 16)
        s_cload.inc(sy.dma_start(out=neg1e9[:], in_=D["neg1e9"].ap()), 16)
        s_cload.inc(sy.dma_start(out=noise_sb[:, :, :], in_=D["noise"].ap()), 16)
        s_cload.inc(sy.dma_start(out=usage_sb[:], in_=D["usage"].ap()), 16)
        s_cload.inc(sy.dma_start(out=rankm_sb[:], in_=D["rankmask"].ap()), 16)
        s_cload.inc(sy.dma_start(out=gb3_sb[:], in_=D["gb3"].ap()), 16)
        s_cload.inc(sy.dma_start(out=affb_sb[:], in_=D["affb"].ap()), 16)

    @blk.gpsimd
    def _(g):
        s_gload.inc(g.dma_start(out=identb[:], in_=D["identf"].ap()), 16)
        s_gload.inc(g.dma_start(out=masks[:, 0:128], in_=D["mband"].ap()), 16)
        s_gload.inc(g.dma_start(out=masks[:, 128:256], in_=D["mdiag"].ap()), 16)
        s_gload.inc(g.dma_start(out=masks[:, 256:384], in_=D["mA0"].ap()), 16)
        s_gload.inc(g.dma_start(out=masks[:, 384:512], in_=D["rperm"].ap()), 16)
        s_gload.inc(g.dma_start(out=masks[:, 512:516], in_=D["gmask0"].ap()), 16)
        s_gload.inc(g.dma_start(out=cos_sb[:], in_=D["cosb"].ap()), 16)
        s_gload.inc(g.dma_start(out=sin_sb[:], in_=D["sinb"].ap()), 16)
        s_gload.inc(g.dma_start(out=ltri_b[:], in_=D["ltri"].ap()), 16)
        s_gload.inc(g.dma_start(out=r2560[:], in_=D["r256"].ap()), 16)
        s_gload.inc(g.dma_start(out=gc16_sb[:], in_=D["gc16"].ap()), 16)
        s_gload.inc(g.dma_start(out=rltm_sb[:], in_=D["rltm"].ap()), 16)
        for nm in ("gb1", "gb2", "lnsc", "lnof"):
            w = {"gb1": 8, "gb2": 4, "lnsc": 8, "lnof": 8}[nm]
            s_gload.inc(g.dma_start(
                out=biases[:, BOFF[nm]:BOFF[nm] + w], in_=D[nm].ap()), 16)
        for nm, per in (("e0b1", 32), ("e0b2", 8), ("e1b1", 16), ("e1b2", 16),
                        ("e1b3", 8)):
            for e in range(3):
                s_gload.inc(g.dma_start(
                    out=biases[:, BOFF[nm] + per * e:BOFF[nm] + per * (e + 1)],
                    in_=D[nm].ap()[e]), 16)
        for nm, per in (("e2b1", 24), ("e2b2", 8)):
            for e in range(2):
                s_gload.inc(g.dma_start(
                    out=biases[:, BOFF[nm] + per * e:BOFF[nm] + per * (e + 1)],
                    in_=D[nm].ap()[e]), 16)

    GLOAD_EARLY = 16 * 12

    @blk.vector
    def _(v):
        s_cload.wait(v)
        s_gload.wait(v, at=GLOAD_EARLY)
        v.memset(ones1[:], 1.0)
        v.memset(scal[:], 0.0)
        v.memset(gcols[:], 0.0)
        s_ap1.inc(v.memset(onesc[:], 1.0))

    AP1_BASE = s_ap1.n

    mband = masks[:, 0:128]
    mdiag = masks[:, 128:256]
    mA0 = masks[:, 256:384]
    rpermb = masks[:, 384:512]
    gmask0 = masks[:, 512:516]

    esA1 = ExitStack()
    xr = [esA1.enter_context(nc.sbuf_tensor(f"xr{j}", [128, EMB], F32))
          for j in range(3)]
    xlt = [esA1.enter_context(nc.sbuf_tensor(f"xlt{j}", [128, EMB], BF16))
           for j in range(2)]
    sq_scr = esA1.enter_context(nc.sbuf_tensor("sq_scr", [128, EMB], F32))
    lncols = esA1.enter_context(nc.sbuf_tensor("lncols", [128, 24], F32))

    @blk.sync
    def _(sy):
        for i in range(NXT):
            if i >= 3:
                s_ap1.wait(sy, at=AP1_BASE + (i - 2))
            s_xload[i % 3].inc(sy.dma_start(
                out=xr[i % 3][:], in_=D["xa"].ap()[128 * i:128 * (i + 1), :]),
                16)

    def _ln_stats(v, x, cols):
        mn = cols[:, 0:1]; rstd = cols[:, 1:2]
        ssq = cols[:, 2:3]; t1 = cols[:, 3:4]; t2 = cols[:, 4:5]
        v.tensor_reduce(mn, x, AX.X, OP.add, negate=True)
        v.tensor_tensor(sq_scr[:], x, x, OP.mult)
        v.drain()
        v.tensor_reduce(ssq, sq_scr[:], AX.X, OP.add)
        v.tensor_tensor(t1, mn, mn, OP.mult)
        v.drain()
        v.scalar_tensor_tensor(ssq, t1, FP(-1.0 / EMB), ssq, OP.mult, OP.add)
        v.drain()
        v.tensor_scalar(t2, ssq, FP(1.0 / EMB), LN_EPS,
                        OP.mult, OP.add)
        v.drain()
        return v.reciprocal(rstd, t2)

    @blk.vector
    def _(v):
        for i in range(NXT):
            cols = lncols[:, 6 * (i % 4):6 * (i % 4) + 6]
            s_xload[i % 3].wait(v, at=16 * (i // 3 + 1))
            s_ln1.inc(_ln_stats(v, xr[i % 3][:], cols))
            s_sq1.wait(v, at=i + 1)
            if i >= 2:
                s_tp1.wait(v, at=i - 1)
            mn = cols[:, 0:1]; rstd = cols[:, 1:2]; t1 = cols[:, 3:4]
            v.tensor_scalar(t1, mn, FP(1.0 / EMB), None, OP.mult)
            v.drain()
            s_ap1.inc(v.tensor_scalar(xlt[i % 2][:], xr[i % 3][:], t1, rstd,
                                      OP.add, OP.mult))

    @blk.scalar
    def _(s):
        for i in range(NXT + 2):
            if i < NXT:
                cols = lncols[:, 6 * (i % 4):6 * (i % 4) + 6]
                s_ln1.wait(s, at=i + 1)
                s_sq1.inc(s.activation(cols[:, 1:2], cols[:, 1:2], AF.Sqrt))
            if i >= 2:
                j = i - 2
                s_tp1.wait(s, at=j + 1)
                bank = (pt0 if j % 2 == 0 else pt1)
                s_ev1.inc(s.copy(xln_fm[:, :, 128 * j:128 * (j + 1)],
                                 bank[:].rearrange("p (c j) -> p c j", c=8)))

    @blk.tensor
    def _(t):
        s_cload.wait(t)
        s_gload.wait(t, at=GLOAD_EARLY)
        for i in range(NXT):
            t.wait_ge(s_ap1.h, AP1_BASE + i + 1)
            bank = (pt0 if i % 2 == 0 else pt1)
            if i >= 2:
                s_ev1.wait(t, at=i - 1)
            for j in range(8):
                ins = t.transpose(bank[:, 128 * j:128 * (j + 1)],
                                  xlt[i % 2][:, 128 * j:128 * (j + 1)],
                                  identb[:])
            s_tp1.inc(ins)


    esA1.close()
    esB = ExitStack()

    def sbB(name, shape, dt):
        return esB.enter_context(nc.sbuf_tensor(name, list(shape), dt))

    wq = [sbB(f"wq{j}", (128, 8, 128), BF16) for j in range(2)]
    wk = [sbB(f"wk{j}", (128, 8, 128), BF16) for j in range(2)]
    wv = [sbB(f"wv{j}", (128, 8, 128), BF16) for j in range(2)]
    qa_sb = sbB("qa_sb", (128, TOK), BF16)
    ka_sb = sbB("ka_sb", (128, XROWS), BF16)
    qrot = sbB("qrot", (128, TOK), BF16)
    krot = sbB("krot", (128, XROWS), BF16)
    rsc1 = sbB("rsc1", (128, 512), F32)
    rsc2 = sbB("rsc2", (128, 512), F32)
    vtmg = [sbB(f"vtmg{j}", (128, 18, 128), BF16) for j in range(2)]
    p_sb = [sbB(f"p_sb{j}", (128, 260), BF16) for j in range(2)]
    pn_sb = [sbB(f"pn_sb{j}", (128, 260), BF16) for j in range(2)]
    pt_sb = [sbB(f"pt_sb{j}", (128, 256), BF16) for j in range(2)]
    ptg_sb = [sbB(f"ptg_sb{j}", (4, 128), BF16) for j in range(2)]
    acols = sbB("acols", (128, 8), F32)

    s_wq = [sem(f"s_wq{j}") for j in range(2)]
    s_wk = [sem(f"s_wk{j}") for j in range(2)]
    s_wv = [sem(f"s_wv{j}") for j in range(2)]
    s_qkq = sem("s_qkq")
    s_qkk = sem("s_qkk")
    s_qkv = sem("s_qkv")
    s_qaev = sem("s_qaev")
    s_kaev = sem("s_kaev")
    s_vev = sem("s_vev")
    s_pmq = sem("s_pmq")
    s_pmk = sem("s_pmk")
    s_rpq = sem("s_rpq")
    s_rpk = sem("s_rpk")
    s_rmax = sem("s_rmax")

    NI = 32
    KCH = [(0, 512), (512, 512), (1024, 512), (1536, 512), (2048, 256)]
    QCH = [(128, 512), (640, 512), (1152, 512), (1664, 512)]

    def hq(i):
        return (i % 2, i // 2)

    CAST_CHUNKS = []
    for snm, n_e, R, C in (("e0w1", 3, 1024, 4096), ("e1w1", 3, 1024, 2048),
                           ("e1w2", 3, 2048, 2048), ("e2w1", 2, 1024, 3072)):
        rc = 128 if C >= 3072 else 256
        for ei in range(n_e):
            for ch in range(R // rc):
                CAST_CHUNKS.append((snm, ei, rc * ch, rc))
    NCAST = len(CAST_CHUNKS)

    @blk.gpsimd
    def _(g):
        s_tp1.wait(g, at=NXT)
        ci = 0
        for gi in range(8):
            j = gi % 2
            if gi >= 2:
                s_qkq.wait(g, at=4 * (gi - 1))
                s_qkk.wait(g, at=5 * (gi - 1))
                s_qkv.wait(g, at=5 * (gi - 1))
            qsrc = D["qkvwT"].ap()[:, 128 * gi:128 * (gi + 1)]
            ksrc = D["qkvwT"].ap()[:, 1024 + 128 * gi:1024 + 128 * (gi + 1)]
            vsrc = D["qkvwT"].ap()[:, 2048 + 128 * gi:2048 + 128 * (gi + 1)]
            s_wq[j].inc(g.dma_start(
                out=wq[j][:, :, :],
                in_=qsrc.rearrange("(c p) e -> p c e", p=128)), 16)
            s_wk[j].inc(g.dma_start(
                out=wk[j][:, :, :],
                in_=ksrc.rearrange("(c p) e -> p c e", p=128)), 16)
            s_wv[j].inc(g.dma_start(
                out=wv[j][:, :, :],
                in_=vsrc.rearrange("(c p) e -> p c e", p=128)), 16)
            if gi >= 1:
                ntake = NCAST * gi // 7 - ci if gi < 7 else NCAST - ci
                for _ in range(ntake):
                    snm, ei, r0, rc = CAST_CHUNKS[ci]
                    s_wcast.inc(g.dma_start(
                        out=D["c_" + snm].ap()[ei][r0:r0 + rc, :],
                        in_=D[snm].ap()[ei][r0:r0 + rc, :]), 16)
                    ci += 1
        wsrc = D["outwT"].ap().rearrange("(c p) e -> p c e", p=128)
        s_gload.inc(g.dma_start(out=outw_sb[:, :, :], in_=wsrc), 16)

    class BankPair:

        def __init__(self):
            self.uses = []

        def acquire(self, eng):
            if len(self.uses) >= 2:
                sem, cnt = self.uses[-2]
                sem.wait(eng, at=cnt)

        def commit(self, release_sem, release_cnt):
            self.uses.append((release_sem, release_cnt))

    def emit_pt(t, io):
        ptbank = pt0 if io % 2 == 0 else pt1
        s_pn.wait(t, at=io + 1)
        if io >= 2:
            s_pte.wait(t, at=io - 1)
        t.transpose(ptbank[:, 0:128], pn_sb[io % 2][:, 0:128], identb[:])
        t.transpose(ptbank[:, 128:256], pn_sb[io % 2][:, 128:256], identb[:])
        s_pt.inc(t.transpose(ptbank[0:4, 256:384],
                             pn_sb[io % 2][:, 256:260], identb[:]))

    def emit_av(t, io):
        go = io // NI
        ho, qto = hq(io % NI)
        bank2 = pb[4 + io % 2]
        s_pte.wait(t, at=io + 1)
        if io >= 2:
            s_ave.wait(t, at=io - 1)
        vt = vtmg[go % 2]
        t.matmul(bank2[0:64, 0:128], vt[:, qto, 64 * ho:64 * ho + 64],
                 pt_sb[io % 2][:, 0:128], start=True, stop=False)
        t.matmul(bank2[0:64, 0:128], vt[:, qto + 1, 64 * ho:64 * ho + 64],
                 pt_sb[io % 2][:, 128:256], start=False, stop=False)
        s_av.inc(t.matmul(bank2[0:64, 0:128],
                          vt[0:4, 17, 64 * ho:64 * ho + 64],
                          ptg_sb[io % 2][:, :], start=False, stop=True))


    bp01 = BankPair()
    bp23 = BankPair()

    @blk.tensor
    def _(t):
        s_ev1.wait(t)
        for gi in range(8):
            j = gi % 2
            s_wq[j].wait(t, at=16 * (gi // 2 + 1))
            for tc, (o, w) in enumerate(QCH):
                bank = pb[tc % 2]
                bp01.acquire(t)
                for k in range(8):
                    ins = t.matmul(bank[:, 0:w], wq[j][:, k, :],
                                   xln_fm[:, k, o:o + w],
                                   start=(k == 0), stop=(k == 7))
                s_qkq.inc(ins)
                bp01.commit(s_qaev, gi * 4 + tc + 1)
            s_wk[j].wait(t, at=16 * (gi // 2 + 1))
            for tc, (o, w) in enumerate(KCH):
                bank = pb[tc % 2]
                bp01.acquire(t)
                for k in range(8):
                    ins = t.matmul(bank[:, 0:w], wk[j][:, k, :],
                                   xln_fm[:, k, o:o + w],
                                   start=(k == 0), stop=(k == 7))
                s_qkk.inc(ins)
                bp01.commit(s_kaev, gi * 5 + tc + 1)
            s_qaev.wait(t, at=4 * (gi + 1))
            for tc, (o, w) in enumerate(QCH):
                bank = pb[2 + tc % 2]
                bp23.acquire(t)
                s_pmq.inc(t.matmul(bank[:, 0:w], rpermb,
                                   qa_sb[:, o - 128:o - 128 + w],
                                   start=True, stop=True))
                bp23.commit(s_rpq, gi * 4 + tc + 1)
            s_kaev.wait(t, at=5 * (gi + 1))
            for tc, (o, w) in enumerate(KCH):
                bank = pb[2 + tc % 2]
                bp23.acquire(t)
                s_pmk.inc(t.matmul(bank[:, 0:w], rpermb, ka_sb[:, o:o + w],
                                   start=True, stop=True))
                bp23.commit(s_rpk, gi * 5 + tc + 1)
            s_wv[j].wait(t, at=16 * (gi // 2 + 1))
            for bg in range(5):
                bank = pb[bg % 2]
                r0 = 4 * bg
                nt = min(4, 18 - r0)
                bp01.acquire(t)
                for r in range(r0, r0 + nt):
                    for k in range(8):
                        ins = t.matmul(
                            bank[:, 128 * (r - r0):128 * (r - r0) + 128],
                            xln_fm[:, k, 128 * r:128 * (r + 1)],
                            wv[j][:, k, :],
                            start=(k == 0 and r == r0),
                            stop=(k == 7 and r == r0 + nt - 1))
                s_qkv.inc(ins)
                bp01.commit(s_vev, gi * 5 + bg + 1)
            s_rpq.wait(t, at=4 * (gi + 1))
            s_rpk.wait(t, at=5 * (gi + 1))
            s_vev.wait(t, at=5 * (gi + 1))
            for i in range(NI):
                h, qt = hq(i)
                ii = gi * NI + i
                bank = pb[2 + i % 2]
                bp23.acquire(t)
                q_sl = qrot[64 * h:64 * h + 64, 128 * qt:128 * qt + 128]
                t.matmul(bank[:, 0:128], q_sl,
                         krot[64 * h:64 * h + 64, 128 * qt:128 * qt + 128],
                         start=True, stop=False)
                t.matmul(bank[:, 128:256], q_sl,
                         krot[64 * h:64 * h + 64,
                              128 * qt + 128:128 * qt + 256],
                         start=False, stop=False)
                t.matmul(bank[:, 256:260], q_sl,
                         krot[64 * h:64 * h + 64, 2176:2180],
                         start=False, stop=False)
                masks_todo = [(slice(128, 256), mdiag)]
                if qt % 2 == 0:
                    masks_todo.append((slice(0, 128), mA0 if qt == 0 else mband))
                if qt in (0, 1):
                    masks_todo.append((slice(256, 260), gmask0))
                for mi, (sl, mt) in enumerate(masks_todo):
                    ins = t.matmul(bank[:, sl], identb[:], mt,
                                   start=False, stop=(mi == len(masks_todo) - 1))
                s_sc.inc(ins)
                bp23.commit(s_exp, ii + 1)
                if ii >= 1:
                    emit_pt(t, ii - 1)
                if ii >= 2:
                    emit_av(t, ii - 2)
        NTOT = 8 * NI
        emit_pt(t, NTOT - 1)
        emit_av(t, NTOT - 2)
        emit_av(t, NTOT - 1)

    @blk.scalar
    def _(s):
        for gi in range(8):
            for tc, (o, w) in enumerate(QCH):
                s_qkq.wait(s, at=gi * 4 + tc + 1)
                if gi > 0 and tc == 0:
                    s_rpq.wait(s, at=4 * gi)
                    s_pmq.wait(s, at=4 * gi)
                s_qaev.inc(s.copy(qa_sb[:, o - 128:o - 128 + w],
                                  pb[tc % 2][:, 0:w]))
            for tc, (o, w) in enumerate(KCH):
                s_qkk.wait(s, at=gi * 5 + tc + 1)
                if gi > 0 and tc == 0:
                    s_rpk.wait(s, at=5 * gi)
                    s_pmk.wait(s, at=5 * gi)
                s_kaev.inc(s.copy(ka_sb[:, o:o + w], pb[tc % 2][:, 0:w]))
            for bg in range(5):
                r0 = 4 * bg
                nt = min(4, 18 - r0)
                s_qkv.wait(s, at=gi * 5 + bg + 1)
                if gi >= 2 and bg == 0:
                    s_av.wait(s, at=(gi - 1) * NI)
                s_vev.inc(s.copy(
                    vtmg[gi % 2][:, r0:r0 + nt, :],
                    pb[bg % 2][:, 0:128 * nt].rearrange(
                        "p (r j) -> p r j", r=nt)))
            for i in range(NI):
                h, qt = hq(i)
                ii = gi * NI + i
                bank = pb[2 + i % 2]
                s_rmax.wait(s, at=ii + 1)
                s_exp.inc(s.activation(
                    p_sb[i % 2][:, 0:260], bank[:, 0:260], AF.Exp,
                    bias=acols[:, 4 + i % 2:5 + i % 2], scale=0.125,
                    accum_out=acols[:, 6 + i % 2:7 + i % 2]))
                if ii >= 1:
                    io = ii - 1
                    ptbank = pt0 if io % 2 == 0 else pt1
                    s_pt.wait(s, at=io + 1)
                    if io >= 2:
                        s_av.wait(s, at=io - 1)
                    s.copy(pt_sb[io % 2][:, 0:256], ptbank[:, 0:256])
                    s_pte.inc(s.copy(ptg_sb[io % 2][:, :],
                                     ptbank[0:4, 256:384]))
                if ii >= 2:
                    io = ii - 2
                    go = io // NI
                    ho, qto = hq(io % NI)
                    s_av.wait(s, at=io + 1)
                    s_ave.inc(s.copy(
                        O_fm[64 * ho:64 * ho + 64, go, 128 * qto:128 * qto + 128],
                        pb[4 + io % 2][0:64, 0:128]))
        NTOT = 8 * NI
        for io in (NTOT - 1,):
            ptbank = pt0 if io % 2 == 0 else pt1
            s_pt.wait(s, at=io + 1)
            s_av.wait(s, at=io - 1)
            s.copy(pt_sb[io % 2][:, 0:256], ptbank[:, 0:256])
            s_pte.inc(s.copy(ptg_sb[io % 2][:, :], ptbank[0:4, 256:384]))
        for io in (NTOT - 2, NTOT - 1):
            go = io // NI
            ho, qto = hq(io % NI)
            s_av.wait(s, at=io + 1)
            s_ave.inc(s.copy(
                O_fm[64 * ho:64 * ho + 64, go, 128 * qto:128 * qto + 128],
                pb[4 + io % 2][0:64, 0:128]))

    @blk.vector
    def _(v):
        for gi in range(8):
            for tc, (o, w) in enumerate(QCH):
                bank = pb[2 + tc % 2]
                s_pmq.wait(v, at=gi * 4 + tc + 1)
                if gi > 0 and tc == 0:
                    s_sc.wait(v, at=NI * gi)
                v.drain()
                v.tensor_tensor(rsc1[:, 0:w], qa_sb[:, o - 128:o - 128 + w],
                                cos_sb[:, o:o + w], OP.mult)
                v.tensor_tensor(rsc2[:, 0:w], bank[:, 0:w],
                                sin_sb[:, o:o + w], OP.mult)
                v.drain()
                s_rpq.inc(v.tensor_tensor(qrot[:, o - 128:o - 128 + w],
                                          rsc1[:, 0:w], rsc2[:, 0:w], OP.add))
            for tc, (o, w) in enumerate(KCH):
                bank = pb[2 + tc % 2]
                s_pmk.wait(v, at=gi * 5 + tc + 1)
                if gi > 0 and tc == 0:
                    s_sc.wait(v, at=NI * gi)
                v.drain()
                v.tensor_tensor(rsc1[:, 0:w], ka_sb[:, o:o + w],
                                cos_sb[:, o:o + w], OP.mult)
                v.tensor_tensor(rsc2[:, 0:w], bank[:, 0:w],
                                sin_sb[:, o:o + w], OP.mult)
                v.drain()
                s_rpk.inc(v.tensor_tensor(krot[:, o:o + w],
                                          rsc1[:, 0:w], rsc2[:, 0:w], OP.add))
            for i in range(NI):
                ii = gi * NI + i
                bank = pb[2 + i % 2]
                s_sc.wait(v, at=ii + 1)
                v.tensor_reduce(acols[:, 0 + i % 2:1 + i % 2], bank[:, 0:260],
                                AX.X, OP.max, negate=True)
                v.drain()
                s_rmax.inc(v.tensor_scalar(acols[:, 4 + i % 2:5 + i % 2],
                                           acols[:, 0 + i % 2:1 + i % 2],
                                           0.125, None, OP.mult))
                s_exp.wait(v, at=ii + 1)
                v.reciprocal(acols[:, 2 + i % 2:3 + i % 2],
                             acols[:, 6 + i % 2:7 + i % 2])
                v.drain()
                if ii >= 1:
                    s_pt.wait(v, at=ii - 1)
                s_pn.inc(v.tensor_scalar(pn_sb[i % 2][:, 0:260],
                                         p_sb[i % 2][:, 0:260],
                                         acols[:, 2 + i % 2:3 + i % 2],
                                         None, OP.mult))

    if "attn" in debug:
        @blk.sync
        def _(sy):
            s_ave.wait(sy)
            s_dbg.inc(sy.dma_start(out=dbg["d_ofm"].ap(), in_=O_fm[:]), 16)

    if "attn2" in debug:
        @blk.sync
        def _(sy):
            s_ave.wait(sy)
            s_dbg.inc(sy.dma_start(out=dbg["d_qrot"].ap(), in_=qrot[:]), 16)
            s_dbg.inc(sy.dma_start(out=dbg["d_krot"].ap(), in_=krot[:]), 16)
            s_dbg.inc(sy.dma_start(out=dbg["d_qa"].ap(), in_=qa_sb[:]), 16)
            s_dbg.inc(sy.dma_start(out=dbg["d_vtm"].ap()[0], in_=vtmg[0][:]), 16)
            s_dbg.inc(sy.dma_start(out=dbg["d_vtm"].ap()[1], in_=vtmg[1][:]), 16)

    esB.close()
    esA.close()

    NTOT = 8 * NI
    esC = ExitStack()

    def sbC(name, shape, dt):
        return esC.enter_context(nc.sbuf_tensor(name, list(shape), dt))

    xr2 = [sbC(f"xr2_{j}", (128, EMB), F32) for j in range(3)]
    y_sb = [sbC(f"y_sb{j}", (128, EMB), F32) for j in range(2)]
    xlt2 = [sbC(f"xlt2_{j}", (128, EMB), BF16) for j in range(2)]
    u_tm = [sbC(f"u_tm{j}", (128, EMB), BF16) for j in range(2)]
    ones128b = sbC("ones128b", (128, 128), BF16)
    ones128f = sbC("ones128f", (128, 128), F32)
    affw_sb = sbC("affw_sb", (128, 8, 8), BF16)
    dev_scr = sbC("dev_scr", (128, 128), F32)
    sq2_scr = sbC("sq2_scr", (128, 128), F32)
    rm_bc = sbC("rm_bc", (128, 8), F32)
    rankm_bf = sbC("rankm_bf", (1, 8), BF16)
    bis_scr = sbC("bis_scr", (128, 8, 128), F32)
    bis_mid = sbC("bis_mid", (128, 2, 8), F32)
    bis_cf = sbC("bis_cf", (128, 2, 8), F32)
    bis_cb = sbC("bis_cb", (128, 2, 8), BF16)
    bis_io = sbC("bis_io", (128, 8), F32)

    s_x2 = s_xload
    X2B = [sm.n for sm in s_x2]
    s_ysp = [sem(f"s_ysp{j}") for j in range(2)]
    s_yv = sem("s_yv")
    s_sq2 = sem("s_sq2")
    s_ln2 = sem("s_ln2")
    s_impw = sem("s_impw")
    s_u = sem("s_u")
    s_bv = [sem(f"s_bv{j}") for j in range(2)]
    s_bt = [sem(f"s_bt{j}") for j in range(2)]
    s_bhi = sem("s_bhi")
    s_tempc = sem("s_tempc")
    s_rmb = sem("s_rmb")
    s_mskc = sem("s_mskc")
    s_affm = sem("s_affm")
    s_selw = sem("s_selw")
    s_cc = sem("s_cc")

    @blk.gpsimd
    def _(g):
        s_ave.wait(g, at=NTOT)
        s_gwl.inc(g.dma_start(
            out=affw_sb[:, :, :],
            in_=D["affw"].ap().rearrange("(c p) e -> p c e", p=128)), 16)

    @blk.sync
    def _(sy):
        s_ave.wait(sy, at=NTOT)
        for ti in range(NOT + 2):
            if ti < NOT:
                if ti >= 3:
                    s_u.wait(sy, at=ti - 2)
                s_x2[ti % 3].inc(sy.dma_start(
                    out=xr2[ti % 3][:],
                    in_=D["xa"].ap()[128 * (ti + 1):128 * (ti + 2), :]), 16)
            if ti >= 2:
                tj = ti - 2
                s_yv.wait(sy, at=tj + 1)
                s_ysp[tj % 2].inc(sy.dma_start(
                    out=D["yspill"].ap()[128 * tj:128 * (tj + 1), :],
                    in_=y_sb[tj % 2][:]), 16)

    @blk.vector
    def _(v):
        s_ave.wait(v, at=NTOT)
        v.memset(ones128b[:], 1.0)
        v.memset(ones128f[:], 1.0)
        v.tensor_copy(rankm_bf[:], rankm_sb[:])
        for ti in range(NOT):
            cols = gcols[:, 10 * ti:10 * ti + 10]
            s_x2[ti % 3].wait(v, at=X2B[ti % 3] + 16 * (ti // 3 + 1))
            s_op.wait(v, at=2 * (ti + 1))
            if ti >= 2:
                s_ysp[ti % 2].wait(v, at=16 * (ti // 2))
            bank0 = pb[2 * (ti % 2)]
            bank1 = pb[2 * (ti % 2) + 1]
            v.tensor_tensor(y_sb[ti % 2][:, 0:512], xr2[ti % 3][:, 0:512],
                            bank0[:], OP.add)
            s_yv.inc(v.tensor_tensor(y_sb[ti % 2][:, 512:1024],
                                     xr2[ti % 3][:, 512:1024], bank1[:],
                                     OP.add))
            v.drain()
            s_ln1.inc(_ln_stats(v, y_sb[ti % 2][:], cols))
            s_sq2.wait(v, at=ti + 1)
            mn = cols[:, 0:1]; rstd = cols[:, 1:2]; t1 = cols[:, 3:4]
            imp2 = cols[:, 5:6]; L1c = cols[:, 6:7]; comb = cols[:, 7:8]
            v.tensor_scalar(t1, mn, FP(1.0 / EMB), None, OP.mult)
            v.tensor_tensor(imp2, rstd, rstd, OP.mult)
            v.drain()
            v.tensor_scalar(xlt2[ti % 2][:], y_sb[ti % 2][:], t1, rstd,
                            OP.add, OP.mult)
            s_ln2.inc(v.tensor_tensor(imp2, imp2, cols[:, 2:3], OP.mult))
            v.drain()
            v.tensor_reduce(L1c, xlt2[ti % 2][:], AX.X, OP.add,
                            apply_absolute_value=True)
            v.drain()
            v.tensor_scalar(L1c, L1c, 1.0, FP(1e-8), OP.mult, OP.add)
            v.drain()
            v.reciprocal(L1c, L1c)
            v.drain()
            v.tensor_tensor(comb, rstd, L1c, OP.mult)
            if ti >= 2:
                s_utp.wait(v, at=ti - 1)
            v.drain()
            s_u.inc(v.tensor_scalar(u_tm[ti % 2][:], y_sb[ti % 2][:], t1,
                                    comb, OP.add, OP.mult))

    @blk.scalar
    def _(s):
        for ti in range(NOT + 2):
            if ti < NOT:
                cols = gcols[:, 10 * ti:10 * ti + 10]
                s_ln1.wait(s, at=NXT + ti + 1)
                s_sq2.inc(s.activation(cols[:, 1:2], cols[:, 1:2], AF.Sqrt))
                s_ln2.wait(s, at=ti + 1)
                s_impw.inc(s.activation(xchg[:, ti:ti + 1], cols[:, 5:6],
                                        AF.Sqrt))
            if ti >= 2:
                j = ti - 2
                s_utp.wait(s, at=j + 1)
                bank = (pt0 if j % 2 == 0 else pt1)
                s_ue.inc(s.copy(u_fm[:, :, 128 * j:128 * (j + 1)],
                                bank[:].rearrange("p (c j) -> p c j", c=8)))

    @blk.tensor
    def _(t):
        s_gload.wait(t)
        s_ave.wait(t, at=NTOT)
        for ti in range(NOT + 2):
            if ti < NOT:
                for ec in range(2):
                    bank = pb[2 * (ti % 2) + ec]
                    if ti >= 2:
                        s_yv.wait(t, at=ti - 1)
                    for k in range(8):
                        ins = t.matmul(bank[:],
                                       O_fm[:, k, 128 * ti:128 * (ti + 1)],
                                       outw_sb[:, k, 512 * ec:512 * (ec + 1)],
                                       start=(k == 0), stop=(k == 7))
                    s_op.inc(ins)
            if ti >= 2:
                tj = ti - 2
                t.wait_ge(s_u.h, tj + 1)
                bank = (pt0 if tj % 2 == 0 else pt1)
                if tj >= 2:
                    s_ue.wait(t, at=tj - 1)
                for j in range(8):
                    ins = t.transpose(bank[:, 128 * j:128 * (j + 1)],
                                      u_tm[tj % 2][:, 128 * j:128 * (j + 1)],
                                      identb[:])
                s_utp.inc(ins)
        s_gwl.wait(t)
        for ti in range(NOT):
            s_ue.wait(t, at=ti + 1)
            for k in range(8):
                ins = t.matmul(pb[4][:, 8 * ti:8 * ti + 8],
                               u_fm[:, k, 128 * ti:128 * (ti + 1)],
                               affw_sb[:, k, :],
                               start=(k == 0 and ti == 0),
                               stop=(k == 7 and ti == NOT - 1))
        s_qtm.inc(ins)

    @blk.scalar
    def _(s):
        s_qtm.wait(s)
        s_qtm.inc(s.copy(xchg[:, 16:144], pb[4][:, 0:128]))

    @blk.gpsimd
    def _(g):
        s_impw.wait(g, at=NOT)
        s_qtm.wait(g, at=2)
        g.dma_start(out=D["agin"].ap(), in_=xchg[:]).then_inc(s_ag.h, 16)
        s_ag.n += 16
        g.wait_ge(s_ag.h, 16)
        g.collective_compute(
            "AllGather", OP.bypass, replica_groups=[list(range(8))],
            ins=[D["agin"].ap()], outs=[D["agout"].ap()],
        ).then_inc(s_cc.h, 1)
        s_cc.n += 1
        g.wait_ge(s_cc.h, 1)
        s_agl.inc(g.dma_start(
            out=gath[:, :, :],
            in_=D["agout"].ap().rearrange("(r p) c -> p r c", p=128)), 16)

    BIS_R = 14
    B_LO = (SC_LO, 16)
    B_HI = (SC_HI, 17)
    B_W = (SC_MID, 18)
    B_NC = (SC_PRED, 19)
    B_TB = 20
    B_TB2 = 21

    @blk.tensor
    def _(t):
        s_bhi.wait(t, at=1)
        s_bhi.inc(t.transpose(pb[5][0:1, 0:128], scal[:, B_TB:B_TB + 1],
                              identf[:]))
        s_bhi.wait(t, at=3)
        s_bhi.inc(t.matmul(pb[5][:, 8:9], ones128f[0:1, :],
                           scal[0:1, B_TB2:B_TB2 + 1], start=True, stop=True))
        for r in range(BIS_R):
            s_bv[0].wait(t, at=r + 1)
            s_bt[0].inc(t.matmul(
                pb[5][:, 16 + 8 * (r % 2):16 + 8 * (r % 2) + 8],
                ones128b[:], bis_cb[:, 0, :], start=True, stop=True))
            s_bv[1].wait(t, at=2 * (r + 1))
            s_bt[1].inc(t.matmul(
                pb[5][:, 40 + 8 * (r % 2):40 + 8 * (r % 2) + 8],
                ones128b[:], bis_cb[:, 1, :], start=True, stop=True))
        s_tempc.wait(t, at=1)
        s_tempc.inc(t.matmul(pb[5][:, 4:5], ones128f[:],
                             scal[:, SC_CNT:SC_CNT + 1], start=True,
                             stop=True))
        s_tempc.wait(t, at=3)
        s_tempc.inc(t.matmul(pb[5][:, 5:6], ones128f[:],
                             scal[:, SC_CNT:SC_CNT + 1], start=True,
                             stop=True))
        s_bhi.wait(t, at=1)
        s_qtm.wait(t, at=2)
        s_rmb.inc(t.matmul(pb[4][:, 8:16], ones1[0:1, :], rankm_bf[0:1, :],
                           start=True, stop=True))
        s_mskc.wait(t, at=1)
        for r in range(8):
            for ti in range(NOT):
                ins = t.matmul(pb[5][0:8, 32:33],
                               gath[:, r, 16 + 8 * ti:16 + 8 * (ti + 1)],
                               dev_scr[:, 16 * r + ti:16 * r + ti + 1],
                               start=(r == 0 and ti == 0),
                               stop=(r == 7 and ti == NOT - 1))
        s_affm.inc(ins)

    @blk.vector
    def _(v):
        s_agl.wait(v)
        v.tensor_scalar(bis_io[:], idx8[:], 1.0, None, OP.add)
        v.drain()
        v.tensor_reduce(scal[:, B_TB:B_TB + 1], gath[:, :, 0:16], AX.XY,
                        OP.max)
        v.drain()
        s_bhi.inc(v.tensor_copy(scal[:, B_TB2:B_TB2 + 1],
                                scal[:, B_TB:B_TB + 1]))
        s_bhi.wait(v, at=2)
        v.tensor_reduce(scal[0:1, B_TB2:B_TB2 + 1], pb[5][0:1, 0:128], AX.X,
                        OP.max)
        v.drain()
        s_bhi.inc(v.tensor_copy(scal[0:1, B_TB:B_TB + 1],
                                scal[0:1, B_TB2:B_TB2 + 1]))
        s_bhi.wait(v, at=4)
        for srch in range(2):
            v.tensor_copy(scal[:, B_HI[srch]:B_HI[srch] + 1], pb[5][:, 8:9])
            v.memset(scal[:, B_LO[srch]:B_LO[srch] + 1], 0.0)
        v.drain()
        for srch in range(2):
            v.tensor_scalar(scal[:, B_W[srch]:B_W[srch] + 1],
                            scal[:, B_HI[srch]:B_HI[srch] + 1], 0.125, None,
                            OP.mult)
        for r in range(BIS_R):
            for srch in range(2):
                LO = scal[:, B_LO[srch]:B_LO[srch] + 1]
                HI = scal[:, B_HI[srch]:B_HI[srch] + 1]
                W = scal[:, B_W[srch]:B_W[srch] + 1]
                NC = scal[:, B_NC[srch]:B_NC[srch] + 1]
                v.drain()
                v.tensor_scalar(bis_mid[:, srch, :], bis_io[:], W, LO,
                                OP.mult, OP.add)
                v.drain()
                for m in range(8):
                    v.tensor_scalar(bis_scr[:, m, :], gath[:, :, 0:16],
                                    bis_mid[:, srch, m:m + 1], None,
                                    OP.is_le, OP.add,
                                    accum_out=bis_cf[:, srch, m:m + 1])
                v.drain()
                ins = v.tensor_copy(bis_cb[:, srch, :], bis_cf[:, srch, :])
                s_bv[srch].inc(ins)
            for srch in range(2):
                LO = scal[:, B_LO[srch]:B_LO[srch] + 1]
                HI = scal[:, B_HI[srch]:B_HI[srch] + 1]
                W = scal[:, B_W[srch]:B_W[srch] + 1]
                NC = scal[:, B_NC[srch]:B_NC[srch] + 1]
                KV = K1 if srch == 0 else K2
                base = 16 if srch == 0 else 40
                reg = pb[5][:, base + 8 * (r % 2):base + 8 * (r % 2) + 8]
                TB = scal[:, B_TB:B_TB + 1]
                TB2 = scal[:, B_TB2:B_TB2 + 1]
                TMP = scal[:, SC_TMP:SC_TMP + 1]
                s_bt[srch].wait(v, at=r + 1)
                v.tensor_scalar(bis_scr[:, 0, 0:8], reg, float(KV), None,
                                OP.is_lt, OP.add, accum_out=NC)
                v.drain()
                v.tensor_scalar(TB, NC, 1.0, None, OP.add)
                v.drain()
                v.tensor_scalar(TB, TB, W, LO, OP.mult, OP.add)
                v.tensor_scalar(TB2, NC, 8.0, None, OP.is_ge)
                v.drain()
                v.tensor_tensor(TMP, HI, TB, OP.subtract)
                v.drain()
                v.scalar_tensor_tensor(HI, TMP, TB2, TB, OP.mult, OP.add)
                v.tensor_scalar(LO, NC, W, LO, OP.mult, OP.add)
                v.drain()
                v.tensor_tensor(W, HI, LO, OP.subtract)
                v.drain()
                ins = v.tensor_scalar(W, W, 0.125, None, OP.mult)
                if srch == 0:
                    s_bv[1].inc(ins)
        v.drain()
        v.tensor_copy(scal[:, SC_S1:SC_S1 + 1], scal[:, B_HI[0]:B_HI[0] + 1])
        v.tensor_copy(scal[:, SC_S2:SC_S2 + 1], scal[:, B_HI[1]:B_HI[1] + 1])
        v.drain()
        v.tensor_scalar(scal[:, SC_T1:SC_T1 + 1], scal[:, SC_S1:SC_S1 + 1],
                        float(_LW), None, OP.mult)
        v.tensor_scalar(scal[:, SC_T2:SC_T2 + 1], scal[:, SC_S2:SC_S2 + 1],
                        float(_HW), None, OP.mult)
        v.drain()
        v.tensor_tensor(scal[:, SC_THR:SC_THR + 1], scal[:, SC_T1:SC_T1 + 1],
                        scal[:, SC_T2:SC_T2 + 1], OP.add)
        v.drain()
        v.tensor_scalar(dev_scr[:], gath[:, :, 0:16], 1.0, None,
                        OP.mult, OP.add, accum_out=scal[:, SC_T1:SC_T1 + 1])
        v.drain()
        s_tempc.inc(v.tensor_copy(scal[:, SC_CNT:SC_CNT + 1],
                                  scal[:, SC_T1:SC_T1 + 1]))
        s_tempc.wait(v, at=2)
        v.tensor_scalar(scal[:, SC_TMP:SC_TMP + 1], pb[5][:, 4:5],
                        FP(1.0 / 16384.0), None, OP.mult)
        v.drain()
        v.tensor_scalar(dev_scr[:], gath[:, :, 0:16],
                        scal[:, SC_TMP:SC_TMP + 1], None, OP.subtract)
        v.drain()
        v.tensor_tensor(dev_scr[:], dev_scr[:], dev_scr[:], OP.mult)
        v.drain()
        v.tensor_scalar(sq2_scr[:], dev_scr[:], 1.0, None, OP.mult,
                        OP.add, accum_out=scal[:, SC_T2:SC_T2 + 1])
        v.drain()
        s_tempc.inc(v.tensor_copy(scal[:, SC_CNT:SC_CNT + 1],
                                  scal[:, SC_T2:SC_T2 + 1]))
        s_tempc.wait(v, at=4)
        s_tempc.inc(v.tensor_scalar(scal[:, SC_TMP2:SC_TMP2 + 1],
                                    pb[5][:, 5:6], FP(1.0 / 16384.0), None,
                                    OP.mult))
        s_tempc.wait(v, at=6)
        v.tensor_scalar(scal[:, SC_RTEMP:SC_RTEMP + 1],
                        scal[:, SC_TMP2:SC_TMP2 + 1], 0.1, 1.0, OP.mult,
                        OP.add)
        v.drain()
        v.reciprocal(scal[:, SC_RTEMP:SC_RTEMP + 1],
                     scal[:, SC_RTEMP:SC_RTEMP + 1])
        v.drain()
        v.tensor_scalar(scal[:, 32:48], xchg[:, 0:16],
                        scal[:, SC_THR:SC_THR + 1], None, OP.is_gt)
        v.drain()
        ins = v.tensor_scalar(scal[:, 32:48], scal[:, 32:48],
                              scal[:, SC_RTEMP:SC_RTEMP + 1], None, OP.mult)
        s_selw.inc(ins)
        s_rmb.wait(v, at=2)
        for r in range(8):
            v.tensor_scalar(dev_scr[:, 16 * r:16 * r + 16],
                            gath[:, r, 0:16], scal[:, SC_THR:SC_THR + 1],
                            None, OP.is_gt)
        v.drain()
        for r in range(8):
            ins = v.tensor_scalar(dev_scr[:, 16 * r:16 * r + 16],
                                  dev_scr[:, 16 * r:16 * r + 16],
                                  rm_bc[:, r:r + 1], None, OP.mult)
        s_mskc.inc(ins)

    @blk.scalar
    def _(s):
        s_tempc.wait(s, at=5)
        s_tempc.inc(s.activation(scal[:, SC_TMP2:SC_TMP2 + 1],
                                 scal[:, SC_TMP2:SC_TMP2 + 1], AF.Sqrt))
        s_rmb.wait(s, at=1)
        s_rmb.inc(s.copy(rm_bc[:], pb[4][:, 8:16]))
        s_affm.wait(s, at=1)
        s_affm.inc(s.copy(scal[0:8, SC_AFF:SC_AFF + 1], pb[5][0:8, 32:33]))

    esC.close()
    esL1b.close()
    esD0 = ExitStack()

    def sbD0(name, shape, dt):
        return esD0.enter_context(nc.sbuf_tensor(name, list(shape), dt))

    logits_tm = sbD0("logits_tm", (128, 16, 8), F32)
    w_tm = sbD0("w_tm", (128, 16, 8), F32)
    tk8a = sbD0("tk8a", (128, 16, 8), F32)
    tk8b = sbD0("tk8b", (128, 16, 8), F32)
    tk8c = sbD0("tk8c", (128, 16, 8), F32)
    tkm_a = sbD0("tkm_a", (128, 16), F32)
    tkm_b = sbD0("tkm_b", (128, 16), F32)
    tkm_c = sbD0("tkm_c", (128, 16), F32)
    srow = sbD0("srow", (1, 16), F32)
    e8c = sbD0("e8c", (8, 4), F32)
    n128 = sbD0("n128", (128, 1), F32)
    ones1f = sbD0("ones1f", (1, 128), F32)
    selflat = sbD0("selflat", (1, 2048), F32)

    esD = ExitStack()

    def sbD(name, shape, dt):
        return esD.enter_context(nc.sbuf_tensor(name, list(shape), dt))

    gw1_sb = sbD("gw1_sb", (128, 8, 1024), BF16)
    gw2_sb = sbD("gw2_sb", (128, 8, 512), BF16)
    gw3_sb = sbD("gw3_sb", (128, 4, 8), BF16)
    g1_sb = sbD("g1_sb", (128, 8, TOK), BF16)
    g2_sb = sbD("g2_sb", (128, 4, TOK), BF16)
    sqg_sb = sbD("sqg_sb", (128, 8, 512), BF16)
    gscr = sbD("gscr", (128, 512), F32)
    gscr2 = sbD("gscr2", (128, 512), F32)
    gscr3 = sbD("gscr3", (128, 512), F32)
    gscr4 = sbD("gscr4", (128, 512), F32)
    grows = sbD("grows", (1, 2048), F32)
    lrow = sbD("lrow", (8, 512), F32)

    s_sst = sem("s_sst")
    s_sse = sem("s_sse")
    s_ssb = sem("s_ssb")
    s_xn = sem("s_xn")
    s_gwx = sem("s_gwx")
    s_g1 = sem("s_g1")
    s_g1e = sem("s_g1e")
    s_gsq = sem("s_gsq")
    s_gl = sem("s_gl")
    s_g2 = sem("s_g2")
    s_g2e = sem("s_g2e")
    s_g3 = sem("s_g3")
    s_lgr = sem("s_lgr")
    s_ltt = sem("s_ltt")
    s_ltm = sem("s_ltm")
    s_ns = sem("s_ns")
    s_tk = sem("s_tk")
    s_tke = sem("s_tke")

    @blk.gpsimd
    def _(g):
        s_affm.wait(g, at=2)
        s_mskc.wait(g, at=1)
        s_tempc.wait(g, at=6)
        s_selw.wait(g, at=1)
        for j in range(2):
            s_ysp[j].wait(g)
        for j in range(3):
            s_x2[j].wait(g)
        s_agl.wait(g)
        s_utp.wait(g, at=NOT)
        s_ue.wait(g, at=NOT)
        s_gwx.inc(g.dma_start(
            out=gw1_sb[:, :, :],
            in_=D["gw1"].ap().rearrange("(c p) e -> p c e", p=128)), 16)
        s_gwx.inc(g.dma_start(
            out=gw2_sb[:, :, :],
            in_=D["gw2"].ap().rearrange("(c p) e -> p c e", p=128)), 16)
        s_gwx.inc(g.dma_start(
            out=gw3_sb[:, :, :],
            in_=D["gw3"].ap().rearrange("(c p) e -> p c e", p=128)), 16)
        for snm, ei in (("e0w2", 0), ("e1w3", 0), ("e2w2", 0), ("e0w2", 1),
                        ("e1w3", 1), ("e2w2", 1), ("e0w2", 2), ("e1w3", 2)):
            R = {"e0w2": 4096, "e1w3": 2048, "e2w2": 3072}[snm]
            for ch in range(R // 512):
                s_wcast.inc(g.dma_start(
                    out=D["c_" + snm].ap()[ei][512 * ch:512 * ch + 512, :],
                    in_=D[snm].ap()[ei][512 * ch:512 * ch + 512, :]), 16)

    @blk.vector
    def _(v):
        s_affm.wait(v, at=2)
        v.memset(ones1f[:], 1.0)
        v.tensor_scalar(srow[:, 5:13], usage_sb[:, :], 1.0, FP(1e-8),
                        OP.mult, OP.add)
        v.drain()
        v.tensor_reduce(srow[:, 4:5], srow[:, 5:13], AX.X, OP.min)
        v.tensor_scalar(srow[:, 5:13], srow[:, 5:13], FP(1e-18), None,
                        OP.max)
        v.drain()
        s_ns.inc(v.tensor_scalar(srow[:, 0:1], srow[:, 4:5], 0.0, None,
                                 OP.is_ge))

    @blk.scalar
    def _(s):
        s_ns.wait(s, at=1)
        s_ns.inc(s.activation(srow[:, 5:13], srow[:, 5:13], AF.Ln))

    @blk.vector
    def _(v):
        s_ns.wait(v, at=2)
        v.tensor_tensor(srow[:, 5:13], usage_sb[:, :], srow[:, 5:13], OP.mult)
        v.drain()
        v.tensor_reduce(srow[:, 3:4], srow[:, 5:13], AX.X, OP.add,
                        negate=True)
        v.drain()
        v.tensor_scalar(srow[:, 2:3], srow[:, 3:4], -0.01, 0.02,
                        OP.mult, OP.add)
        v.drain()
        v.tensor_scalar(srow[:, 2:3], srow[:, 2:3], FP(1e-4), None, OP.max)
        v.drain()
        v.tensor_tensor(srow[:, 1:2], srow[:, 2:3], srow[:, 0:1], OP.mult)
        v.drain()
        s_ns.inc(v.tensor_scalar(srow[:, 1:2], srow[:, 1:2], FP(1.0 / 1.5),
                                 None, OP.mult))

    @blk.tensor
    def _(t):
        s_ns.wait(t, at=3)
        t.matmul(pb[4][0:8, 0:1], ones1f[0:1, 0:8], srow[:, 0:1],
                 start=True, stop=True)
        t.matmul(pb[4][0:8, 1:2], ones1f[0:1, 0:8], srow[:, 1:2],
                 start=True, stop=True)
        s_ns.inc(t.matmul(pb[4][0:128, 2:3], ones1f[0:1, :], srow[:, 1:2],
                          start=True, stop=True))

    @blk.scalar
    def _(s):
        s_ns.wait(s, at=4)
        s.copy(e8c[0:8, 0:2], pb[4][0:8, 0:2])
        s_ns.inc(s.copy(n128[:, 0:1], pb[4][0:128, 2:3]))

    @blk.vector
    def _(v):
        s_ns.wait(v, at=5)
        v.tensor_scalar(e8c[0:8, 2:3], gb3_sb[0:8, 0:1], FP(1.0 / 1.5), None,
                        OP.mult)
        v.drain()
        v.scalar_tensor_tensor(e8c[0:8, 2:3], scal[0:8, SC_AFF:SC_AFF + 1],
                               FP(0.3 / 1.5), e8c[0:8, 2:3], OP.mult, OP.add)
        v.drain()
        v.tensor_tensor(e8c[0:8, 2:3], e8c[0:8, 2:3], e8c[0:8, 0:1], OP.mult)
        s_ns.inc(v.tensor_scalar(e8c[0:8, 3:4], e8c[0:8, 0:1], FP(1.0 / 1.5),
                                 None, OP.mult))

    @blk.tensor
    def _(t):
        s_selw.wait(t, at=1)
        s_qtm.wait(t, at=2)
        s_ns.wait(t, at=5)
        s_affm.wait(t, at=2)
        s_tempc.wait(t, at=5)
        s_rmb.wait(t, at=2)
        for ti in range(NOT):
            bank = pb[4 + ti % 2]
            if ti >= 2:
                s_sse.wait(t, at=ti - 1)
                s_xn.wait(t, at=ti - 1)
            s_sst.inc(t.transpose(bank[0:1, 0:128],
                                  scal[:, 32 + ti:33 + ti], identf[:]))
            s_sse.wait(t, at=ti + 1)
            s_ssb.inc(t.matmul(bank[:, 0:128], ones1f[0:1, :],
                               selflat[0:1, 128 * ti:128 * ti + 128],
                               start=True, stop=True))

    @blk.scalar
    def _(s):
        for ti in range(NOT):
            s_sst.wait(s, at=ti + 1)
            s_sse.inc(s.copy(selflat[0:1, 128 * ti:128 * ti + 128],
                             pb[4 + ti % 2][0:1, 0:128]))

    @blk.vector
    def _(v):
        for ti in range(NOT):
            s_ssb.wait(v, at=ti + 1)
            for j in range(8):
                ins = v.tensor_tensor(
                    u_fm[:, j, 128 * ti:128 * (ti + 1)],
                    u_fm[:, j, 128 * ti:128 * (ti + 1)],
                    pb[4 + ti % 2][:, 0:128], OP.mult)
            s_xn.inc(ins)

    @blk.tensor
    def _(t):
        s_gwx.wait(t, at=48)
        s_xn.wait(t, at=NOT)
        for tc in range(4):
            for oc in range(8):
                n = 8 * tc + oc
                bank = pb[n % 4]
                if n >= 4:
                    s_g1e.wait(t, at=n - 3)
                for k in range(8):
                    ins = t.matmul(bank[:], gw1_sb[:, k, 128 * oc:128 * oc + 128],
                                   u_fm[:, k, 512 * tc:512 * tc + 512],
                                   start=(k == 0), stop=(k == 7))
                s_g1.inc(ins)

    @blk.scalar
    def _(s):
        s_gload.wait(s)
        for tc in range(4):
            for oc in range(8):
                n = 8 * tc + oc
                s_g1.wait(s, at=n + 1)
                s_g1e.inc(s.activation(
                    g1_sb[:, oc, 512 * tc:512 * tc + 512], pb[n % 4][:],
                    AF.Silu, bias=biases[:, BOFF["gb1"] + oc:BOFF["gb1"] + oc + 1]))

    @blk.vector
    def _(v):
        s_gload.wait(v)
        for tc in range(4):
            s_gl.wait(v, at=5 * tc + 1)
            v.tensor_scalar(grows[0:1, 0:512], pb[4][0:1, 0:512],
                            FP(1.0 / 1024.0), None, OP.mult)
            v.tensor_scalar(grows[0:1, 512:1024], pb[5][0:1, 0:512],
                            FP(1.0 / 1024.0), None, OP.mult)
            v.drain()
            v.tensor_tensor(grows[0:1, 1024:1536], grows[0:1, 0:512],
                            grows[0:1, 0:512], OP.mult)
            v.drain()
            v.tensor_tensor(grows[0:1, 512:1024], grows[0:1, 512:1024],
                            grows[0:1, 1024:1536], OP.subtract)
            v.drain()
            v.tensor_scalar(grows[0:1, 512:1024], grows[0:1, 512:1024], 1.0,
                            LN_EPS, OP.mult, OP.add)
            v.drain()
            s_gl.inc(v.reciprocal(grows[0:1, 512:1024],
                                  grows[0:1, 512:1024]))
            s_gl.wait(v, at=5 * tc + 4)
            scr4 = (gscr, gscr2, gscr3, gscr4)
            for jq in range(2):
                for ji in range(4):
                    j = 4 * jq + ji
                    v.tensor_tensor(scr4[ji][:],
                                    g1_sb[:, j, 512 * tc:512 * tc + 512],
                                    pb[4][:, 0:512], OP.subtract)
                v.drain()
                for ji in range(4):
                    v.tensor_tensor(scr4[ji][:], scr4[ji][:],
                                    pb[5][:, 0:512], OP.mult)
                v.drain()
                for ji in range(4):
                    j = 4 * jq + ji
                    ins = v.tensor_scalar(
                        g1_sb[:, j, 512 * tc:512 * tc + 512], scr4[ji][:],
                        biases[:, BOFF["lnsc"] + j:BOFF["lnsc"] + j + 1],
                        biases[:, BOFF["lnof"] + j:BOFF["lnof"] + j + 1],
                        OP.mult, OP.add)
                v.drain()
            s_gl.inc(ins)

    @blk.scalar
    def _(s):
        for tc in range(4):
            s_g1e.wait(s, at=8 * (tc + 1))
            if tc >= 1:
                s_gl.wait(s, at=5 * (tc - 1) + 1)
            for j in range(8):
                ins = s.activation(sqg_sb[:, j, :],
                                   g1_sb[:, j, 512 * tc:512 * tc + 512],
                                   AF.Square)
            s_gsq.inc(ins)
            s_gl.wait(s, at=5 * tc + 2)
            s_gl.inc(s.activation(grows[0:1, 512:1024],
                                  grows[0:1, 512:1024], AF.Sqrt))

    @blk.tensor
    def _(t):
        for tc in range(4):
            s_g1e.wait(t, at=8 * (tc + 1))
            s_gsq.wait(t, at=tc + 1)
            if tc >= 1:
                s_gl.wait(t, at=5 * tc)
            for j in range(8):
                t.matmul(pb[4][0:1, 0:512], onesc[:, 0:1],
                         g1_sb[:, j, 512 * tc:512 * tc + 512],
                         start=(j == 0), stop=(j == 7))
            for j in range(8):
                ins = t.matmul(pb[5][0:1, 0:512], onesc[:, 0:1],
                               sqg_sb[:, j, :], start=(j == 0), stop=(j == 7))
            s_gl.inc(ins)
            s_gl.wait(t, at=5 * tc + 3)
            t.matmul(pb[4][:, 0:512], ones1f[0:1, :], grows[0:1, 0:512],
                     start=True, stop=True)
            ins = t.matmul(pb[5][:, 0:512], ones1f[0:1, :],
                           grows[0:1, 512:1024], start=True, stop=True)
            s_gl.inc(ins)

    @blk.tensor
    def _(t):
        for tc in range(4):
            s_gl.wait(t, at=5 * tc + 5)
            for oc in range(4):
                n = 4 * tc + oc
                bank = pb[n % 4]
                if n >= 4:
                    s_g2e.wait(t, at=n - 3)
                for k in range(8):
                    ins = t.matmul(bank[:],
                                   gw2_sb[:, k, 128 * oc:128 * oc + 128],
                                   g1_sb[:, k, 512 * tc:512 * tc + 512],
                                   start=(k == 0), stop=(k == 7))
                s_g2.inc(ins)

    @blk.scalar
    def _(s):
        for tc in range(4):
            for oc in range(4):
                n = 4 * tc + oc
                s_g2.wait(s, at=n + 1)
                s_g2e.inc(s.activation(
                    g2_sb[:, oc, 512 * tc:512 * tc + 512], pb[n % 4][:],
                    AF.Silu,
                    bias=biases[:, BOFF["gb2"] + oc:BOFF["gb2"] + oc + 1]))

    @blk.tensor
    def _(t):
        s_ns.wait(t, at=6)
        s_g2e.wait(t, at=16)
        for tc in range(4):
            bank = pb[tc % 2]
            if tc >= 2:
                s_lgr.wait(t, at=tc - 1)
            for k in range(4):
                ins = t.matmul(bank[0:8, 0:512], gw3_sb[:, k, :],
                               g2_sb[:, k, 512 * tc:512 * tc + 512],
                               start=(k == 0), stop=(k == 3))
            s_g3.inc(ins)
            s_lgr.wait(t, at=tc + 1)
            for q in range(4):
                ti = 4 * tc + q
                if ti >= 2:
                    s_ltm.wait(t, at=ti - 1)
                s_ltt.inc(t.transpose(
                    pb[2 + ti % 2][0:128, 0:8],
                    lrow[0:8, 128 * q:128 * q + 128], identf[0:8, 0:8]))

    @blk.vector
    def _(v):
        for tc in range(4):
            s_g3.wait(v, at=tc + 1)
            s_lgr.inc(v.tensor_scalar(lrow[0:8, :], pb[tc % 2][0:8, 0:512],
                                      e8c[0:8, 3:4], e8c[0:8, 2:3],
                                      OP.mult, OP.add))
            for q in range(4):
                ti = 4 * tc + q
                s_ltt.wait(v, at=ti + 1)
                s_ltm.inc(v.scalar_tensor_tensor(
                    logits_tm[:, ti, :], noise_sb[:, ti, :], n128[:, 0:1],
                    pb[2 + ti % 2][0:128, 0:8], OP.mult, OP.add))

    @blk.vector
    def _(v):
        s_ltm.wait(v, at=NOT)
        L3 = logits_tm
        v.drain()
        v.tensor_reduce(tkm_a[:], L3[:, :, :], AX.X, OP.max)
        v.drain()
        for e in range(8):
            v.tensor_tensor(tk8a[:, :, e:e + 1], L3[:, :, e:e + 1],
                            tkm_a[:], OP.is_equal)
        v.drain()
        for e in range(8):
            v.tensor_scalar(tk8b[:, :, e:e + 1], tk8a[:, :, e:e + 1],
                            float(100 - e), None, OP.mult)
        v.drain()
        for e in range(8):
            v.tensor_scalar(tk8b[:, :, e:e + 1], tk8b[:, :, e:e + 1],
                            -1.0, 100.0, OP.mult, OP.add)
        v.drain()
        v.tensor_reduce(tkm_b[:], tk8b[:, :, :], AX.X, OP.min)
        v.drain()
        for e in range(8):
            v.tensor_scalar(tk8a[:, :, e:e + 1], tkm_b[:], float(e), None,
                            OP.is_equal)
        v.drain()
        for e in range(8):
            v.scalar_tensor_tensor(tk8b[:, :, e:e + 1], tk8a[:, :, e:e + 1],
                                   -3e38, L3[:, :, e:e + 1],
                                   OP.mult, OP.add)
        v.drain()
        v.tensor_reduce(tkm_c[:], tk8b[:, :, :], AX.X, OP.max)
        v.drain()
        for e in range(8):
            v.tensor_tensor(tk8b[:, :, e:e + 1], tk8b[:, :, e:e + 1],
                            tkm_c[:], OP.is_equal)
        v.drain()
        for e in range(8):
            v.tensor_scalar(tk8b[:, :, e:e + 1], tk8b[:, :, e:e + 1],
                            float(100 - e), None, OP.mult)
        v.drain()
        for e in range(8):
            v.tensor_scalar(tk8b[:, :, e:e + 1], tk8b[:, :, e:e + 1],
                            -1.0, 100.0, OP.mult, OP.add)
        v.drain()
        v.tensor_reduce(tkm_c[:], tk8b[:, :, :], AX.X, OP.min)
        v.drain()
        for e in range(8):
            v.tensor_scalar(tk8b[:, :, e:e + 1], tkm_c[:], float(e), None,
                            OP.is_equal)
        v.drain()
        for e in range(8):
            v.tensor_tensor(tk8c[:, :, e:e + 1], tk8a[:, :, e:e + 1],
                            tk8b[:, :, e:e + 1], OP.add)
        for e in range(8):
            v.tensor_tensor(tk8b[:, :, e:e + 1], tk8b[:, :, e:e + 1],
                            L3[:, :, e:e + 1], OP.mult)
        v.drain()
        v.tensor_reduce(tkm_c[:], tk8b[:, :, :], AX.X, OP.add)
        v.drain()
        for e in range(8):
            v.tensor_tensor(tk8a[:, :, e:e + 1], L3[:, :, e:e + 1],
                            tkm_c[:], OP.is_ge)
        v.drain()
        for e in range(8):
            v.tensor_tensor(tk8b[:, :, e:e + 1], tk8a[:, :, e:e + 1],
                            L3[:, :, e:e + 1], OP.mult)
        for e in range(8):
            v.tensor_scalar(tk8a[:, :, e:e + 1], tk8a[:, :, e:e + 1],
                            FP(1e9), FP(-1e9), OP.mult, OP.add)
        v.drain()
        for e in range(8):
            v.tensor_tensor(tk8b[:, :, e:e + 1], tk8b[:, :, e:e + 1],
                            tk8a[:, :, e:e + 1], OP.add)
        v.drain()
        v.tensor_reduce(tkm_a[:], tk8b[:, :, :], AX.X, OP.max)
        v.drain()
        for e in range(8):
            ins = v.tensor_tensor(tk8b[:, :, e:e + 1], tk8b[:, :, e:e + 1],
                                  tkm_a[:], OP.subtract)
        s_tk.inc(ins)
        s_tke.wait(v, at=1)
        v.tensor_reduce(tkm_b[:], tk8a[:, :, :], AX.X, OP.add)
        v.drain()
        v.reciprocal(tkm_b[:], tkm_b[:])
        v.drain()
        for e in range(8):
            v.tensor_tensor(tk8a[:, :, e:e + 1], tk8a[:, :, e:e + 1],
                            tkm_b[:], OP.mult)
        v.drain()
        for e in range(8):
            ins = v.tensor_tensor(w_tm[:, :, e:e + 1], tk8a[:, :, e:e + 1],
                                  tk8c[:, :, e:e + 1], OP.mult)
        s_tk.inc(ins)

    @blk.scalar
    def _(s):
        s_tk.wait(s, at=1)
        s_tke.inc(s.activation(tk8a[:, :, :], tk8b[:, :, :], AF.Exp))

    if "gate" in debug:
        @blk.sync
        def _(sy):
            s_ltm.wait(sy)
            s_dbg.inc(sy.dma_start(out=dbg["d_logits"].ap(),
                                   in_=logits_tm[:]), 16)
            s_tk.wait(sy)
            s_dbg.inc(sy.dma_start(out=dbg["d_w"].ap(), in_=w_tm[:]), 16)
            s_dbg.inc(sy.dma_start(out=dbg["d_sel2"].ap(), in_=tk8c[:]), 16)
            s_dbg.inc(sy.dma_start(out=dbg["d_i1"].ap(), in_=tkm_b[:]), 16)
            s_dbg.inc(sy.dma_start(out=dbg["d_i2"].ap(), in_=tkm_c[:]), 16)
            s_xn.wait(sy)
            s_dbg.inc(sy.dma_start(out=dbg["d_xnorm"].ap(), in_=u_fm[:]), 16)

    esD.close()
    NSLOT = 256
    esE0 = ExitStack()

    def sbE0(name, shape, dt):
        return esE0.enter_context(nc.sbuf_tensor(name, list(shape), dt))

    grankm = sbE0("grankm", (128, 16), F32)
    ecst = sbE0("ecst", (128, 8, 8), F32)
    b2mx = sbE0("b2mx", (128, 8, 8), F32)
    econT = sbE0("econT", (8, 1024), F32)
    b2T = sbE0("b2T", (8, 1024), F32)
    wsl_tm = sbE0("wsl_tm", (128, 2, 8), F32)

    esEa = ExitStack()

    def sbEa(name, shape, dt):
        return esEa.enter_context(nc.sbuf_tensor(name, list(shape), dt))

    x_tm = sbEa("x_tm", (128, 16, 1024), BF16)
    m_bf = sbEa("m_bf", (128, 16), BF16)
    m_f = sbEa("m_f", (128, 16), F32)
    cnt8p = sbEa("cnt8p", (128, 8), F32)
    cnt8b = sbEa("cnt8b", (128, 8), BF16)
    cntc_f = sbEa("cntc_f", (8, 1), F32)
    cntrow = sbEa("cntrow", (1, 8), F32)
    totc = sbEa("totc", (16, 1), BF16)
    prefc = sbEa("prefc", (16, 1), F32)
    tioff = sbEa("tioff", (1, 16), F32)
    pre_sb = sbEa("pre_sb", (128, 16), F32)
    coff = sbEa("coff", (128, 1), F32)
    gsh8 = sbEa("gsh8", (128, 16), F32)
    gscr8 = sbEa("gscr8", (128, 16), F32)
    pbuf_b = [sbEa(f"pbuf_b{j}", (128, NSLOT), BF16) for j in range(2)]
    pbuf_f = [sbEa(f"pbuf_f{j}", (128, NSLOT), F32) for j in range(2)]
    partial = sbEa("partial", (128, 16, 1024), BF16)
    wpart = sbEa("wpart", (8, 8, NSLOT), F32)

    s_sp = SemAlias(s_bhi)
    s_xtt = SemAlias(s_sst)
    s_xte = SemAlias(s_sse)
    s_pb = SemAlias(s_bv[0])
    s_pc = SemAlias(s_bv[1])
    s_gev = SemAlias(s_bt[1])
    s_rsx = SemAlias(s_cload)
    s_rsd = sem("s_rsd")
    s_cc2 = sem("s_cc2")
    s_gxt = SemAlias(s_g3)
    s_ew1 = s_wq + [sem("s_ew1c"), sem("s_ew1d")]
    s_ew2 = s_wk + [sem("s_ew2c")]
    EW1B = [sm.n for sm in s_ew1]
    EW2B = [sm.n for sm in s_ew2]
    s_hx = sem("s_hx")
    s_he = SemAlias(s_lgr)
    s_h2v = SemAlias(s_ltt)
    s_whk = sem("s_whk")
    s_oc = SemAlias(s_tke)
    s_ecn = SemAlias(s_sq2)
    s_agd = sem("s_agd")
    s_rw = SemAlias(s_yv)
    s_scm = SemAlias(s_utp)
    s_y0m = SemAlias(s_ue)
    s_y0e = SemAlias(s_ave)
    s_qrw = SemAlias(s_op)
    s_acv = SemAlias(s_ev1)
    s_ftr = SemAlias(s_mskc)
    s_fot = SemAlias(s_impw)
    s_yr2 = s_ysp
    YR2B = [sm.n for sm in s_yr2]
    s_out2 = [sem(f"s_out2_{j}") for j in range(2)]

    ESP = {}
    for k in range(8):
        fam, ei = FAM[k]
        if fam == "e0":
            ESP[k] = dict(H=4096, w1=("c_e0w1", ei), b1=BOFF["e0b1"] + 32 * ei,
                          act=AF.Gelu_apprx_tanh, w2=("c_e0w2", ei),
                          b2=BOFF["e0b2"] + 8 * ei, mid=None, h2base=0,
                          nhk=32, e2=False)
        elif fam == "e1":
            ESP[k] = dict(H=2048, w1=("c_e1w1", ei), b1=BOFF["e1b1"] + 16 * ei,
                          act=AF.Silu, mid=("c_e1w2", ei),
                          bmid=BOFF["e1b2"] + 16 * ei, w2=("c_e1w3", ei),
                          b2=BOFF["e1b3"] + 8 * ei, h2base=16, nhk=16,
                          e2=False)
        else:
            ESP[k] = dict(H=3072, w1=("c_e2w1", ei), b1=BOFF["e2b1"] + 24 * ei,
                          act=None, w2=("c_e2w2", ei),
                          b2=BOFF["e2b2"] + 8 * ei, mid=None, h2base=0,
                          nhk=24, e2=True)

    A_SETS = []
    for k in range(8):
        sp = ESP[k]
        for q in range(sp["H"] // 1024):
            A_SETS.append((k, "q", q))
        if sp["mid"] is not None:
            for mp in range(4):
                A_SETS.append((k, "m", mp))
    NA = len(A_SETS)
    ECN_BASE = {}
    for k in range(8):
        ECN_BASE[k] = 3 * k
    ECN_TOT = 24
    NB_END = {}
    _nbe = 0
    for k in range(8):
        _nbe += ESP[k]["nhk"]
        NB_END[k] = _nbe
    HX_AT = []
    _hxl = 0
    for k in range(8):
        sp = ESP[k]
        for q in range(sp["H"] // 1024):
            _hxl += 8
            HX_AT.append(_hxl)
        if sp["mid"] is not None:
            for hh in range(2):
                _hxl += 8
                HX_AT.append(_hxl)
                HX_AT.append(_hxl)

    @blk.vector
    def _(v):
        s_tk.wait(v, at=2)
        s_xn.wait(v, at=NOT)
        v.tensor_scalar(m_bf[:], xchg[:, 0:16], scal[:, SC_THR:SC_THR + 1],
                        None, OP.is_gt)
        v.tensor_scalar(m_f[:], xchg[:, 0:16], scal[:, SC_THR:SC_THR + 1],
                        None, OP.is_gt)
        for r in range(8):
            v.tensor_scalar(gscr8[:], gath[:, r, 0:16],
                            scal[:, SC_THR:SC_THR + 1], None, OP.is_gt,
                            OP.add, accum_out=cnt8p[:, r:r + 1])
            v.drain()
        s_sp.inc(v.tensor_copy(cnt8b[:], cnt8p[:]))
        s_sp.wait(v, at=8)
        v.tensor_tensor(gscr8[:, 0:8], pb[2][:, 0:8], rltm_sb[:], OP.mult)
        v.drain()
        v.tensor_reduce(coff[:], gscr8[:, 0:8], AX.X, OP.add)
        v.drain()
        v.tensor_tensor(grankm[:], pre_sb[:], pb[4][:, 0:16], OP.add)
        v.drain()
        v.tensor_scalar(grankm[:], grankm[:], coff[:], None, OP.add)
        v.drain()
        v.tensor_tensor(grankm[:], grankm[:], m_f[:], OP.mult)
        v.tensor_scalar(gscr8[:], m_f[:], 1.0, -1.0, OP.mult, OP.add)
        v.drain()
        ins = v.tensor_tensor(grankm[:], grankm[:], gscr8[:], OP.add)
        s_sp.inc(ins)

    @blk.tensor
    def _(t):
        s_tk.wait(t, at=2)
        s_gl.wait(t, at=20)
        s_sp.wait(t, at=1)
        t.matmul(pb[0][:, 0:16], ltri_b[:], m_bf[:], start=True, stop=True)
        t.matmul(pb[1][0:16, 0:1], m_bf[:], onesc[:, 0:1], start=True,
                 stop=True)
        ins = t.matmul(pb[1][0:8, 2:3], cnt8b[:], onesc[:, 0:1], start=True,
                       stop=True)
        s_sp.inc(ins)
        s_sp.wait(t, at=3)
        t.matmul(pb[3][0:16, 0:1], ltri_b[0:16, 0:16], totc[:, 0:1],
                 start=True, stop=True)
        ins = t.transpose(pb[3][0:1, 4:12], cntc_f[:, 0:1],
                          identf[0:8, 0:8])
        s_sp.inc(ins)
        s_sp.wait(t, at=5)
        t.transpose(pb[3][0:1, 16:32], prefc[:, 0:1], identf[0:16, 0:16])
        ins = t.matmul(pb[2][:, 0:8], ones1f[0:1, :], cntrow[0:1, :],
                       start=True, stop=True)
        s_sp.inc(ins)
        s_sp.wait(t, at=7)
        ins = t.matmul(pb[4][:, 0:16], ones1f[0:1, :], tioff[0:1, :],
                       start=True, stop=True)
        s_sp.inc(ins)

    @blk.scalar
    def _(s):
        s_sp.wait(s, at=2)
        s.copy(pre_sb[:], pb[0][:, 0:16])
        s.copy(totc[:, 0:1], pb[1][0:16, 0:1])
        ins = s.copy(cntc_f[:, 0:1], pb[1][0:8, 2:3])
        s_sp.inc(ins)
        s_sp.wait(s, at=4)
        s.copy(prefc[:, 0:1], pb[3][0:16, 0:1])
        ins = s.copy(cntrow[0:1, :], pb[3][0:1, 4:12])
        s_sp.inc(ins)
        s_sp.wait(s, at=6)
        ins = s.copy(tioff[0:1, :], pb[3][0:1, 16:32])
        s_sp.inc(ins)

    @blk.tensor
    def _(t):
        s_xn.wait(t, at=NOT)
        s_tk.wait(t, at=2)
        for ti in range(NOT):
            ptbank = pt0 if ti % 2 == 0 else pt1
            if ti >= 2:
                s_xte.wait(t, at=ti - 1)
            for k in range(8):
                ins = t.transpose(ptbank[:, 128 * k:128 * k + 128],
                                  u_fm[:, k, 128 * ti:128 * ti + 128],
                                  identb[:])
            s_xtt.inc(ins)

    @blk.scalar
    def _(s):
        for ti in range(NOT):
            ptbank = pt0 if ti % 2 == 0 else pt1
            s_xtt.wait(s, at=ti + 1)
            s_xte.inc(s.copy(x_tm[:, ti, :], ptbank[:, 0:1024]))

    @blk.vector
    def _(v):
        s_sp.wait(v, at=9)
        n = 0
        for g8 in range(8):
            v.drain()
            v.tensor_scalar(gsh8[:], grankm[:], float(-g8), None, OP.add)
            v.drain()
            for ti in range(NOT):
                j = n % 2
                if n >= 2:
                    s_pc.wait(v, at=n - 1)
                v.tensor_scalar(pbuf_b[j][:], r2560[:], gsh8[:, ti:ti + 1],
                                None, OP.is_equal)
                v.drain()
                ins = v.tensor_copy(pbuf_f[j][:], pbuf_b[j][:])
                s_pb.inc(ins)
                n += 1

    @blk.tensor
    def _(t):
        s_xte.wait(t, at=NOT)
        n = 0
        for g8 in range(8):
            if g8 >= 1:
                s_gev.wait(t, at=5 * g8)
            for ti in range(NOT):
                j = n % 2
                s_pb.wait(t, at=n + 1)
                t.matmul(pb[4][0:8, 0:256], w_tm[:, ti, :], pbuf_f[j][:],
                         start=(ti == 0), stop=(ti == NOT - 1))
                for stl in range(2):
                    for fb in range(2):
                        ins = t.matmul(
                            pb[2 * stl + fb][:],
                            pbuf_b[j][:, 128 * stl:128 * stl + 128],
                            x_tm[:, ti, 512 * fb:512 * fb + 512],
                            start=(ti == 0), stop=(ti == NOT - 1))
                s_pc.inc(ins)
                n += 1

    @blk.scalar
    def _(s):
        for g8 in range(8):
            s_pc.wait(s, at=16 * (g8 + 1))
            for stl in range(2):
                for fb in range(2):
                    s_gev.inc(s.copy(
                        partial[:, 2 * g8 + stl, 512 * fb:512 * fb + 512],
                        pb[2 * stl + fb][:]))
            s_gev.inc(s.copy(wpart[:, g8, :], pb[4][0:8, 0:256]))

    @blk.sync
    def _(sy):
        for g8 in range(8):
            s_gev.wait(sy, at=5 * (g8 + 1))
            s_rsx.inc(sy.dma_start(
                out=D["rsin"].ap()[g8],
                in_=partial[:, 2 * g8:2 * g8 + 2, :]), 16)
        s_rsx.inc(sy.dma_start(
            out=D["wrsin"].ap().rearrange("c e j -> e c j"),
            in_=wpart[:, :, :]), 16)

    esEa.close()
    esEb = ExitStack()

    def sbEb(name, shape, dt):
        return esEb.enter_context(nc.sbuf_tensor(name, list(shape), dt))

    gx_tm = sbEb("gx_tm", (128, 2, 1024), BF16)
    gxf = sbEb("gxf", (128, 8, NSLOT), BF16)
    wsl_raw = sbEb("wsl_raw", (8, NSLOT), F32)
    h_sb = sbEb("h_sb", (128, 32, NSLOT), BF16)
    bscr = [sbEb(f"bscr{j}", (128, NSLOT), BF16) for j in range(4)]
    wq1 = [sbEb(f"wq1_{j}", (128, 8, 1024), BF16) for j in range(4)]
    wq2 = [sbEb(f"wq2_{j}", (128, 1024), BF16) for j in range(3)]
    comb = sbEb("comb", (128, 2, 1024), F32)
    comb_b = sbEb("comb_b", (128, 2, 1024), BF16)
    erow = sbEb("erow", (1, 1024), F32)

    @blk.gpsimd
    def _(g):
        s_rsx.wait(g, at=16 * 9)
        g.collective_compute(
            "ReduceScatter", OP.add, replica_groups=[list(range(8))],
            ins=[D["rsin"].ap()], outs=[D["rsout"].ap()],
        ).then_inc(s_cc2.h, 1)
        s_cc2.n += 1
        g.collective_compute(
            "ReduceScatter", OP.add, replica_groups=[list(range(8))],
            ins=[D["wrsin"].ap()], outs=[D["wrsout"].ap()],
        ).then_inc(s_cc2.h, 1)
        s_cc2.n += 1
        g.wait_ge(s_cc2.h, 2)
        s_rsd.inc(g.dma_start(out=gx_tm[:, :, :], in_=D["rsout"].ap()), 16)
        s_rsd.inc(g.dma_start(out=wsl_raw[:, :], in_=D["wrsout"].ap()), 16)

    @blk.tensor
    def _(t):
        s_rsd.wait(t, at=32)
        for sl in range(2):
            ptbank = pt0 if sl == 0 else pt1
            for k in range(8):
                ins = t.transpose(ptbank[:, 128 * k:128 * k + 128],
                                  gx_tm[:, sl, 128 * k:128 * k + 128],
                                  identb[:])
            s_gxt.inc(ins)
        s_gxt.wait(t, at=4)
        t.transpose(pb[5][:, 0:8], wsl_raw[0:8, 0:128], identf[0:8, 0:8])
        ins = t.transpose(pb[5][:, 8:16], wsl_raw[0:8, 128:256],
                          identf[0:8, 0:8])
        s_gxt.inc(ins)

    @blk.scalar
    def _(s):
        for sl in range(2):
            s_gxt.wait(s, at=2 * sl + 1)
            ptbank = pt0 if sl == 0 else pt1
            for k in range(8):
                ins = s.copy(gxf[:, k, 128 * sl:128 * sl + 128],
                             ptbank[:, 128 * k:128 * k + 128])
            s_gxt.inc(ins)
        s_gxt.wait(s, at=5)
        s.copy(wsl_tm[:, 0, :], pb[5][:, 0:8])
        ins = s.copy(wsl_tm[:, 1, :], pb[5][:, 8:16])
        s_gxt.inc(ins)

    @blk.gpsimd
    def _(g):
        s_wcast.wait(g)
        s_ltm.wait(g, at=NOT)
        s_tk.wait(g, at=2)
        s_xn.wait(g, at=NOT)
        s_gl.wait(g, at=20)
        s_g2e.wait(g, at=16)
        s_g1e.wait(g, at=32)
        n = 0
        nb = 0
        for k in range(8):
            sp = ESP[k]
            nsets = sp["H"] // 1024 + (4 if sp["mid"] is not None else 0)
            for si in range(nsets):
                kind = "q" if si < sp["H"] // 1024 else "m"
                idx = si if kind == "q" else si - sp["H"] // 1024
                j = n % 4
                if n >= 4:
                    g.wait_ge(s_hx.h, HX_AT[n - 4])
                if kind == "q":
                    nm, ei = sp["w1"]
                    srcw = D[nm].ap()[ei][:, 1024 * idx:1024 * idx + 1024]
                else:
                    nm, ei = sp["mid"]
                    hh, prt = idx // 2, idx % 2
                    srcw = D[nm].ap()[ei][1024 * prt:1024 * prt + 1024,
                                          1024 * hh:1024 * hh + 1024]
                s_ew1[j].inc(g.dma_start(
                    out=wq1[j][:, 0:8, :],
                    in_=srcw.rearrange("(c p) f -> p c f", p=128)), 16)
                n += 1
            nm, ei = sp["w2"]
            for hk in range(sp["nhk"]):
                j = nb % 3
                if nb >= 3:
                    g.wait_ge(s_whk.h, nb - 2)
                srcw = D[nm].ap()[ei][128 * hk:128 * hk + 128, :]
                s_ew2[j].inc(g.dma_start(out=wq2[j][:, :], in_=srcw), 16)
                nb += 1

    @blk.tensor
    def _(t):
        s_gxt.wait(t, at=6)
        n = 0
        nb = 0
        hx = 0
        h2v = 0
        for k in range(8):
            sp = ESP[k]
            eb = ECN_BASE[k]
            nq = sp["H"] // 1024
            for q in range(nq):
                j = n % 4
                s_ew1[j].wait(t, at=EW1B[j] + 16 * (n // 4 + 1))
                if k >= 1 and q == 0:
                    s_oc.wait(t, at=4 * k)
                    s_ecn.wait(t, at=eb)
                for hc in range(8):
                    reg = pb[hc % 4][:, 256 * (hc // 4):256 * (hc // 4)
                                     + 256]
                    if hx >= 4:
                        s_he.wait(t, at=hx - 3)
                    for kk in range(8):
                        ins = t.matmul(
                            reg, wq1[j][:, kk, 128 * hc:128 * hc + 128],
                            gxf[:, kk, :], start=(kk == 0), stop=(kk == 7))
                    s_hx.inc(ins)
                    hx += 1
                    if sp["e2"]:
                        h2v += 1
                n += 1
            if sp["mid"] is not None:
                s_he.wait(t, at=hx)
                for hh in range(2):
                    n0 = n
                    for jp in range(2):
                        jj = (n0 + jp) % 4
                        s_ew1[jj].wait(
                            t, at=EW1B[jj] + 16 * ((n0 + jp) // 4 + 1))
                    for hc in range(8):
                        reg = pb[hc % 4][:, 256 * (hc // 4):
                                         256 * (hc // 4) + 256]
                        if hx >= 4:
                            s_he.wait(t, at=hx - 3)
                        for h1c in range(16):
                            jj = (n0 + h1c // 8) % 4
                            ins = t.matmul(
                                reg,
                                wq1[jj][:, h1c % 8, 128 * hc:128 * hc + 128],
                                h_sb[:, h1c, :], start=(h1c == 0),
                                stop=(h1c == 15))
                        s_hx.inc(ins)
                        hx += 1
                    n += 2
            s_he.wait(t, at=hx)
            if sp["e2"]:
                s_h2v.wait(t, at=h2v)
            for hk in range(sp["nhk"]):
                j = nb % 3
                s_ew2[j].wait(t, at=EW2B[j] + 16 * (nb // 3 + 1))
                if k >= 1 and hk == 0:
                    s_oc.wait(t, at=4 * k)
                for stl in range(2):
                    for fb in range(2):
                        ins = t.matmul(
                            pb[2 * stl + fb][:],
                            h_sb[:, sp["h2base"] + hk,
                                 128 * stl:128 * stl + 128],
                            wq2[j][:, 512 * fb:512 * fb + 512],
                            start=(hk == 0), stop=(hk == sp["nhk"] - 1))
                s_whk.inc(ins)
                nb += 1
            s_ecn.wait(t, at=eb + 1)
            for fc in range(8):
                ins = t.transpose(pb[4][:, fc:fc + 1],
                                  erow[0:1, 128 * fc:128 * fc + 128],
                                  identf[0:1, 0:1])
            s_ecn.inc(ins)

    @blk.scalar
    def _(s):
        s_gxt.wait(s, at=6)
        hx = 0
        h2v = 0
        for k in range(8):
            sp = ESP[k]
            eb = ECN_BASE[k]
            nq = sp["H"] // 1024
            for q in range(nq):
                for hc in range(8):
                    reg = pb[hc % 4][:, 256 * (hc // 4):256 * (hc // 4)
                                     + 256]
                    s_hx.wait(s, at=hx + 1)
                    hr = 8 * q + hc
                    bias = biases[:, sp["b1"] + hr:sp["b1"] + hr + 1]
                    if sp["e2"]:
                        if h2v >= 2:
                            s_h2v.wait(s, at=h2v - 1)
                        s.activation(bscr[2 * (h2v % 2)][:], reg, AF.Silu,
                                     bias=bias)
                        s_he.inc(s.activation(bscr[2 * (h2v % 2) + 1][:],
                                              reg, AF.Sigmoid, bias=bias))
                        h2v += 1
                    else:
                        s_he.inc(s.activation(h_sb[:, hr, :], reg,
                                              sp["act"], bias=bias))
                    hx += 1
            if sp["mid"] is not None:
                for hh in range(2):
                    for hc in range(8):
                        reg = pb[hc % 4][:, 256 * (hc // 4):
                                         256 * (hc // 4) + 256]
                        s_hx.wait(s, at=hx + 1)
                        hr = 8 * hh + hc
                        bias = biases[:, sp["bmid"] + hr:sp["bmid"] + hr + 1]
                        s_he.inc(s.activation(h_sb[:, 16 + hr, :], reg,
                                              AF.Silu, bias=bias))
                        hx += 1
            s_whk.wait(s, at=NB_END[k])
            s.copy(erow[0:1, 0:512], pb[2][64:65, 0:512])
            ins = s.copy(erow[0:1, 512:1024], pb[3][64:65, 0:512])
            s_ecn.inc(ins)
            s_ecn.wait(s, at=eb + 2)
            for fc in range(8):
                s.copy(ecst[:, fc, k:k + 1], pb[4][:, fc:fc + 1])
            for fc in range(8):
                ins = s.copy(b2mx[:, fc, k:k + 1],
                             biases[:, sp["b2"] + fc:sp["b2"] + fc + 1])
            s_ecn.inc(ins)

    @blk.vector
    def _(v):
        s_gxt.wait(v, at=6)
        v.memset(comb[:], 0.0)
        v.drain()
        hx = 0
        h2v = 0
        ox = 0
        for k in range(8):
            sp = ESP[k]
            nq = sp["H"] // 1024
            for q in range(nq):
                for hc in range(8):
                    if sp["e2"]:
                        s_he.wait(v, at=hx + 1)
                        hr = 8 * q + hc
                        s_h2v.inc(v.tensor_tensor(
                            h_sb[:, hr, :], bscr[2 * (h2v % 2)][:],
                            bscr[2 * (h2v % 2) + 1][:], OP.mult))
                        h2v += 1
                    hx += 1
            if sp["mid"] is not None:
                hx += 16
            s_whk.wait(v, at=NB_END[k])
            for stl in range(2):
                for fb in range(2):
                    ins = v.scalar_tensor_tensor(
                        comb[:, stl, 512 * fb:512 * fb + 512],
                        pb[2 * stl + fb][:], wsl_tm[:, stl, k:k + 1],
                        comb[:, stl, 512 * fb:512 * fb + 512],
                        OP.mult, OP.add)
                    s_oc.inc(ins)
                    ox += 1
        v.drain()
        ins = v.tensor_copy(comb_b[:], comb[:])
        s_oc.inc(ins)

    @blk.sync
    def _(sy):
        s_oc.wait(sy, at=33)
        s_rsx.inc(sy.dma_start(
            out=D["agin2"].ap().rearrange("s p f -> p s f"),
            in_=comb_b[:, :, :]), 16)

    esEb.close()
    esEc = ExitStack()

    def sbEc(name, shape, dt):
        return esEc.enter_context(nc.sbuf_tensor(name, list(shape), dt))

    agx = sbEc("agx", (128, 16, 1024), BF16)
    grow = sbEc("grow", (1, 2048), F32)
    grbcq = sbEc("grbcq", (128, 512), F32)
    w_em = sbEc("w_em", (8, 2048), F32)
    wns_scr = sbEc("wns_scr", (8, 512), F32)
    nmem_scr = sbEc("nmem_scr", (8, 512), F32)
    pth = sbEc("pth", (128, 16, 512), BF16)
    yscrE = sbEc("yscrE", (128, 512), F32)
    acc = sbEc("acc", (128, 8, 512), F32)
    yrld2 = [sbEc(f"yrld2_{j}", (128, EMB), F32) for j in range(2)]
    out_sb = [sbEc(f"out_sb{j}", (128, EMB), F32) for j in range(2)]

    @blk.gpsimd
    def _(g):
        s_rsx.wait(g, at=16 * 10)
        g.collective_compute(
            "AllGather", OP.bypass, replica_groups=[list(range(8))],
            ins=[D["agin2"].ap()], outs=[D["agout2"].ap()],
        ).then_inc(s_cc2.h, 1)
        s_cc2.n += 1
        g.wait_ge(s_cc2.h, 3)
        s_agd.inc(g.dma_start(
            out=agx[:, :, :],
            in_=D["agout2"].ap().rearrange("c s p f -> p (c s) f")), 16)

    @blk.tensor
    def _(t):
        s_oc.wait(t, at=33)
        for ti in range(NOT):
            if ti >= 1:
                s_rw.wait(t, at=2 * ti)
            t.transpose(pb[0][0:1, 0:128], grankm[:, ti:ti + 1], identf[:])
            ins = t.transpose(pb[0][0:8, 128:256], w_tm[:, ti, :],
                              identf[:])
            s_rw.inc(ins)
        s_ecn.wait(t, at=ECN_TOT)
        for fc in range(8):
            s_rw.wait(t, at=32 + 2 * fc)
            t.transpose(pb[2][0:8, 0:128], ecst[:, fc, :], identf[:])
            ins = t.transpose(pb[3][0:8, 0:128], b2mx[:, fc, :], identf[:])
            s_rw.inc(ins)

    @blk.scalar
    def _(s):
        for ti in range(NOT):
            s_rw.wait(s, at=2 * ti + 1)
            s.copy(grow[0:1, 128 * ti:128 * ti + 128], pb[0][0:1, 0:128])
            ins = s.copy(w_em[0:8, 128 * ti:128 * ti + 128],
                         pb[0][0:8, 128:256])
            s_rw.inc(ins)
        for fc in range(8):
            s_rw.wait(s, at=33 + 2 * fc)
            s.copy(econT[0:8, 128 * fc:128 * fc + 128], pb[2][0:8, 0:128])
            ins = s.copy(b2T[0:8, 128 * fc:128 * fc + 128],
                         pb[3][0:8, 0:128])
            s_rw.inc(ins)


    @blk.vector
    def _(v):
        for qr in range(4):
            s_qrw.wait(v, at=3 * qr + 2)
            if qr >= 1:
                s_scm.wait(v, at=8 * qr)
            for st in range(16):
                v.tensor_scalar(pth[:, st, :], grbcq[:, :],
                                gc16_sb[:, st:st + 1], None, OP.is_equal)
            v.tensor_scalar(nmem_scr[0:8, :], grbcq[0:8, :], 0.0, None,
                            OP.is_lt)
            v.drain()
            ins = v.tensor_tensor(wns_scr[0:8, :],
                                  w_em[0:8, 512 * qr:512 * qr + 512],
                                  nmem_scr[0:8, :], OP.mult)
            s_qrw.inc(ins)
            for fc in range(8):
                u = 8 * qr + fc
                s_scm.wait(v, at=u + 1)
                s_y0e.wait(v, at=u + 1)
                v.tensor_tensor(acc[:, fc, :], pb[fc % 2][:], yscrE[:],
                                OP.add)
                v.drain()
                ins = v.scalar_tensor_tensor(
                    acc[:, fc, :], u_fm[:, fc, 512 * qr:512 * qr + 512],
                    0.1, acc[:, fc, :], OP.mult, OP.add)
                s_acv.inc(ins)
            for tj in range(4):
                ti = 4 * qr + tj
                s_ftr.wait(v, at=ti + 1)
                s_yr2[ti % 2].wait(v, at=YR2B[ti % 2] + 16 * (ti // 2 + 1))
                if ti >= 2:
                    s_out2[ti % 2].wait(v, at=16 * ((ti - 2) // 2 + 1))
                v.tensor_tensor(out_sb[ti % 2][:, 0:512],
                                yrld2[ti % 2][:, 0:512], pb[4][:], OP.add)
                ins = v.tensor_tensor(out_sb[ti % 2][:, 512:1024],
                                      yrld2[ti % 2][:, 512:1024], pb[5][:],
                                      OP.add)
                s_fot.inc(ins)

    @blk.tensor
    def _(t):
        s_agd.wait(t, at=16)
        s_rw.wait(t, at=48)
        for qr in range(4):
            if qr >= 1:
                s_acv.wait(t, at=8 * qr)
            ins = t.matmul(pb[1][:, 0:512], ones1f[0:1, :],
                           grow[0:1, 512 * qr:512 * qr + 512], start=True,
                           stop=True)
            s_qrw.inc(ins)
            s_qrw.wait(t, at=3 * qr + 3)
            for fc in range(8):
                u = 8 * qr + fc
                if u >= 2:
                    s_acv.wait(t, at=u - 1)
                ba = pb[fc % 2]
                for st in range(16):
                    ins = t.matmul(
                        ba[:], agx[:, st, 128 * fc:128 * fc + 128],
                        pth[:, st, :], start=(st == 0), stop=(st == 15))
                s_scm.inc(ins)
                by = pb[2 + fc % 2]
                if u >= 2:
                    s_y0e.wait(t, at=u - 1)
                t.matmul(by[:], econT[0:8, 128 * fc:128 * fc + 128],
                         wns_scr[0:8, :], start=True, stop=False)
                ins = t.matmul(by[:], b2T[0:8, 128 * fc:128 * fc + 128],
                               w_em[0:8, 512 * qr:512 * qr + 512],
                               start=False, stop=True)
                s_y0m.inc(ins)
            s_acv.wait(t, at=8 * (qr + 1))
            for tj in range(4):
                ti = 4 * qr + tj
                if ti >= 1:
                    s_fot.wait(t, at=ti)
                for jc in range(8):
                    bank = pb[4 + jc // 4]
                    ins = t.transpose(
                        bank[:, 128 * (jc % 4):128 * (jc % 4) + 128],
                        acc[:, jc, 128 * tj:128 * tj + 128], identf[:])
                s_ftr.inc(ins)

    @blk.scalar
    def _(s):
        for qr in range(4):
            s_qrw.wait(s, at=3 * qr + 1)
            ins = s.copy(grbcq[:, :], pb[1][:, 0:512])
            s_qrw.inc(ins)
            for fc in range(8):
                u = 8 * qr + fc
                s_y0m.wait(s, at=u + 1)
                if u >= 1:
                    s_acv.wait(s, at=u)
                ins = s.copy(yscrE[:], pb[2 + fc % 2][:])
                s_y0e.inc(ins)

    @blk.sync
    def _(sy):
        for j in range(2):
            s_ysp[j].wait(sy)
        for ti in range(NOT + 1):
            if ti < NOT:
                if ti >= 2:
                    s_fot.wait(sy, at=ti - 1)
                s_yr2[ti % 2].inc(sy.dma_start(
                    out=yrld2[ti % 2][:],
                    in_=D["yspill"].ap()[128 * ti:128 * (ti + 1), :]), 16)
            if ti >= 1:
                tj = ti - 1
                s_fot.wait(sy, at=tj + 1)
                s_out2[tj % 2].inc(sy.dma_start(
                    out=D["out"].ap()[128 * tj:128 * (tj + 1), :],
                    in_=out_sb[tj % 2][:]), 16)

    if "sp" in debug:
        @blk.sync
        def _(sy):
            s_sp.wait(sy)
            s_dbg.inc(sy.dma_start(out=dbg["d_grank"].ap(), in_=grankm[:]),
                      16)
            s_rsd.wait(sy)
            s_dbg.inc(sy.dma_start(out=dbg["d_gx"].ap(), in_=gx_tm[:]), 16)
            s_dbg.inc(sy.dma_start(out=dbg["d_wsl"].ap(), in_=wsl_raw[:]),
                      16)
            s_oc.wait(sy)
            s_dbg.inc(sy.dma_start(out=dbg["d_comb"].ap(), in_=comb[:]), 16)

    @blk.gpsimd
    def _(g):
        s_sp.wait(g)
        s_agd.wait(g)
        s_hx.wait(g)
        s_y0m.wait(g)
        s_xtt.wait(g)
        s_xte.wait(g)
        s_pb.wait(g)
        s_pc.wait(g)
        s_gev.wait(g)
        s_he.wait(g)
        s_h2v.wait(g)
        s_whk.wait(g)
        s_ecn.wait(g)
        s_rw.wait(g)
        s_qrw.wait(g)
        s_scm.wait(g)
        s_y0e.wait(g)
        s_acv.wait(g)
        s_fot.wait(g)
        for j in range(2):
            s_ysp[j].wait(g)
        s_dbg.wait(g)
        for j in range(2):
            s_out2[j].wait(g)

    esEc.close()
    esE0.close()

    esD0.close()
    es.close()
    return nc, D


def kernel(**inputs):
    noise = _noise_table()
    nc, D = build_nc()
    in_maps = [_core_inputs(c, inputs, noise) for c in range(8)]
    res = run_bass_kernel_spmd(nc, in_maps, core_ids=list(range(8)))
    out = np.zeros((4, S, EMB), np.float32)
    for c in range(8):
        b, h = c // 2, c % 2
        out[b, 2048 * h:2048 * (h + 1)] = np.asarray(res.results[c]["out"],
                                                     np.float32)
    return out
